# version 03
# baseline (speedup 1.0000x reference)
import numpy as np
import concourse.bass as bass
import concourse.mybir as mybir
from concourse.bass_utils import run_bass_kernel_spmd
from concourse.bass_core_rust import IndirectOffsetOnAxis

F32 = mybir.dt.float32
BF16 = mybir.dt.bfloat16
U32 = mybir.dt.uint32
AF = mybir.ActivationFunctionType
ALU = mybir.AluOpType

D = 1024
NCH = 8
SEQ = 2048
NSEQ = 2
TOK = SEQ * NSEQ
TT = 512
NTILES = TOK // TT
FEPS = float(np.finfo(np.float32).eps)


class B:

    def __init__(self, nc, stack):
        self.nc = nc
        self.stack = stack
        self.pcsem = stack.enter_context(nc.semaphore("pc"))
        self.pc = 0
        self._zero_cache = {}

    def wait(self, eng):
        if self.pc > 0:
            eng.wait_ge(self.pcsem, self.pc)

    def bump(self, inst, n=1):
        inst.then_inc(self.pcsem, n)
        self.pc += n

    def step(self, eng, fn, inc=1):
        self.wait(eng)
        last = fn(eng)
        self.bump(last, inc)

    def sbuf(self, name, shape, dtype):
        return self.stack.enter_context(self.nc.sbuf_tensor(name, shape, dtype))

    def psum(self, name, shape, dtype=F32):
        return self.stack.enter_context(self.nc.psum_tensor(name, shape, dtype))



def softplus_emit(b, v, s, out_f32, in_ap, bias_pp, tmp1, tmp2):
    b.step(s, lambda e: e.activation(tmp1, in_ap, AF.Identity, bias=bias_pp, scale=1.0))
    b.step(s, lambda e: e.activation(tmp2, tmp1, AF.Abs))
    b.step(s, lambda e: e.activation(tmp2, tmp2, AF.Exp, bias=0.0, scale=-1.0))
    b.step(s, lambda e: e.activation(tmp2, tmp2, AF.Ln, bias=1.0, scale=1.0))
    b.step(v, lambda e: e.tensor_single_scalar(tmp1, tmp1, 0.0, ALU.max))
    b.step(v, lambda e: e.tensor_add(out_f32, tmp1, tmp2))



def build_l0attn(nc, io_in="x_fm", io_out="x1_fm", stack=None, b=None, dram=None):
    own = stack is None
    if own:
        from contextlib import ExitStack
        stack = ExitStack()
        stack.__enter__()
    if dram is None:
        dram = {}

    def din(name, shape, dtype=F32):
        if name not in dram:
            dram[name] = nc.dram_tensor(name, shape, dtype, kind="ExternalInput").ap()
        return dram[name]

    x = din(io_in, [D, TOK])
    w1t = din("l0a_w1t", [D, D])
    w2t = din("l0a_w2t", [D, D])
    dwt_big = din("l0a_dwtBig", [D, D])
    wlt = din("l0a_wlt", [D, D])
    cwt = din("l0a_cwt", [64, D])
    dwsm = din("l0a_dwsm", [D, 64])
    bwsm = din("l0a_bwsm", [D, 64])
    convw = din("l0a_convw", [D, 3])
    vecs8 = din("l0a_vecs8", [128, 8 * 5])
    vecs64 = din("l0a_vecs64", [64, 3])
    if io_out not in dram:
        dram[io_out] = nc.dram_tensor(io_out, [D, TOK], F32, kind="ExternalOutput").ap()
    xo = dram[io_out]

    if b is None:
        b = B(nc, stack)
    sy, g, v, s, t = nc.sync, nc.gpsimd, nc.vector, nc.scalar, nc.tensor

    w1s = b.sbuf("l0_w1s", [128, NCH * D], BF16)
    w2s = b.sbuf("l0_w2s", [128, NCH * D], BF16)
    dws = b.sbuf("l0_dws", [128, NCH * D], BF16)
    wls = b.sbuf("l0_wls", [128, NCH * D], BF16)
    cws = b.sbuf("l0_cws", [64, D], BF16)
    dwsms = b.sbuf("l0_dwsms", [128, NCH * 64], BF16)
    bwsms = b.sbuf("l0_bwsms", [128, NCH * 64], BF16)
    convs = b.sbuf("l0_convs", [128, NCH * 3], F32)
    v8 = b.sbuf("l0_v8", [128, 40], F32)
    v64 = b.sbuf("l0_v64", [64, 3], F32)
    alogneg = b.sbuf("l0_alogneg", [64, 1], F32)
    cbdb = b.sbuf("l0_cbdb", [128, 8], F32)

    ones_bf = b.sbuf("l0_onesbf", [128, 1], BF16)
    ones1f = b.sbuf("l0_ones1f", [1, 128], F32)
    zeros64 = b.sbuf("l0_zeros64", [64, TT], F32)

    xt = b.sbuf("l0_xt", [128, NCH * TT], F32)
    sq = b.sbuf("l0_sq", [128, NCH * TT], BF16)
    xnt = b.sbuf("l0_xnt", [128, NCH * TT], BF16)
    xbt = b.sbuf("l0_xbt", [128, NCH * (TT + 2)], BF16)
    zt = b.sbuf("l0_zt", [128, NCH * TT], BF16)
    xct = b.sbuf("l0_xct", [128, NCH * TT], BF16)
    st_ = b.sbuf("l0_st", [128, NCH * TT], F32)
    sbf = b.sbuf("l0_sbf", [128, NCH * TT], BF16)
    x1t = b.sbuf("l0_x1t", [128, NCH * TT], F32)
    rowa = b.sbuf("l0_rowa", [1, TT], F32)
    rowb = b.sbuf("l0_rowb", [1, TT], F32)
    rowbf = b.sbuf("l0_rowbf", [1, 2 * TT], F32)

    c1 = b.sbuf("l0_c1", [64, TT], F32)
    c2 = b.sbuf("l0_c2", [64, TT], F32)
    delta = b.sbuf("l0_delta", [64, TT], F32)
    bbar = b.sbuf("l0_bbar", [64, TT], F32)
    acum = b.sbuf("l0_acum", [64, TT], F32)
    acumc = b.sbuf("l0_acumc", [64, TT], F32)
    ashift = b.sbuf("l0_ashift", [64, TT], F32)
    csum = b.sbuf("l0_csum", [64, TT], F32)
    hbf = b.sbuf("l0_hbf", [64, TT], BF16)
    sztile = b.sbuf("l0_sz", [128, TT], F32)
    carry = b.sbuf("l0_carry", [64, 4], F32)
    sp1 = b.sbuf("l0_sp1", [64, TT], F32)
    sp2 = b.sbuf("l0_sp2", [64, TT], F32)

    pA = b.psum("l0_pA", [128, TT])
    pB = b.psum("l0_pB", [128, TT])
    pd = b.psum("l0_pd", [64, TT])
    pb2 = b.psum("l0_pb2", [64, TT])
    prow = b.psum("l0_prow", [1, TT])
    prow2 = b.psum("l0_prow2", [1, TT])
    pbc1 = b.psum("l0_pbc1", [128, TT])
    pbc2 = b.psum("l0_pbc2", [128, TT])

    def ldw(dst, src, width):
        b.wait(g)
        b.bump(g.dma_start(out=dst[:].rearrange("p (c j) -> p c j", c=NCH),
                           in_=src.rearrange("(c p) j -> p c j", p=128)), 16)

    ldw(w1s, w1t, D)
    ldw(w2s, w2t, D)
    ldw(dws, dwt_big, D)
    ldw(wls, wlt, D)
    b.wait(g)
    b.bump(g.dma_start(out=cws[:], in_=cwt[:, :]), 16)
    ldw(dwsms, dwsm, 64)
    ldw(bwsms, bwsm, 64)
    b.wait(sy)
    b.bump(sy.dma_start(out=convs[:].rearrange("p (c k) -> p c k", c=NCH),
                        in_=convw.rearrange("(c p) k -> p c k", p=128)), 16)
    b.wait(sy)
    b.bump(sy.dma_start(out=v8[:], in_=vecs8[:, :]), 16)
    b.wait(sy)
    b.bump(sy.dma_start(out=v64[:], in_=vecs64[:, :]), 16)

    b.step(v, lambda e: e.memset(ones_bf[:], 1.0))
    b.step(v, lambda e: e.memset(ones1f[:], 1.0))
    b.step(v, lambda e: e.memset(zeros64[:], 0.0))
    b.step(v, lambda e: e.tensor_add(cbdb[:], v8[:, 16:24], v8[:, 24:32]))
    softplus_emit(b, v, s, alogneg[:, 0:1], v64[:, 0:1], 0.0, c1[:, 0:1], c2[:, 0:1])

    NW, CB = v8[:, 0:8], v8[:, 8:16]
    BB, DBSM = v64[:, 1:2], v64[:, 2:3]

    for tile in range(NTILES):
        t0 = tile * TT
        seq_start = (tile % (NTILES // NSEQ)) == 0
        b.wait(sy)
        b.bump(sy.dma_start(
            out=xt[:], in_=x.rearrange("(c p) t -> p c t", p=128)[:, :, t0:t0 + TT]), 16)

        b.step(s, lambda e: e.activation(sq[:], xt[:], AF.Square))
        b.wait(t)
        for c in range(NCH):
            last = t.matmul(prow[:, :], ones_bf[:, :], sq[:, c * TT:(c + 1) * TT],
                            start=(c == 0), stop=(c == NCH - 1))
        b.bump(last)
        b.step(v, lambda e: e.tensor_scalar(rowa[:], prow[:, :], 1.0 / D, FEPS, ALU.mult, ALU.add))
        b.step(v, lambda e: e.reciprocal(rowb[:], rowa[:]))
        b.step(s, lambda e: e.activation(rowbf[:, TT:2 * TT], rowb[:], AF.Sqrt))
        b.step(t, lambda e: e.matmul(pbc1[:, :], ones1f[:, :], rowbf[:, TT:2 * TT], start=True, stop=True))
        for c in range(NCH):
            b.step(v, lambda e, c=c: e.scalar_tensor_tensor(
                xnt[:, c * TT:(c + 1) * TT], xt[:, c * TT:(c + 1) * TT], NW[:, c:c + 1], pbc1[:, :],
                ALU.mult, ALU.mult))
        for jc in range(NCH):
            b.wait(t)
            for dc in range(NCH):
                last = t.matmul(pA[:, :], w1s[:, dc * D + jc * 128: dc * D + (jc + 1) * 128],
                                xnt[:, dc * TT:(dc + 1) * TT], start=(dc == 0), stop=(dc == NCH - 1))
            b.bump(last)
            b.step(v, lambda e, jc=jc: e.tensor_copy(xbt[:, jc * (TT + 2) + 2: (jc + 1) * (TT + 2)], pA[:, :]))
            b.wait(t)
            for dc in range(NCH):
                last = t.matmul(pB[:, :], w2s[:, dc * D + jc * 128: dc * D + (jc + 1) * 128],
                                xnt[:, dc * TT:(dc + 1) * TT], start=(dc == 0), stop=(dc == NCH - 1))
            b.bump(last)
            b.step(s, lambda e, jc=jc: e.copy(zt[:, jc * TT:(jc + 1) * TT], pB[:, :]))
        for jc in range(NCH):
            base = jc * (TT + 2)
            if seq_start:
                b.step(v, lambda e, base=base: e.memset(xbt[:, base:base + 2], 0.0))
            b.step(v, lambda e, jc=jc, base=base: e.tensor_single_scalar(
                sztile[:, :], xbt[:, base:base + TT], convs[:, jc * 3:jc * 3 + 1], ALU.mult))
            b.step(v, lambda e, jc=jc, base=base: e.scalar_tensor_tensor(
                sztile[:, :], xbt[:, base + 1:base + 1 + TT], convs[:, jc * 3 + 1:jc * 3 + 2],
                sztile[:, :], ALU.mult, ALU.add))
            b.step(v, lambda e, jc=jc, base=base: e.scalar_tensor_tensor(
                sztile[:, :], xbt[:, base + 2:base + 2 + TT], convs[:, jc * 3 + 2:jc * 3 + 3],
                sztile[:, :], ALU.mult, ALU.add))
            b.step(s, lambda e, jc=jc: e.activation(
                xct[:, jc * TT:(jc + 1) * TT], sztile[:, :], AF.Silu, bias=CB[:, jc:jc + 1], scale=1.0))
            b.step(v, lambda e, base=base: e.tensor_copy(xbt[:, base:base + 2], xbt[:, base + TT:base + TT + 2]))
        b.wait(t)
        for dc in range(NCH):
            last = t.matmul(pd[:, :], dwsms[:, dc * 64:(dc + 1) * 64], xct[:, dc * TT:(dc + 1) * TT],
                            start=(dc == 0), stop=(dc == NCH - 1))
        b.bump(last)
        softplus_emit(b, v, s, delta[:, :], pd[:, :], DBSM, sp1[:, :], sp2[:, :])
        b.wait(t)
        for dc in range(NCH):
            last = t.matmul(pb2[:, :], bwsms[:, dc * 64:(dc + 1) * 64], xct[:, dc * TT:(dc + 1) * TT],
                            start=(dc == 0), stop=(dc == NCH - 1))
        b.bump(last)
        b.step(s, lambda e: e.activation(c1[:, :], pb2[:, :], AF.Identity, bias=BB, scale=1.0))
        b.step(v, lambda e: e.tensor_mul(bbar[:, :], delta[:, :], c1[:, :]))
        b.step(v, lambda e: e.tensor_single_scalar(c2[:, :], delta[:, :], alogneg[:, 0:1], ALU.mult))
        if seq_start:
            b.step(v, lambda e: e.memset(carry[:, :], 0.0))
        b.step(v, lambda e: e.tensor_single_scalar(c2[:, :], c2[:, :], -1.0, ALU.mult))
        b.step(v, lambda e: e.tensor_tensor_scan(acum[:, :], c2[:, :], zeros64[:, :], carry[:, 0:1], ALU.add, ALU.add))
        b.step(v, lambda e: e.tensor_copy(carry[:, 0:1], acum[:, TT - 1:TT]))
        b.step(v, lambda e: e.tensor_scalar(acumc[:, :], acum[:, :], 20.0, -20.0, ALU.min, ALU.max))
        b.step(v, lambda e: e.tensor_copy(ashift[:, 1:TT], acumc[:, 0:TT - 1]))
        b.step(v, lambda e: e.tensor_copy(ashift[:, 0:1], carry[:, 1:2]))
        b.step(v, lambda e: e.tensor_copy(carry[:, 1:2], acumc[:, TT - 1:TT]))
        b.step(s, lambda e: e.activation(ashift[:, :], ashift[:, :], AF.Exp, bias=0.0, scale=-1.0))
        b.step(v, lambda e: e.tensor_mul(c1[:, :], bbar[:, :], ashift[:, :]))
        b.step(v, lambda e: e.tensor_tensor_scan(csum[:, :], c1[:, :], zeros64[:, :], carry[:, 2:3], ALU.add, ALU.add))
        b.step(v, lambda e: e.tensor_copy(carry[:, 2:3], csum[:, TT - 1:TT]))
        b.step(s, lambda e: e.activation(acumc[:, :], acumc[:, :], AF.Exp))
        b.step(v, lambda e: e.tensor_mul(hbf[:, :], csum[:, :], acumc[:, :]))
        for jc in range(NCH):
            b.wait(t)
            last = t.matmul(pA[:, :], cws[:, jc * 128:(jc + 1) * 128], hbf[:, :], start=True, stop=False)
            for dc in range(NCH):
                last = t.matmul(pA[:, :], dws[:, dc * D + jc * 128: dc * D + (jc + 1) * 128],
                                xct[:, dc * TT:(dc + 1) * TT], start=False, stop=(dc == NCH - 1))
            b.bump(last)
            b.step(s, lambda e, jc=jc: e.activation(sztile[:, :], zt[:, jc * TT:(jc + 1) * TT], AF.Silu))
            b.step(v, lambda e, jc=jc: e.scalar_tensor_tensor(
                xnt[:, jc * TT:(jc + 1) * TT], pA[:, :], cbdb[:, jc:jc + 1], sztile[:, :],
                ALU.add, ALU.mult))
        for jc in range(NCH):
            b.wait(t)
            for dc in range(NCH):
                last = t.matmul(pA[:, :], wls[:, dc * D + jc * 128: dc * D + (jc + 1) * 128],
                                xnt[:, dc * TT:(dc + 1) * TT], start=(dc == 0), stop=(dc == NCH - 1))
            b.bump(last)
            b.step(v, lambda e, jc=jc: e.scalar_tensor_tensor(
                st_[:, jc * TT:(jc + 1) * TT], xt[:, jc * TT:(jc + 1) * TT], 2.0, pA[:, :],
                ALU.mult, ALU.add))
        b.step(v, lambda e: e.tensor_copy(sbf[:], st_[:]))
        b.step(s, lambda e: e.activation(sq[:], st_[:], AF.Square))
        b.wait(t)
        for c in range(NCH):
            last = t.matmul(prow[:, :], ones_bf[:, :], sbf[:, c * TT:(c + 1) * TT],
                            start=(c == 0), stop=(c == NCH - 1))
        b.bump(last)
        b.wait(t)
        for c in range(NCH):
            last = t.matmul(prow2[:, :], ones_bf[:, :], sq[:, c * TT:(c + 1) * TT],
                            start=(c == 0), stop=(c == NCH - 1))
        b.bump(last)
        b.step(v, lambda e: e.tensor_single_scalar(rowbf[:, 0:TT], prow[:, :], 1.0 / D, ALU.mult))
        b.step(s, lambda e: e.activation(rowa[:], rowbf[:, 0:TT], AF.Square))
        b.step(v, lambda e: e.scalar_tensor_tensor(rowa[:], prow2[:, :], 1.0 / D, rowa[:], ALU.mult, ALU.subtract))
        b.step(v, lambda e: e.tensor_single_scalar(rowa[:], rowa[:], 1e-5, ALU.add))
        b.step(v, lambda e: e.reciprocal(rowb[:], rowa[:]))
        b.step(s, lambda e: e.activation(rowbf[:, TT:2 * TT], rowb[:], AF.Sqrt))
        b.step(t, lambda e: e.matmul(pbc1[:, :], ones1f[:, :], rowbf[:, 0:TT], start=True, stop=True))
        b.step(t, lambda e: e.matmul(pbc2[:, :], ones1f[:, :], rowbf[:, TT:2 * TT], start=True, stop=True))
        for c in range(NCH):
            b.step(v, lambda e, c=c: e.tensor_sub(x1t[:, c * TT:(c + 1) * TT], st_[:, c * TT:(c + 1) * TT], pbc1[:, :]))
            b.step(v, lambda e, c=c: e.tensor_mul(x1t[:, c * TT:(c + 1) * TT], x1t[:, c * TT:(c + 1) * TT], pbc2[:, :]))
        b.wait(sy)
        b.bump(sy.dma_start(out=xo.rearrange("(c p) t -> p c t", p=128)[:, :, t0:t0 + TT], in_=x1t[:]), 16)

    b.wait(sy)
    if own:
        stack.__exit__(None, None, None)
    return b



def fm(a):
    a2 = np.asarray(a).reshape(-1, a.shape[-1])
    return np.ascontiguousarray(a2.T)


def l0attn_inputs(inp, core):
    x = inp["L0_attn_x"][2 * core:2 * core + 2]
    v8 = np.zeros((128, 40), np.float32)
    v8[:, 0:8] = inp["L0_attn_norm_w"].reshape(8, 128).T
    v8[:, 8:16] = inp["L0_attn_conv_b"].reshape(8, 128).T
    v8[:, 16:24] = inp["L0_attn_Cb"].reshape(8, 128).T
    v8[:, 24:32] = inp["L0_attn_Db"].reshape(8, 128).T
    v64 = np.zeros((64, 3), np.float32)
    v64[:, 0] = inp["L0_attn_A"]
    v64[:, 1] = inp["L0_attn_Bb"]
    v64[:, 2] = inp["L0_attn_db"]
    return {
        "x_fm": fm(x),
        "l0a_w1t": np.ascontiguousarray(inp["L0_attn_W1"].T),
        "l0a_w2t": np.ascontiguousarray(inp["L0_attn_W2"].T),
        "l0a_dwtBig": np.ascontiguousarray(inp["L0_attn_Dw"].T),
        "l0a_wlt": np.ascontiguousarray(inp["L0_attn_Wlast"].T),
        "l0a_cwt": np.ascontiguousarray(inp["L0_attn_Cw"].T),
        "l0a_dwsm": np.ascontiguousarray(inp["L0_attn_dw"].T),
        "l0a_bwsm": np.ascontiguousarray(inp["L0_attn_Bw"].T),
        "l0a_convw": np.ascontiguousarray(inp["L0_attn_conv_w"]),
        "l0a_vecs8": v8,
        "l0a_vecs64": v64,
    }


def _lng_layout(inp):
    lng = np.zeros((128, 16), np.float32)
    lng[:, 0:8] = inp["L0_ffn_ln_g"].reshape(8, 128).T
    lng[:, 8:16] = inp["L0_ffn_ln_b"].reshape(8, 128).T
    return lng


def _run(build_fn, in_maps, **bkw):
    nc = bass.Bass("TRN2", target_bir_lowering=False, num_devices=8)
    build_fn(nc, **bkw)
    return run_bass_kernel_spmd(nc, in_maps, core_ids=list(range(8))).results


def kernel(**inputs):
    import ml_dtypes
    inp = inputs
    NCORE = 8
    maps = [l0attn_inputs(inp, c) for c in range(NCORE)]
    resA = _run(build_l0attn, maps)
    x1 = [resA[c]["x1_fm"] for c in range(NCORE)]
    lng = _lng_layout(inp)
    wr = np.ascontiguousarray(inp["L0_ffn_Wr"])
    maps = [{"x1_fm": x1[c], "l0f_lng": lng, "l0f_wr": wr} for c in range(NCORE)]
    resB = _run(build_l0route, maps)
    h_all = np.concatenate([np.asarray(resB[c]["h_fm"]).T for c in range(NCORE)], axis=0)
    logits_all = np.ascontiguousarray(np.concatenate([resB[c]["logits_tm"] for c in range(NCORE)], axis=0))
    maps = [{"logits_all": logits_all} for _ in range(NCORE)]
    resC = _run(build_l0rank, maps)
    info = resC[0]["route_info"]
    slot0 = info[:, 0:256].T.ravel().astype(np.int64)
    slot1 = info[:, 256:512].T.ravel().astype(np.int64)
    kg0 = info[:, 512:768].T.ravel().astype(np.float32)
    kg1 = info[:, 768:1024].T.ravel().astype(np.float32)
    NE7 = 7
    table = np.zeros(NE7 * CAP + 1, np.int64)
    for sl in (slot0, slot1):
        valid = sl < NE7 * CAP
        table[sl[valid]] = np.nonzero(valid)[0] % 32768
    maps = []
    for c in range(NCORE):
        e = min(c, NE7 - 1)
        toks = table[e * CAP:(e + 1) * CAP]
        hs = np.zeros((ESLOTS, D), ml_dtypes.bfloat16)
        hs[0:CAP] = h_all[toks]
        maps.append({
            "hs_fm": np.ascontiguousarray(hs.T),
            "l0f_gw": np.ascontiguousarray(inp["L0_ffn_W1"][e]),
            "l0f_uw": np.ascontiguousarray(inp["L0_ffn_W3"][e]),
            "l0f_dw": np.ascontiguousarray(inp["L0_ffn_W2"][e]),
        })
    resD = _run(build_l0slots, maps)
    ye_all = np.zeros((NE7 * CAP + 1, D), np.float32)
    for e in range(NE7):
        ye_all[e * CAP:(e + 1) * CAP] = np.asarray(resD[e]["ye_fm"]).T[0:CAP]
    ye0 = ye_all[slot0]
    ye1 = ye_all[slot1]
    maps = []
    for c in range(NCORE):
        sl = slice(c * TOK, (c + 1) * TOK)
        maps.append({
            "x1_fm": x1[c],
            "ye0_fm": np.ascontiguousarray(ye0[sl].T),
            "ye1_fm": np.ascontiguousarray(ye1[sl].T),
            "kg_rows": np.ascontiguousarray(np.concatenate([kg0[sl], kg1[sl]])[None, :]),
            "l0f_lng": lng,
            "l0f_w1s": np.ascontiguousarray(inp["L0_ffn_W1s"]),
            "l0f_w3s": np.ascontiguousarray(inp["L0_ffn_W3s"]),
            "l0f_w2s": np.ascontiguousarray(inp["L0_ffn_W2s"]),
        })
    resE = _run(build_l0combine, maps)
    x2 = [resE[c]["x2_fm"] for c in range(NCORE)]
    maps = []
    for c in range(NCORE):
        m = l1attn_inputs(inp, c)
        m["x2_fm"] = x2[c]
        maps.append(m)
    resF = _run(build_l1attn, maps)
    x3 = [resF[c]["x3_fm"] for c in range(NCORE)]
    maps = []
    for c in range(NCORE):
        m = l1ffn_inputs(inp, c)
        m["x3_fm"] = x3[c]
        maps.append(m)
    resG = _run(build_l1ffn, maps)
    out = np.empty((16, 2048, 1024), np.float32)
    for c in range(NCORE):
        out[2 * c:2 * c + 2] = np.asarray(resG[c]["x4_fm"]).T.reshape(2, SEQ, D)
    return out



def build_l1ffn(nc, io_in="x3_fm", io_out="x4_fm", stack=None, b=None, dram=None):
    own = stack is None
    if own:
        from contextlib import ExitStack
        stack = ExitStack()
        stack.__enter__()
    if dram is None:
        dram = {}

    def din(name, shape, dtype=F32):
        if name not in dram:
            dram[name] = nc.dram_tensor(name, shape, dtype, kind="ExternalInput").ap()
        return dram[name]

    x = din(io_in, [D, TOK])
    rwt = din("l1f_rwt", [D, 16])
    biasr = din("l1f_bias", [1, 16])
    gw = din("l1f_gw", [16, D, D])
    uw = din("l1f_uw", [16, D, D])
    dw = din("l1f_dw", [16, D, D])
    sgw = din("l1f_sgw", [D, 2048])
    suw = din("l1f_suw", [D, 2048])
    sdw = din("l1f_sdw", [2048, D])
    if io_out not in dram:
        dram[io_out] = nc.dram_tensor(io_out, [D, TOK], F32, kind="ExternalOutput").ap()
    xo = dram[io_out]

    if b is None:
        b = B(nc, stack)
    sy, g, v, s, t = nc.sync, nc.gpsimd, nc.vector, nc.scalar, nc.tensor

    MEGA = 1024
    NMEGA = TOK // MEGA
    TPM = MEGA // TT

    x3b = b.sbuf("f_x3b", [128, NCH * MEGA], BF16)
    xt = b.sbuf("f_xt", [128, NCH * TT], F32)
    racc = b.sbuf("f_racc", [128, NCH * MEGA], F32)
    we_g = b.sbuf("f_weg", [128, NCH * D], BF16)
    we_u = b.sbuf("f_weu", [128, NCH * D], BF16)
    we_d = b.sbuf("f_wed", [128, NCH * D], BF16)
    sh_g = b.sbuf("f_shg", [128, NCH * 128], BF16)
    sh_u = b.sbuf("f_shu", [128, NCH * 128], BF16)
    sh_d = b.sbuf("f_shd", [128, 2 * NCH * 128], BF16)
    mprime = b.sbuf("f_mp", [128, NCH * TT], BF16)
    msh = b.sbuf("f_msh", [128, 2 * NCH * TT], BF16)
    rws = b.sbuf("f_rws", [128, NCH * 16], F32)
    biasb = b.sbuf("f_biasb", [128, 16], F32)
    eall = b.sbuf("f_eall", [16, 16 * 128], BF16)
    iden_sc = b.sbuf("f_idsc", [16, 16 * 128], F32)
    scores = b.sbuf("f_scores", [128, 16], F32)
    splus = b.sbuf("f_splus", [128, 16], F32)
    r1 = b.sbuf("f_r1", [128, 4], F32)
    r2 = b.sbuf("f_r2", [128, 4], F32)
    gsc = b.sbuf("f_gsc", [128, 4], F32)
    grow = b.sbuf("f_grow", [128, 2], F32)
    msel = b.sbuf("f_msel", [128, 16], F32)
    mx8 = b.sbuf("f_mx8", [128, 8], F32)
    ohall = b.sbuf("f_ohall", [128, 16], F32)
    comb = b.sbuf("f_comb", [128, 16], F32)
    combT = b.sbuf("f_combT", [16, MEGA], BF16)
    iden = b.sbuf("f_iden", [128, 128], F32)
    idsc2 = b.sbuf("f_idsc2", [128, 128], F32)
    ones_bf = b.sbuf("f_onesbf", [128, 1], BF16)
    ones1f = b.sbuf("f_ones1f", [1, 128], F32)
    sq = b.sbuf("f_sq", [128, NCH * TT], BF16)
    sbf = b.sbuf("f_sbf", [128, NCH * TT], BF16)
    st_ = b.sbuf("f_st", [128, 2 * TT], F32)
    x4t = b.sbuf("f_x4t", [128, NCH * TT], F32)
    rowa = b.sbuf("f_rowa", [1, TT], F32)
    rowb = b.sbuf("f_rowb", [1, TT], F32)
    rowbf = b.sbuf("f_rowbf", [1, 2 * TT], F32)

    pa = b.psum("f_pa", [128, TT])
    pu = b.psum("f_pu", [128, TT])
    pc = b.psum("f_pc", [128, TT])
    pdn0 = b.psum("f_pdn0", [128, TT])
    pdn1 = b.psum("f_pdn1", [128, TT])
    pbc1 = b.psum("f_pbc1", [128, TT])
    pbc2 = b.psum("f_pbc2", [128, TT])

    b.step(v, lambda e: e.memset(ones_bf[:], 1.0))
    b.step(v, lambda e: e.memset(ones1f[:], 1.0))
    b.step(g, lambda e: e.iota(iden_sc[:].rearrange("p (e i) -> p e i", e=16), pattern=[[-1, 16], [0, 128]],
                               base=0, channel_multiplier=1, allow_small_or_imprecise_dtypes=True))
    b.step(v, lambda e: e.tensor_single_scalar(eall[:], iden_sc[:], 0.0, ALU.is_equal))
    b.step(g, lambda e: e.iota(idsc2[:], pattern=[[-1, 128]], base=0, channel_multiplier=1,
                               allow_small_or_imprecise_dtypes=True))
    b.step(v, lambda e: e.tensor_single_scalar(iden[:], idsc2[:], 0.0, ALU.is_equal))
    b.wait(sy)
    b.bump(sy.dma_start(out=rws[:].rearrange("p (c j) -> p c j", c=NCH),
                        in_=rwt.rearrange("(c p) j -> p c j", p=128)), 16)
    biasrow = b.sbuf("f_biasrow", [1, 16], F32)
    b.wait(sy)
    b.bump(sy.dma_start(out=biasrow[:], in_=biasr[:, :]), 16)
    b.step(t, lambda e: e.matmul(pc[:, 0:16], ones1f[:, :], biasrow[:, :], start=True, stop=True))
    b.step(v, lambda e: e.tensor_copy(biasb[:], pc[:, 0:16]))

    for mega in range(NMEGA):
        m0 = mega * MEGA
        b.wait(g)
        b.bump(g.dma_start(out=x3b[:].rearrange("p (c t) -> p c t", c=NCH),
                           in_=x.rearrange("(c p) t -> p c t", p=128)[:, :, m0:m0 + MEGA]), 16)
        for ti in range(TPM):
            t0 = m0 + ti * TT
            b.wait(sy)
            b.bump(sy.dma_start(out=xt[:].rearrange("p (c t) -> p c t", c=NCH),
                                in_=x.rearrange("(c p) t -> p c t", p=128)[:, :, t0:t0 + TT]), 16)
            for sub in range(TT // 128):
                so = sub * 128
                b.wait(t)
                for dc in range(NCH):
                    last = t.matmul(pc[:, 0:16], xt[:, dc * TT + so: dc * TT + so + 128],
                                    rws[:, dc * 16:(dc + 1) * 16], start=(dc == 0), stop=(dc == NCH - 1))
                b.bump(last)
                b.step(s, lambda e: e.activation(scores[:], pc[:, 0:16], AF.Sigmoid))
                b.step(v, lambda e: e.tensor_add(splus[:], scores[:], biasb[:]))
                b.step(v, lambda e: e.tensor_reduce(r1[:].rearrange("p (g o) -> p g o", o=1),
                                                    splus[:].rearrange("p (g e) -> p g e", g=4),
                                                    mybir.AxisListType.X, ALU.max))
                b.step(v, lambda e: e.tensor_tensor(msel[:].rearrange("p (g e) -> p g e", g=4),
                                                    splus[:].rearrange("p (g e) -> p g e", g=4),
                                                    r1[:].rearrange("p (g o) -> p g o", o=1).to_broadcast([128, 4, 4]),
                                                    ALU.is_equal))
                b.step(v, lambda e: e.scalar_tensor_tensor(msel[:], msel[:], -1e30, splus[:], ALU.mult, ALU.add))
                b.step(v, lambda e: e.tensor_reduce(r2[:].rearrange("p (g o) -> p g o", o=1),
                                                    msel[:].rearrange("p (g e) -> p g e", g=4),
                                                    mybir.AxisListType.X, ALU.max))
                b.step(v, lambda e: e.tensor_add(gsc[:], r1[:], r2[:]))
                b.step(v, lambda e: e.tensor_reduce(grow[:, 0:1], gsc[:], mybir.AxisListType.X, ALU.max))
                b.step(v, lambda e: e.tensor_single_scalar(r2[:], gsc[:], grow[:, 0:1], ALU.is_equal))
                b.step(v, lambda e: e.scalar_tensor_tensor(r2[:], r2[:], -1e30, gsc[:], ALU.mult, ALU.add))
                b.step(v, lambda e: e.tensor_reduce(grow[:, 1:2], r2[:], mybir.AxisListType.X, ALU.max))
                b.step(v, lambda e: e.tensor_single_scalar(gsc[:], gsc[:], grow[:, 1:2], ALU.is_ge))
                b.step(v, lambda e: e.tensor_single_scalar(splus[:], splus[:], 1.0, ALU.add))
                b.step(v, lambda e: e.tensor_tensor(msel[:].rearrange("p (g e) -> p g e", g=4),
                                                    splus[:].rearrange("p (g e) -> p g e", g=4),
                                                    gsc[:].rearrange("p (g o) -> p g o", o=1).to_broadcast([128, 4, 4]),
                                                    ALU.mult))
                b.step(v, lambda e: e.tensor_single_scalar(msel[:], msel[:], -1.0, ALU.add))
                b.step(v, lambda e: e.max(out=mx8[:], in_=msel[:]))
                b.step(v, lambda e: e.tensor_single_scalar(ohall[:], msel[:], mx8[:, 3:4], ALU.is_ge))
                b.step(v, lambda e: e.tensor_mul(ohall[:], ohall[:], scores[:]))
                b.step(v, lambda e: e.tensor_reduce(grow[:, 0:1], ohall[:], mybir.AxisListType.X, ALU.add))
                b.step(v, lambda e: e.tensor_single_scalar(grow[:, 0:1], grow[:, 0:1], 1e-20, ALU.add))
                b.step(v, lambda e: e.reciprocal(grow[:, 1:2], grow[:, 0:1]))
                b.step(v, lambda e: e.tensor_scalar(comb[:], ohall[:], grow[:, 1:2], 2.5, ALU.mult, ALU.mult))
                b.step(t, lambda e: e.transpose(pc[0:16, 0:128], comb[:], iden[:]))
                b.step(v, lambda e, ti=ti, so=so: e.tensor_copy(combT[:, ti * TT + so: ti * TT + so + 128],
                                                                pc[0:16, 0:128]))
        for e_i in range(16):
            b.wait(g)
            b.bump(g.dma_start(out=we_g[:].rearrange("p (c f) -> p c f", c=NCH),
                               in_=gw[e_i].rearrange("(c p) f -> p c f", p=128)), 16)
            b.wait(g)
            b.bump(g.dma_start(out=we_u[:].rearrange("p (c f) -> p c f", c=NCH),
                               in_=uw[e_i].rearrange("(c p) f -> p c f", p=128)), 16)
            b.wait(g)
            b.bump(g.dma_start(out=we_d[:].rearrange("p (c f) -> p c f", c=NCH),
                               in_=dw[e_i].rearrange("(c p) f -> p c f", p=128)), 16)
            for ti in range(TPM):
                ts0 = ti * TT
                b.step(t, lambda e, e_i=e_i, ts0=ts0: e.matmul(
                    pc[:, :], eall[:, e_i * 128:(e_i + 1) * 128], combT[:, ts0:ts0 + TT], start=True, stop=True))
                for fc in range(NCH):
                    b.wait(t)
                    for dc in range(NCH):
                        last = t.matmul(pa[:, :], we_g[:, dc * D + fc * 128: dc * D + (fc + 1) * 128],
                                        x3b[:, dc * MEGA + ts0: dc * MEGA + ts0 + TT],
                                        start=(dc == 0), stop=(dc == NCH - 1))
                    b.bump(last)
                    b.wait(t)
                    for dc in range(NCH):
                        last = t.matmul(pu[:, :], we_u[:, dc * D + fc * 128: dc * D + (fc + 1) * 128],
                                        x3b[:, dc * MEGA + ts0: dc * MEGA + ts0 + TT],
                                        start=(dc == 0), stop=(dc == NCH - 1))
                    b.bump(last)
                    b.step(s, lambda e, fc=fc: e.activation(st_[:, 0:TT], pa[:, :], AF.Silu))
                    b.step(v, lambda e, fc=fc: e.tensor_mul(st_[:, TT:2 * TT], st_[:, 0:TT], pu[:, :]))
                    b.step(v, lambda e, fc=fc: e.tensor_mul(mprime[:, fc * TT:(fc + 1) * TT], st_[:, TT:2 * TT], pc[:, :]))
                for dc in range(NCH):
                    pd_ = pdn0 if dc % 2 == 0 else pdn1
                    b.wait(t)
                    for fc in range(NCH):
                        last = t.matmul(pd_[:, :], we_d[:, fc * D + dc * 128: fc * D + (dc + 1) * 128],
                                        mprime[:, fc * TT:(fc + 1) * TT],
                                        start=(fc == 0), stop=(fc == NCH - 1))
                    b.bump(last)
                    if e_i == 0:
                        b.step(v, lambda e, dc=dc, ts0=ts0: e.tensor_copy(
                            racc[:, dc * MEGA + ts0: dc * MEGA + ts0 + TT], pd_[:, :]))
                    else:
                        b.step(v, lambda e, dc=dc, ts0=ts0: e.tensor_add(
                            racc[:, dc * MEGA + ts0: dc * MEGA + ts0 + TT],
                            racc[:, dc * MEGA + ts0: dc * MEGA + ts0 + TT], pd_[:, :]))
        for ti in range(TPM):
            ts0 = ti * TT
            for fc in range(2 * NCH):
                if ti == 0:
                    b.wait(g)
                    b.bump(g.dma_start(out=sh_g[:].rearrange("p (c f) -> p c f", c=NCH),
                                       in_=sgw.rearrange("(c p) f -> p c f", p=128)[:, :, fc * 128:(fc + 1) * 128]), 16)
                    b.wait(g)
                    b.bump(g.dma_start(out=sh_u[:].rearrange("p (c f) -> p c f", c=NCH),
                                       in_=suw.rearrange("(c p) f -> p c f", p=128)[:, :, fc * 128:(fc + 1) * 128]), 16)
                elif True:
                    b.wait(g)
                    b.bump(g.dma_start(out=sh_g[:].rearrange("p (c f) -> p c f", c=NCH),
                                       in_=sgw.rearrange("(c p) f -> p c f", p=128)[:, :, fc * 128:(fc + 1) * 128]), 16)
                    b.wait(g)
                    b.bump(g.dma_start(out=sh_u[:].rearrange("p (c f) -> p c f", c=NCH),
                                       in_=suw.rearrange("(c p) f -> p c f", p=128)[:, :, fc * 128:(fc + 1) * 128]), 16)
                b.wait(t)
                for dc in range(NCH):
                    last = t.matmul(pa[:, :], sh_g[:, dc * 128: (dc + 1) * 128],
                                    x3b[:, dc * MEGA + ts0: dc * MEGA + ts0 + TT],
                                    start=(dc == 0), stop=(dc == NCH - 1))
                b.bump(last)
                b.wait(t)
                for dc in range(NCH):
                    last = t.matmul(pu[:, :], sh_u[:, dc * 128: (dc + 1) * 128],
                                    x3b[:, dc * MEGA + ts0: dc * MEGA + ts0 + TT],
                                    start=(dc == 0), stop=(dc == NCH - 1))
                b.bump(last)
                b.step(s, lambda e: e.activation(st_[:, 0:TT], pa[:, :], AF.Silu))
                b.step(v, lambda e, fc=fc: e.tensor_mul(msh[:, fc * TT:(fc + 1) * TT], st_[:, 0:TT], pu[:, :]))
            for dc in range(NCH):
                pd_ = pdn0 if dc % 2 == 0 else pdn1
                b.wait(g)
                b.bump(g.dma_start(out=sh_d[:].rearrange("p (c f) -> p c f", c=2 * NCH),
                                   in_=sdw.rearrange("(c p) f -> p c f", p=128)[:, :, dc * 128:(dc + 1) * 128]), 16)
                b.wait(t)
                for fc in range(2 * NCH):
                    last = t.matmul(pd_[:, :], sh_d[:, fc * 128: (fc + 1) * 128],
                                    msh[:, fc * TT:(fc + 1) * TT],
                                    start=(fc == 0), stop=(fc == 2 * NCH - 1))
                b.bump(last)
                b.step(v, lambda e, dc=dc, ts0=ts0: e.tensor_add(
                    racc[:, dc * MEGA + ts0: dc * MEGA + ts0 + TT],
                    racc[:, dc * MEGA + ts0: dc * MEGA + ts0 + TT], pd_[:, :]))
        for ti in range(TPM):
            t0 = m0 + ti * TT
            ts0 = ti * TT
            b.wait(sy)
            b.bump(sy.dma_start(out=xt[:].rearrange("p (c t) -> p c t", c=NCH),
                                in_=x.rearrange("(c p) t -> p c t", p=128)[:, :, t0:t0 + TT]), 16)
            for c in range(NCH):
                b.step(v, lambda e, c=c, ts0=ts0: e.tensor_add(
                    x4t[:, c * TT:(c + 1) * TT], xt[:, c * TT:(c + 1) * TT], racc[:, c * MEGA + ts0: c * MEGA + ts0 + TT]))
            b.step(v, lambda e: e.tensor_copy(sbf[:], x4t[:]))
            b.step(s, lambda e: e.activation(sq[:], x4t[:], AF.Square))
            b.wait(t)
            for c in range(NCH):
                last = t.matmul(pa[0:1, :], ones_bf[:, :], sbf[:, c * TT:(c + 1) * TT],
                                start=(c == 0), stop=(c == NCH - 1))
            b.bump(last)
            b.wait(t)
            for c in range(NCH):
                last = t.matmul(pu[0:1, :], ones_bf[:, :], sq[:, c * TT:(c + 1) * TT],
                                start=(c == 0), stop=(c == NCH - 1))
            b.bump(last)
            b.step(v, lambda e: e.tensor_single_scalar(rowbf[:, 0:TT], pa[0:1, :], 1.0 / D, ALU.mult))
            b.step(s, lambda e: e.activation(rowa[:], rowbf[:, 0:TT], AF.Square))
            b.step(v, lambda e: e.scalar_tensor_tensor(rowa[:], pu[0:1, :], 1.0 / D, rowa[:], ALU.mult, ALU.subtract))
            b.step(v, lambda e: e.tensor_single_scalar(rowa[:], rowa[:], 1e-5, ALU.add))
            b.step(v, lambda e: e.reciprocal(rowb[:], rowa[:]))
            b.step(s, lambda e: e.activation(rowbf[:, TT:2 * TT], rowb[:], AF.Sqrt))
            b.step(t, lambda e: e.matmul(pbc1[:, :], ones1f[:, :], rowbf[:, 0:TT], start=True, stop=True))
            b.step(t, lambda e: e.matmul(pbc2[:, :], ones1f[:, :], rowbf[:, TT:2 * TT], start=True, stop=True))
            for c in range(NCH):
                b.step(v, lambda e, c=c: e.tensor_sub(x4t[:, c * TT:(c + 1) * TT], x4t[:, c * TT:(c + 1) * TT], pbc1[:, :]))
                b.step(v, lambda e, c=c: e.tensor_mul(x4t[:, c * TT:(c + 1) * TT], x4t[:, c * TT:(c + 1) * TT], pbc2[:, :]))
            b.wait(sy)
            b.bump(sy.dma_start(out=xo.rearrange("(c p) t -> p c t", p=128)[:, :, t0:t0 + TT], in_=x4t[:]), 16)

    b.wait(sy)
    if own:
        stack.__exit__(None, None, None)
    return b


def l1ffn_inputs(inp, core):
    return {
        "l1f_rwt": np.ascontiguousarray(inp["L1_ffn_router_w"].T),
        "l1f_bias": np.ascontiguousarray(inp["L1_ffn_bias"][None, :]),
        "l1f_gw": np.ascontiguousarray(inp["L1_ffn_gate_w"]),
        "l1f_uw": np.ascontiguousarray(inp["L1_ffn_up_w"]),
        "l1f_dw": np.ascontiguousarray(inp["L1_ffn_down_w"]),
        "l1f_sgw": np.ascontiguousarray(inp["L1_ffn_sh_gate_w"]),
        "l1f_suw": np.ascontiguousarray(inp["L1_ffn_sh_up_w"]),
        "l1f_sdw": np.ascontiguousarray(inp["L1_ffn_sh_down_w"]),
    }



def build_l1attn(nc, io_in="x2_fm", io_out="x3_fm", stack=None, b=None, dram=None, dbg=False):
    own = stack is None
    if own:
        from contextlib import ExitStack
        stack = ExitStack()
        stack.__enter__()
    if dram is None:
        dram = {}

    def din(name, shape, dtype=F32):
        if name not in dram:
            dram[name] = nc.dram_tensor(name, shape, dtype, kind="ExternalInput").ap()
        return dram[name]

    x = din(io_in, [D, TOK])
    wx1t = din("l1a_wx1t", [D, 160])
    wx2r = din("l1a_wx2r", [32, 5 * D])
    xbias = din("l1a_xbias", [128, 5 * 8])
    wrt = din("l1a_wrt", [D, 512])
    wkt = din("l1a_wkt", [D, 512])
    wvt = din("l1a_wvt", [D, D])
    wgt = din("l1a_wgt", [D, D])
    awt = din("l1a_awt", [D, 64])
    bwt = din("l1a_bwt", [64, 512])
    bw_b = din("l1a_bwb", [128, 4])
    bonus = din("l1a_bonus", [128, 4])
    gnwb = din("l1a_gnwb", [1, 2 * D])
    wot = din("l1a_wot", [D, D])
    mux = din("l1a_mux", [128, 8])
    if io_out not in dram:
        dram[io_out] = nc.dram_tensor(io_out, [D, TOK], F32, kind="ExternalOutput").ap()
    xo = dram[io_out]

    if b is None:
        b = B(nc, stack)
    sy, g, v, s, t = nc.sync, nc.gpsimd, nc.vector, nc.scalar, nc.tensor
    HD = 4
    CK = 64
    NCHK = TT // CK

    wx1s = b.sbuf("a_wx1s", [128, NCH * 160], BF16)
    wx2s = b.sbuf("a_wx2s", [32, 5 * D], BF16)
    xb_s = b.sbuf("a_xbs", [128, 40], F32)
    wvs = b.sbuf("a_wvs", [128, NCH * D], BF16)
    wgs = b.sbuf("a_wgs", [128, NCH * 512], BF16)
    wrs = wgs
    wks = wgs
    aws = b.sbuf("a_aws", [128, NCH * 64], BF16)
    bws = b.sbuf("a_bws", [64, 512], BF16)
    bwb_s = b.sbuf("a_bwbs", [128, 4], F32)
    u_s = b.sbuf("a_us", [128, 4], F32)
    mux_s = b.sbuf("a_muxs", [128, 8], F32)
    wos = b.sbuf("a_wos", [128, NCH * 128], BF16)
    gnw_b = b.sbuf("a_gnwb", [128, D], F32)
    gnb_b = b.sbuf("a_gnbb", [128, D], F32)
    gnrow = b.sbuf("a_gnrow", [1, 2 * D], F32)
    ones_bf = b.sbuf("a_onesbf", [128, 1], BF16)
    ones1f = b.sbuf("a_ones1f", [1, 128], F32)
    zerosf = b.sbuf("a_zerosf", [128, TT], F32)
    maskLT = b.sbuf("a_maskLT", [64, 64], mybir.dt.uint8)

    xt = b.sbuf("a_xt", [128, NCH * (TT + 1)], BF16)
    dlt = b.sbuf("a_dlt", [128, NCH * TT], BF16)
    dltf = b.sbuf("a_dltf", [128, TT], F32)
    xm = b.sbuf("a_xm", [128, NCH * TT], BF16)
    zall = b.sbuf("a_zall", [32, 5 * TT], BF16)
    zsl5 = [zall[:, n * TT:(n + 1) * TT] for n in range(5)]
    mubuf = b.sbuf("a_mubuf", [128, TT], F32)
    r_bf = b.sbuf("a_rbf", [128, 4 * TT], BF16)
    k_bf = b.sbuf("a_kbf", [128, 4 * TT], BF16)
    wp = b.sbuf("a_wp", [128, 4 * TT], F32)
    cumw = b.sbuf("a_cumw", [128, 4 * TT], F32)
    v_tm = b.sbuf("a_vtm", [64, NCHK * D], BF16)
    g_tm = b.sbuf("a_gtm", [64, NCHK * D], BF16)
    qmid = b.sbuf("a_qmid", [128, TT], BF16)
    kmid = b.sbuf("a_kmid", [128, TT], BF16)
    qst = b.sbuf("a_qst", [128, TT], BF16)
    kendf = b.sbuf("a_kendf", [128, TT], F32)
    kend_tm = b.sbuf("a_kendtm", [64, 128], BF16)
    at_bf = b.sbuf("a_atbf", [64, 64], BF16)
    o_buf = b.sbuf("a_obuf", [64, NCHK * D], BF16)
    sdiag = b.sbuf("a_sdiag", [64, 4 * NCHK], F32)
    scr1 = b.sbuf("a_scr1", [128, TT], F32)
    scr2 = b.sbuf("a_scr2", [128, TT], F32)
    scr3 = b.sbuf("a_scr3", [128, TT], BF16)
    bnd = b.sbuf("a_bnd", [128, 4 * 24], F32)
    dend8 = b.sbuf("a_dend8", [128, 4 * 8], F32)
    carry_w = b.sbuf("a_carryw", [128, 4], F32)
    prev_carry = b.sbuf("a_prevc", [128, 4], F32)
    S_s = b.sbuf("a_Ss", [128, 4 * 256], F32)
    S_bf = b.sbuf("a_Sbf", [128, 4 * 256], BF16)
    iden_a = b.sbuf("a_idena", [128, 128], F32)
    iden_b = b.sbuf("a_idenb", [64, 64], BF16)
    gnm = b.sbuf("a_gnm", [64, 2 * HD], F32)
    st_ = b.sbuf("a_st", [128, NCH * TT], F32)
    sq = xm
    sbf = dlt
    rowa = scr1[0:1, :]
    rowb = scr2[0:1, :]
    row_mu = dltf[0:1, :]
    row_rs = kendf[0:1, :]

    p0 = b.psum("a_p0", [128, TT])
    p1 = b.psum("a_p1", [128, TT])
    p2 = b.psum("a_p2", [128, TT])
    p3 = b.psum("a_p3", [128, TT])
    p4 = b.psum("a_p4", [128, TT])
    p5 = b.psum("a_p5", [128, TT])
    p6 = b.psum("a_p6", [128, TT])
    pbf = b.psum("a_pbf", [128, 256], BF16)

    def ldw(dst, src, c=NCH):
        b.wait(g)
        b.bump(g.dma_start(out=dst[:].rearrange("p (c j) -> p c j", c=c),
                           in_=src.rearrange("(c p) j -> p c j", p=128)), 16)

    ldw(wx1s, wx1t)
    b.wait(g); b.bump(g.dma_start(out=wx2s[:], in_=wx2r[:, :]), 16)
    ldw(wvs, wvt); ldw(aws, awt)
    b.wait(g); b.bump(g.dma_start(out=bws[:], in_=bwt[:, :]), 16)
    b.wait(sy); b.bump(sy.dma_start(out=xb_s[:], in_=xbias[:, :]), 16)
    b.wait(sy); b.bump(sy.dma_start(out=bwb_s[:], in_=bw_b[:, :]), 16)
    b.wait(sy); b.bump(sy.dma_start(out=u_s[:], in_=bonus[:, :]), 16)
    b.wait(sy); b.bump(sy.dma_start(out=mux_s[:], in_=mux[:, :]), 16)
    b.wait(sy); b.bump(sy.dma_start(out=gnrow[:], in_=gnwb[:, :]), 16)

    b.step(v, lambda e: e.memset(ones_bf[:], 1.0))
    b.step(v, lambda e: e.memset(ones1f[:], 1.0))
    b.step(v, lambda e: e.memset(zerosf[:], 0.0))
    b.step(g, lambda e: e.iota(scr1[0:64, 0:64], pattern=[[1, 64]], base=0, channel_multiplier=-1,
                               allow_small_or_imprecise_dtypes=True))
    b.step(v, lambda e: e.tensor_single_scalar(maskLT[:], scr1[0:64, 0:64], 0.0, ALU.is_gt))
    b.step(g, lambda e: e.iota(scr1[:, 0:128], pattern=[[-1, 128]], base=0, channel_multiplier=1,
                               allow_small_or_imprecise_dtypes=True))
    b.step(v, lambda e: e.tensor_single_scalar(iden_a[:], scr1[:, 0:128], 0.0, ALU.is_equal))
    b.step(v, lambda e: e.tensor_single_scalar(iden_b[:], scr1[0:64, 0:64], 0.0, ALU.is_equal))
    b.step(t, lambda e: e.matmul(p0[:, :], ones1f[:, :], gnrow[:, 0:512], start=True, stop=True))
    b.step(t, lambda e: e.matmul(p1[:, :], ones1f[:, :], gnrow[:, 512:1024], start=True, stop=True))
    b.step(v, lambda e: e.tensor_copy(gnw_b[:, 0:512], p0[:, :]))
    b.step(v, lambda e: e.tensor_copy(gnw_b[:, 512:1024], p1[:, :]))
    b.step(t, lambda e: e.matmul(p0[:, :], ones1f[:, :], gnrow[:, 1024:1536], start=True, stop=True))
    b.step(t, lambda e: e.matmul(p1[:, :], ones1f[:, :], gnrow[:, 1536:2048], start=True, stop=True))
    b.step(v, lambda e: e.tensor_copy(gnb_b[:, 0:512], p0[:, :]))
    b.step(v, lambda e: e.tensor_copy(gnb_b[:, 512:1024], p1[:, :]))

    for tile in range(NTILES):
        t0 = tile * TT
        seq_start = (tile % (NTILES // NSEQ)) == 0
        xv = xt[:].rearrange("p (c t) -> p c t", c=NCH)
        b.wait(g)
        if seq_start:
            b.bump(g.dma_start(out=xv[:, :, 1:TT + 1],
                               in_=x.rearrange("(c p) t -> p c t", p=128)[:, :, t0:t0 + TT]), 16)
            b.step(v, lambda e: e.memset(xv[:, :, 0:1], 0.0))
        else:
            b.bump(g.dma_start(out=xv,
                               in_=x.rearrange("(c p) t -> p c t", p=128)[:, :, t0 - 1:t0 + TT]), 16)
        b.step(v, lambda e: e.tensor_sub(dlt[:].rearrange("p (c t) -> p c t", c=NCH),
                                         xv[:, :, 0:TT], xv[:, :, 1:TT + 1]))
        for c in range(NCH):
            b.step(v, lambda e, c=c: e.scalar_tensor_tensor(
                xm[:, c * TT:(c + 1) * TT], dlt[:, c * TT:(c + 1) * TT], mux_s[:, c:c + 1],
                xv[:, c, 1:TT + 1], ALU.mult, ALU.add))
        zp = [p0, p1, p2, p3, p4]
        for n in range(5):
            b.wait(t)
            for dc in range(NCH):
                last = t.matmul(zp[n][0:32, :], wx1s[:, dc * 160 + n * 32: dc * 160 + (n + 1) * 32],
                                xm[:, dc * TT:(dc + 1) * TT], start=(dc == 0), stop=(dc == NCH - 1))
            b.bump(last)
        for n in range(5):
            b.step(s, lambda e, n=n: e.activation(zsl5[n], zp[n][0:32, :], AF.Tanh))
        zsl = zsl5

        def make_xm(n):
            for dc in range(NCH):
                b.step(t, lambda e, n=n, dc=dc: e.matmul(
                    p3[:, :], wx2s[:, n * D + dc * 128:n * D + (dc + 1) * 128], zsl[n],
                    start=True, stop=True))
                b.step(s, lambda e, n=n, dc=dc: e.activation(
                    mubuf[:, :], p3[:, :], AF.Identity, bias=xb_s[:, n * 8 + dc:n * 8 + dc + 1], scale=1.0))
                b.step(v, lambda e, dc=dc: e.tensor_mul(dltf[:, :], dlt[:, dc * TT:(dc + 1) * TT], mubuf[:, :]))
                b.step(v, lambda e, dc=dc: e.tensor_add(
                    xm[:, dc * TT:(dc + 1) * TT], dltf[:, :], xv[:, dc, 1:TT + 1]))

        make_xm(0)
        b.wait(g)
        b.bump(g.dma_start(out=wgs[:].rearrange("p (c j) -> p c j", c=NCH),
                           in_=wrt.rearrange("(c p) j -> p c j", p=128)), 16)
        for jc in range(4):
            b.wait(t)
            for dc in range(NCH):
                last = t.matmul(p4[:, :], wrs[:, dc * 512 + jc * 128: dc * 512 + (jc + 1) * 128],
                                xm[:, dc * TT:(dc + 1) * TT], start=(dc == 0), stop=(dc == NCH - 1))
            b.bump(last)
            b.step(v, lambda e, jc=jc: e.tensor_copy(r_bf[:, jc * TT:(jc + 1) * TT], p4[:, :]))
        make_xm(1)
        b.wait(t)
        for dc in range(NCH):
            last = t.matmul(p4[0:64, :], aws[:, dc * 64:(dc + 1) * 64],
                            xm[:, dc * TT:(dc + 1) * TT], start=(dc == 0), stop=(dc == NCH - 1))
        b.bump(last)
        b.step(s, lambda e: e.activation(scr3[0:64, :], p4[0:64, :], AF.Tanh))
        for jc in range(4):
            b.step(t, lambda e, jc=jc: e.matmul(p5[:, :], bws[:, jc * 128:(jc + 1) * 128],
                                                scr3[0:64, :], start=True, stop=True))
            b.step(s, lambda e, jc=jc: e.activation(scr1[:, :], p5[:, :], AF.Identity,
                                                    bias=bwb_s[:, jc:jc + 1], scale=1.0))
            b.step(s, lambda e, jc=jc: e.activation(wp[:, jc * TT:(jc + 1) * TT], scr1[:, :], AF.Exp))
        make_xm(2)
        b.wait(g)
        b.bump(g.dma_start(out=wgs[:].rearrange("p (c j) -> p c j", c=NCH),
                           in_=wkt.rearrange("(c p) j -> p c j", p=128)), 16)
        for jc in range(4):
            b.wait(t)
            for dc in range(NCH):
                last = t.matmul(p4[:, :], wks[:, dc * 512 + jc * 128: dc * 512 + (jc + 1) * 128],
                                xm[:, dc * TT:(dc + 1) * TT], start=(dc == 0), stop=(dc == NCH - 1))
            b.bump(last)
            b.step(v, lambda e, jc=jc: e.tensor_copy(k_bf[:, jc * TT:(jc + 1) * TT], p4[:, :]))
        make_xm(3)
        for cc in range(NCHK):
            for half in range(2):
                b.wait(t)
                for dc in range(NCH):
                    last = t.matmul(p4[0:64, :], xm[:, dc * TT + cc * 64: dc * TT + (cc + 1) * 64],
                                    wvs[:, dc * D + half * 512: dc * D + (half + 1) * 512],
                                    start=(dc == 0), stop=(dc == NCH - 1))
                b.bump(last)
                b.step(v, lambda e, cc=cc, half=half: e.tensor_copy(
                    v_tm[:, cc * D + half * 512: cc * D + (half + 1) * 512], p4[0:64, :]))
        make_xm(4)
        for half in range(2):
            b.wait(g)
            b.bump(g.dma_start(out=wgs[:].rearrange("p (c j) -> p c j", c=NCH),
                               in_=wgt.rearrange("(c p) j -> p c j", p=128)[:, :, half * 512:(half + 1) * 512]), 16)
            for cc in range(NCHK):
                b.wait(t)
                for dc in range(NCH):
                    last = t.matmul(p4[0:64, :], xm[:, dc * TT + cc * 64: dc * TT + (cc + 1) * 64],
                                    wgs[:, dc * 512: (dc + 1) * 512],
                                    start=(dc == 0), stop=(dc == NCH - 1))
                b.bump(last)
                b.step(v, lambda e, cc=cc, half=half: e.tensor_copy(
                    g_tm[:, cc * D + half * 512: cc * D + (half + 1) * 512], p4[0:64, :]))

        if seq_start:
            b.step(v, lambda e: e.memset(carry_w[:], 0.0))
            b.step(v, lambda e: e.memset(S_s[:], 0.0))
            b.step(v, lambda e: e.memset(S_bf[:], 0.0))
        b.step(v, lambda e: e.tensor_copy(prev_carry[:], carry_w[:]))
        for hc in range(HD):
            hsl = slice(hc * TT, (hc + 1) * TT)
            b.step(v, lambda e, hsl=hsl, hc=hc: e.tensor_tensor_scan(
                cumw[:, hsl], wp[:, hsl], zerosf[:, :], carry_w[:, hc:hc + 1], ALU.add, ALU.add))
            b.step(v, lambda e, hc=hc: e.tensor_copy(carry_w[:, hc:hc + 1], cumw[:, hc * TT + TT - 1: hc * TT + TT]))
        for hc in range(HD):
            bb = hc * 24
            b.step(v, lambda e, hc=hc, bb=bb: e.tensor_copy(
                bnd[:, bb:bb + 8], cumw[:, hc * TT + CK // 2 - 1: (hc + 1) * TT: CK]))
            b.step(v, lambda e, hc=hc, bb=bb: e.tensor_copy(
                bnd[:, bb + 8:bb + 16], cumw[:, hc * TT + CK - 1: (hc + 1) * TT: CK]))
            if seq_start:
                b.step(v, lambda e, bb=bb: e.memset(bnd[:, bb + 16:bb + 17], 0.0))
            else:
                b.step(v, lambda e, bb=bb, hc=hc: e.tensor_copy(bnd[:, bb + 16:bb + 17], prev_carry[:, hc:hc + 1]))
            b.step(v, lambda e, bb=bb: e.tensor_copy(bnd[:, bb + 17:bb + 24], bnd[:, bb + 8:bb + 15]))
            b.step(v, lambda e, hc=hc, bb=bb: e.tensor_sub(dend8[:, hc * 8:(hc + 1) * 8],
                                                           bnd[:, bb + 16:bb + 24], bnd[:, bb + 8:bb + 16]))
            b.step(s, lambda e, hc=hc: e.activation(dend8[:, hc * 8:(hc + 1) * 8],
                                                    dend8[:, hc * 8:(hc + 1) * 8], AF.Exp))

        for hc in range(HD):
            hsl = slice(hc * TT, (hc + 1) * TT)
            bb = hc * 24
            cview = cumw[:, hsl].rearrange("p (n c) -> p n c", n=NCHK)
            b.step(v, lambda e, hsl=hsl: e.tensor_sub(scr1[:, :], cumw[:, hsl], wp[:, hsl]))
            ecview = scr1[:, :].rearrange("p (n c) -> p n c", n=NCHK)
            scv = scr2[:, :].rearrange("p (n c) -> p n c", n=NCHK)
            b.step(v, lambda e, bb=bb, ecview=ecview, scv=scv: e.tensor_sub(
                scv, ecview, bnd[:, bb:bb + 8].rearrange("p (n o) -> p n o", o=1).to_broadcast([128, NCHK, CK])))
            b.step(s, lambda e: e.activation(scr2[:, :], scr2[:, :], AF.Exp, bias=0.0, scale=-1.0))
            b.step(v, lambda e, hsl=hsl: e.tensor_mul(qmid[:, :], r_bf[:, hsl], scr2[:, :]))
            b.step(v, lambda e, bb=bb, cview=cview, scv=scv: e.tensor_sub(
                scv, bnd[:, bb:bb + 8].rearrange("p (n o) -> p n o", o=1).to_broadcast([128, NCHK, CK]), cview))
            b.step(s, lambda e: e.activation(scr2[:, :], scr2[:, :], AF.Exp, bias=0.0, scale=-1.0))
            b.step(v, lambda e, hsl=hsl: e.tensor_mul(kmid[:, :], k_bf[:, hsl], scr2[:, :]))
            b.step(v, lambda e, bb=bb, ecview=ecview, scv=scv: e.tensor_sub(
                scv, ecview, bnd[:, bb + 16:bb + 24].rearrange("p (n o) -> p n o", o=1).to_broadcast([128, NCHK, CK])))
            b.step(s, lambda e: e.activation(scr2[:, :], scr2[:, :], AF.Exp, bias=0.0, scale=-1.0))
            b.step(v, lambda e, hsl=hsl: e.tensor_mul(qst[:, :], r_bf[:, hsl], scr2[:, :]))
            b.step(v, lambda e, bb=bb, cview=cview, scv=scv: e.tensor_sub(
                scv, cview, bnd[:, bb + 8:bb + 16].rearrange("p (n o) -> p n o", o=1).to_broadcast([128, NCHK, CK])))
            b.step(s, lambda e: e.activation(scr2[:, :], scr2[:, :], AF.Exp))
            b.step(v, lambda e, hsl=hsl: e.tensor_mul(kendf[:, :], k_bf[:, hsl], scr2[:, :]))
            b.step(v, lambda e, hsl=hsl: e.tensor_mul(scr2[:, :], r_bf[:, hsl], k_bf[:, hsl]))
            b.step(v, lambda e, hc=hc: e.tensor_single_scalar(scr3[:, :], scr2[:, :], u_s[:, hc:hc + 1], ALU.mult))
            for cc in range(NCHK):
                b.step(t, lambda e, cc=cc: e.matmul(p6[0:64, 0:1], scr3[:, cc * 64:(cc + 1) * 64],
                                                    ones_bf[:, :], start=True, stop=True))
                b.step(v, lambda e, hc=hc, cc=cc: e.tensor_copy(sdiag[:, hc * 8 + cc:hc * 8 + cc + 1], p6[0:64, 0:1]))
            for cc in range(NCHK):
                csl = slice(cc * CK, (cc + 1) * CK)
                vsl = v_tm[:, cc * D + hc * 256: cc * D + (hc + 1) * 256]
                b.step(t, lambda e, csl=csl: e.transpose(p6[0:64, 0:128], kendf[:, csl], iden_a[:]))
                b.step(v, lambda e: e.tensor_copy(kend_tm[:, :], p6[0:64, 0:128]))
                b.step(t, lambda e, csl=csl: e.matmul(p0[0:64, 0:64], kmid[:, csl], qmid[:, csl], start=True, stop=True))
                b.step(v, lambda e: e.select(at_bf[:, :], maskLT[:, :], p0[0:64, 0:64], zerosf[0:64, 0:64]))
                b.wait(t)
                t.matmul(p1[0:64, 0:256], at_bf[:, :], vsl, start=True, stop=False)
                last = t.matmul(p1[0:64, 0:256], qst[:, csl], S_bf[:, hc * 256:(hc + 1) * 256],
                                start=False, stop=True)
                b.bump(last)
                b.step(v, lambda e, hc=hc, cc=cc, vsl=vsl: e.scalar_tensor_tensor(
                    o_buf[:, cc * D + hc * 256: cc * D + (hc + 1) * 256],
                    vsl, sdiag[:, hc * 8 + cc:hc * 8 + cc + 1],
                    p1[0:64, 0:256], ALU.mult, ALU.add))
                b.step(t, lambda e, vsl=vsl: e.matmul(p2[:, 0:256], kend_tm[:, :], vsl, start=True, stop=True))
                b.step(v, lambda e, cc=cc, hc=hc: e.scalar_tensor_tensor(
                    S_s[:, hc * 256:(hc + 1) * 256], S_s[:, hc * 256:(hc + 1) * 256],
                    dend8[:, hc * 8 + cc:hc * 8 + cc + 1], p2[:, 0:256], ALU.mult, ALU.add))
                b.step(v, lambda e, hc=hc: e.tensor_copy(S_bf[:, hc * 256:(hc + 1) * 256],
                                                         S_s[:, hc * 256:(hc + 1) * 256]))

        for cc in range(NCHK):
            osl = o_buf[:, cc * D:(cc + 1) * D]
            b.step(v, lambda e, osl=osl: e.tensor_reduce(
                gnm[:, 0:HD].rearrange("p (h o) -> p h o", o=1),
                osl.rearrange("p (h v) -> p h v", h=HD), mybir.AxisListType.X, ALU.add))
            b.step(s, lambda e, osl=osl: e.activation(st_[0:64, 0:D], osl, AF.Square))
            b.step(v, lambda e: e.tensor_reduce(
                gnm[:, HD:2 * HD].rearrange("p (h o) -> p h o", o=1),
                st_[0:64, 0:D].rearrange("p (h v) -> p h v", h=HD), mybir.AxisListType.X, ALU.add))
            b.step(v, lambda e: e.tensor_single_scalar(gnm[:, 0:HD], gnm[:, 0:HD], 1.0 / 256, ALU.mult))
            b.step(v, lambda e: e.tensor_single_scalar(gnm[:, HD:2 * HD], gnm[:, HD:2 * HD], 1.0 / 256, ALU.mult))
            b.step(s, lambda e: e.activation(st_[0:64, 0:HD], gnm[:, 0:HD], AF.Square))
            b.step(v, lambda e: e.tensor_sub(gnm[:, HD:2 * HD], gnm[:, HD:2 * HD], st_[0:64, 0:HD]))
            b.step(v, lambda e: e.tensor_single_scalar(gnm[:, HD:2 * HD], gnm[:, HD:2 * HD], 1e-5, ALU.add))
            b.step(v, lambda e: e.reciprocal(gnm[:, HD:2 * HD], gnm[:, HD:2 * HD]))
            b.step(s, lambda e: e.activation(gnm[:, HD:2 * HD], gnm[:, HD:2 * HD], AF.Sqrt))
            for hc in range(HD):
                oslh = o_buf[:, cc * D + hc * 256: cc * D + (hc + 1) * 256]
                b.step(v, lambda e, oslh=oslh, hc=hc: e.tensor_single_scalar(
                    st_[0:64, 0:256], oslh, gnm[:, hc:hc + 1], ALU.subtract))
                b.step(v, lambda e, hc=hc: e.tensor_single_scalar(
                    st_[0:64, 0:256], st_[0:64, 0:256], gnm[:, HD + hc:HD + hc + 1], ALU.mult))
                b.step(v, lambda e, hc=hc: e.tensor_mul(st_[0:64, 0:256], st_[0:64, 0:256], gnw_b[0:64, hc * 256:(hc + 1) * 256]))
                b.step(v, lambda e, hc=hc: e.tensor_add(st_[0:64, 0:256], st_[0:64, 0:256], gnb_b[0:64, hc * 256:(hc + 1) * 256]))
                b.step(s, lambda e, cc=cc, hc=hc: e.activation(
                    st_[0:64, 256:512], g_tm[:, cc * D + hc * 256: cc * D + (hc + 1) * 256], AF.Silu))
                b.step(v, lambda e, oslh=oslh: e.tensor_mul(oslh, st_[0:64, 0:256], st_[0:64, 256:512]))
            for vc in range(NCH):
                b.step(t, lambda e, cc=cc, vc=vc: e.transpose(
                    pbf[:, 0:64], o_buf[:, cc * D + vc * 128: cc * D + (vc + 1) * 128], iden_b[:]))
                b.step(v, lambda e, cc=cc, vc=vc: e.tensor_copy(
                    xm[:, vc * TT + cc * 64: vc * TT + (cc + 1) * 64], pbf[:, 0:64]))
        for jc in range(NCH):
            b.wait(g)
            b.bump(g.dma_start(out=wos[:].rearrange("p (c j) -> p c j", c=NCH),
                               in_=wot.rearrange("(c p) j -> p c j", p=128)[:, :, jc * 128:(jc + 1) * 128]), 16)
            b.wait(t)
            for dc in range(NCH):
                last = t.matmul(p4[:, :], wos[:, dc * 128: (dc + 1) * 128],
                                xm[:, dc * TT:(dc + 1) * TT], start=(dc == 0), stop=(dc == NCH - 1))
            b.bump(last)
            b.step(v, lambda e, jc=jc: e.tensor_add(
                st_[:, jc * TT:(jc + 1) * TT], xv[:, jc, 1:TT + 1], p4[:, :]))
        b.step(v, lambda e: e.tensor_copy(sbf[:], st_[:]))
        b.step(s, lambda e: e.activation(sq[:], st_[:], AF.Square))
        b.wait(t)
        for c in range(NCH):
            last = t.matmul(p5[0:1, :], ones_bf[:, :], sbf[:, c * TT:(c + 1) * TT],
                            start=(c == 0), stop=(c == NCH - 1))
        b.bump(last)
        b.wait(t)
        for c in range(NCH):
            last = t.matmul(p6[0:1, :], ones_bf[:, :], sq[:, c * TT:(c + 1) * TT],
                            start=(c == 0), stop=(c == NCH - 1))
        b.bump(last)
        b.step(v, lambda e: e.tensor_single_scalar(row_mu, p5[0:1, :], 1.0 / D, ALU.mult))
        b.step(s, lambda e: e.activation(rowa, row_mu, AF.Square))
        b.step(v, lambda e: e.scalar_tensor_tensor(rowa, p6[0:1, :], 1.0 / D, rowa, ALU.mult, ALU.subtract))
        b.step(v, lambda e: e.tensor_single_scalar(rowa, rowa, 1e-5, ALU.add))
        b.step(v, lambda e: e.reciprocal(rowb, rowa))
        b.step(s, lambda e: e.activation(row_rs, rowb, AF.Sqrt))
        b.step(t, lambda e: e.matmul(p0[:, :], ones1f[:, :], row_mu, start=True, stop=True))
        b.step(t, lambda e: e.matmul(p1[:, :], ones1f[:, :], row_rs, start=True, stop=True))
        for c in range(NCH):
            b.step(v, lambda e, c=c: e.tensor_sub(st_[:, c * TT:(c + 1) * TT], st_[:, c * TT:(c + 1) * TT], p0[:, :]))
            b.step(v, lambda e, c=c: e.tensor_mul(st_[:, c * TT:(c + 1) * TT], st_[:, c * TT:(c + 1) * TT], p1[:, :]))
        b.wait(sy)
        b.bump(sy.dma_start(out=xo.rearrange("(c p) t -> p c t", p=128)[:, :, t0:t0 + TT], in_=st_[:]), 16)
        if dbg and tile == 0:
            for nm, tile_ap in [("dlt", dlt[:]), ("xm", xm[:]), ("rbf", r_bf[:]), ("kbf", k_bf[:]),
                                ("wp", wp[:]), ("cumw", cumw[:]), ("vtm", v_tm[:]), ("gtm", g_tm[:]),
                                ("obuf", o_buf[:]), ("Ss", S_s[:]), ("sdiag", sdiag[:]),
                                ("qmid", qmid[:]), ("kmid", kmid[:]), ("qst", qst[:]), ("kendf", kendf[:]),
                                ("bnd", bnd[:]), ("dend8", dend8[:])]:
                dt_ = tile_ap.dtype
                od = nc.dram_tensor("dbg_" + nm, list(tile_ap.shape), dt_, kind="ExternalOutput").ap()
                b.wait(sy)
                b.bump(sy.dma_start(out=od[:, :], in_=tile_ap), 16)
            break

    b.wait(sy)
    if own:
        stack.__exit__(None, None, None)
    return b


def l1attn_inputs(inp, core):
    wx2 = inp["L1_attn_W_x2"].reshape(D, 5, 32)
    wx2r = np.ascontiguousarray(wx2.transpose(2, 1, 0).reshape(32, 5 * D))
    wx2r = np.ascontiguousarray(np.transpose(wx2, (2, 1, 0))).reshape(32, 5 * D)
    xb = np.zeros((128, 40), np.float32)
    for n in range(5):
        xb[:, n * 8:(n + 1) * 8] = inp["L1_attn_x_bias"][n].reshape(8, 128).T
    gnwb = np.concatenate([inp["L1_attn_gn_weight"], inp["L1_attn_gn_bias"]])[None, :]
    return {
        "l1a_wx1t": np.ascontiguousarray(inp["L1_attn_W_x1"].T),
        "l1a_wx2r": wx2r.astype(np.float32),
        "l1a_xbias": xb,
        "l1a_wrt": np.ascontiguousarray(inp["L1_attn_W_r"].T),
        "l1a_wkt": np.ascontiguousarray(inp["L1_attn_W_k"].T),
        "l1a_wvt": np.ascontiguousarray(inp["L1_attn_W_v"].T),
        "l1a_wgt": np.ascontiguousarray(inp["L1_attn_W_g"].T),
        "l1a_awt": np.ascontiguousarray(inp["L1_attn_A_w"].T),
        "l1a_bwt": np.ascontiguousarray(inp["L1_attn_B_w"].T),
        "l1a_bwb": np.ascontiguousarray(inp["L1_attn_b_w"].reshape(4, 128).T),
        "l1a_bonus": np.ascontiguousarray(inp["L1_attn_bonus"].T),
        "l1a_gnwb": np.ascontiguousarray(gnwb),
        "l1a_wot": np.ascontiguousarray(inp["L1_attn_W_o"].T),
        "l1a_mux": np.ascontiguousarray(inp["L1_attn_mu_x"].reshape(8, 128).T),
    }


CAP = 2926
SLOT_TILES = 6
ESLOTS = SLOT_TILES * 512


def _ln_rows_emit(b, t, v, s, ones_bf, ones1f, psum_s, psum_q, pb1, pb2, sbf, sq, row_mu, row_rs, rowa, rowb):
    b.wait(t)
    for c in range(NCH):
        last = t.matmul(psum_s, ones_bf[:, :], sbf[:, c * TT:(c + 1) * TT],
                        start=(c == 0), stop=(c == NCH - 1))
    b.bump(last)
    b.wait(t)
    for c in range(NCH):
        last = t.matmul(psum_q, ones_bf[:, :], sq[:, c * TT:(c + 1) * TT],
                        start=(c == 0), stop=(c == NCH - 1))
    b.bump(last)
    b.step(v, lambda e: e.tensor_single_scalar(row_mu, psum_s, 1.0 / D, ALU.mult))
    b.step(s, lambda e: e.activation(rowa, row_mu, AF.Square))
    b.step(v, lambda e: e.scalar_tensor_tensor(rowa, psum_q, 1.0 / D, rowa, ALU.mult, ALU.subtract))
    b.step(v, lambda e: e.tensor_single_scalar(rowa, rowa, 1e-5, ALU.add))
    b.step(v, lambda e: e.reciprocal(rowb, rowa))
    b.step(s, lambda e: e.activation(row_rs, rowb, AF.Sqrt))
    b.step(t, lambda e: e.matmul(pb1, ones1f[:, :], row_mu, start=True, stop=True))
    b.step(t, lambda e: e.matmul(pb2, ones1f[:, :], row_rs, start=True, stop=True))


def build_l0route(nc, io_in="x1_fm", stack=None, b=None, dram=None):
    own = stack is None
    if own:
        from contextlib import ExitStack
        stack = ExitStack()
        stack.__enter__()
    if dram is None:
        dram = {}

    def din(name, shape, dtype=F32):
        if name not in dram:
            dram[name] = nc.dram_tensor(name, shape, dtype, kind="ExternalInput").ap()
        return dram[name]

    x = din(io_in, [D, TOK])
    lng = din("l0f_lng", [128, 16])
    wr = din("l0f_wr", [D, 7])
    hout = nc.dram_tensor("h_fm", [D, TOK], BF16, kind="ExternalOutput").ap()
    lout = nc.dram_tensor("logits_tm", [TOK, 7], F32, kind="ExternalOutput").ap()

    if b is None:
        b = B(nc, stack)
    sy, g, v, s, t = nc.sync, nc.gpsimd, nc.vector, nc.scalar, nc.tensor

    xt = b.sbuf("r_xt", [128, NCH * TT], F32)
    hf = b.sbuf("r_hf", [128, NCH * TT], F32)
    hb = b.sbuf("r_hb", [128, NCH * TT], BF16)
    sq = b.sbuf("r_sq", [128, NCH * TT], BF16)
    sbf = b.sbuf("r_sbf", [128, NCH * TT], BF16)
    lng_s = b.sbuf("r_lng", [128, 16], F32)
    wr_s = b.sbuf("r_wrs", [128, NCH * 7], F32)
    lrow = b.sbuf("r_lrow", [128, 7], F32)
    ones_bf = b.sbuf("r_onesbf", [128, 1], BF16)
    ones1f = b.sbuf("r_ones1f", [1, 128], F32)
    rowscr = b.sbuf("r_rowscr", [1, 4 * TT], F32)
    rowa, rowb = rowscr[0:1, 0:TT], rowscr[0:1, TT:2 * TT]
    row_mu, row_rs = rowscr[0:1, 2 * TT:3 * TT], rowscr[0:1, 3 * TT:4 * TT]

    p0 = b.psum("r_p0", [128, TT])
    p1 = b.psum("r_p1", [128, TT])
    p2 = b.psum("r_p2", [128, TT])
    p3 = b.psum("r_p3", [128, TT])

    b.step(v, lambda e: e.memset(ones_bf[:], 1.0))
    b.step(v, lambda e: e.memset(ones1f[:], 1.0))
    b.wait(sy); b.bump(sy.dma_start(out=lng_s[:], in_=lng[:, :]), 16)
    b.wait(sy); b.bump(sy.dma_start(out=wr_s[:].rearrange("p (c j) -> p c j", c=NCH),
                                    in_=wr.rearrange("(c p) j -> p c j", p=128)), 16)

    for tile in range(NTILES):
        t0 = tile * TT
        b.wait(sy)
        b.bump(sy.dma_start(out=xt[:].rearrange("p (c t) -> p c t", c=NCH),
                            in_=x.rearrange("(c p) t -> p c t", p=128)[:, :, t0:t0 + TT]), 16)
        b.step(v, lambda e: e.tensor_copy(sbf[:], xt[:]))
        b.step(s, lambda e: e.activation(sq[:], xt[:], AF.Square))
        _ln_rows_emit(b, t, v, s, ones_bf, ones1f, p0[0:1, :], p1[0:1, :], p2[:, :], p3[:, :],
                      sbf, sq, row_mu, row_rs, rowa, rowb)
        for c in range(NCH):
            b.step(v, lambda e, c=c: e.tensor_sub(hf[:, c * TT:(c + 1) * TT], xt[:, c * TT:(c + 1) * TT], p2[:, :]))
            b.step(v, lambda e, c=c: e.tensor_mul(hf[:, c * TT:(c + 1) * TT], hf[:, c * TT:(c + 1) * TT], p3[:, :]))
            b.step(v, lambda e, c=c: e.tensor_scalar(hf[:, c * TT:(c + 1) * TT], hf[:, c * TT:(c + 1) * TT],
                                                     lng_s[:, c:c + 1], lng_s[:, 8 + c:8 + c + 1], ALU.mult, ALU.add))
        b.step(v, lambda e: e.tensor_copy(hb[:], hf[:]))
        b.wait(sy)
        b.bump(sy.dma_start(out=hout.rearrange("(c p) t -> p c t", p=128)[:, :, t0:t0 + TT], in_=hb[:]), 16)
        for sub in range(4):
            so = sub * 128
            b.wait(t)
            for dc in range(NCH):
                last = t.matmul(p0[:, 0:7], hf[:, dc * TT + so: dc * TT + so + 128],
                                wr_s[:, dc * 7:(dc + 1) * 7], start=(dc == 0), stop=(dc == NCH - 1))
            b.bump(last)
            b.step(v, lambda e: e.tensor_copy(lrow[:], p0[:, 0:7]))
            b.wait(sy)
            b.bump(sy.dma_start(out=lout[t0 + so:t0 + so + 128, :], in_=lrow[:]), 16)

    b.wait(sy)
    if own:
        stack.__exit__(None, None, None)
    return b


def build_l0rank(nc, stack=None, b=None, dram=None):
    own = stack is None
    if own:
        from contextlib import ExitStack
        stack = ExitStack()
        stack.__enter__()
    if dram is None:
        dram = {}
    NT = 256
    NE = 7
    W = NT * NE
    logits = nc.dram_tensor("logits_all", [32768, 7], F32, kind="ExternalInput").ap()
    info = nc.dram_tensor("route_info", [128, 4 * NT], F32, kind="ExternalOutput").ap()
    if b is None:
        b = B(nc, stack)
    sy, g, v, s, t = nc.sync, nc.gpsimd, nc.vector, nc.scalar, nc.tensor

    lg = b.sbuf("k_lg", [128, W], F32)
    v1 = b.sbuf("k_v1", [128, NT], F32)
    v2 = b.sbuf("k_v2", [128, NT], F32)
    oh1 = b.sbuf("k_oh1", [128, W], BF16)
    oh2 = b.sbuf("k_oh2", [128, W], BF16)
    msk = b.sbuf("k_msk", [128, W], F32)
    wt1 = b.sbuf("k_wt1", [128, W], F32)
    wt2 = b.sbuf("k_wt2", [128, W], F32)
    cnt = b.sbuf("k_cnt", [1, 2 * W], F32)
    base = b.sbuf("k_base", [1, 2 * W], F32)
    eoff = b.sbuf("k_eoff", [128, NE], F32)
    ls = b.sbuf("k_ls", [128, 128], BF16)
    ones_bf = b.sbuf("k_onesbf", [128, 1], BF16)
    ones1f = b.sbuf("k_ones1f", [1, 128], F32)
    g1 = b.sbuf("k_g1", [128, NT], F32)
    g2 = b.sbuf("k_g2", [128, NT], F32)
    scr = b.sbuf("k_scr", [128, W], F32)
    out_s = b.sbuf("k_out", [128, 4 * NT], F32)

    p0 = b.psum("k_p0", [128, TT])
    p1 = b.psum("k_p1", [128, TT])
    p2 = b.psum("k_p2", [128, TT])
    p3 = b.psum("k_p3", [128, TT])
    p4 = b.psum("k_p4", [128, TT])
    p5 = b.psum("k_p5", [128, TT])
    p6 = b.psum("k_p6", [128, TT])
    p7 = b.psum("k_p7", [128, TT])

    b.step(v, lambda e: e.memset(ones_bf[:], 1.0))
    b.step(v, lambda e: e.memset(ones1f[:], 1.0))
    b.step(g, lambda e: e.iota(scr[:, 0:128], pattern=[[1, 128]], base=0, channel_multiplier=-1,
                               allow_small_or_imprecise_dtypes=True))
    b.step(v, lambda e: e.tensor_single_scalar(ls[:], scr[:, 0:128], 0.0, ALU.is_gt))
    b.step(g, lambda e: e.iota(eoff[:], pattern=[[CAP, NE]], base=0, channel_multiplier=0,
                               allow_small_or_imprecise_dtypes=True))
    b.wait(sy)
    b.bump(sy.dma_start(out=lg[:].rearrange("p (T e) -> p T e", T=NT),
                        in_=logits.rearrange("(T p) e -> p T e", p=128)), 16)
    lgv = lg[:].rearrange("p (T e) -> p T e", T=NT)
    b.step(v, lambda e: e.tensor_reduce(v1[:].rearrange("p (T o) -> p T o", o=1), lgv, mybir.AxisListType.X, ALU.max))
    b.step(v, lambda e: e.tensor_tensor(oh1[:].rearrange("p (T e) -> p T e", T=NT), lgv,
                                        v1[:].rearrange("p (T o) -> p T o", o=1).to_broadcast([128, NT, NE]), ALU.is_equal))
    b.step(v, lambda e: e.scalar_tensor_tensor(msk[:], oh1[:], -1e30, lg[:], ALU.mult, ALU.add))
    b.step(v, lambda e: e.tensor_reduce(v2[:].rearrange("p (T o) -> p T o", o=1),
                                        msk[:].rearrange("p (T e) -> p T e", T=NT), mybir.AxisListType.X, ALU.max))
    b.step(v, lambda e: e.tensor_tensor(oh2[:].rearrange("p (T e) -> p T e", T=NT),
                                        msk[:].rearrange("p (T e) -> p T e", T=NT),
                                        v2[:].rearrange("p (T o) -> p T o", o=1).to_broadcast([128, NT, NE]), ALU.is_equal))
    b.step(v, lambda e: e.tensor_sub(g1[:], v1[:], v2[:]))
    b.step(s, lambda e: e.activation(g1[:], g1[:], AF.Sigmoid))
    b.step(v, lambda e: e.tensor_scalar(g2[:], g1[:], -1.0, 1.0, ALU.mult, ALU.add))
    for k_, oh in ((0, oh1), (1, oh2)):
        for q in range(4):
            b.step(t, lambda e, k_=k_, oh=oh, q=q: e.matmul(
                [p0, p1, p2, p3][q][0:1, 0:448], ones_bf[:, :], oh[:, q * 448:(q + 1) * 448], start=True, stop=True))
        for q in range(4):
            b.step(v, lambda e, k_=k_, q=q: e.tensor_copy(cnt[0:1, k_ * W + q * 448:k_ * W + (q + 1) * 448],
                                                          [p0, p1, p2, p3][q][0:1, 0:448]))
    zr = scr[0:1, 0:NT]
    b.step(v, lambda e: e.memset(zr, 0.0))
    for e_i in range(NE):
        b.step(v, lambda e, e_i=e_i: e.tensor_tensor_scan(
            base[0:1, e_i:W:NE], cnt[0:1, e_i:W:NE], zr, 0.0, ALU.add, ALU.add))
        b.step(v, lambda e, e_i=e_i: e.tensor_tensor_scan(
            base[0:1, W + e_i:2 * W:NE], cnt[0:1, W + e_i:2 * W:NE], zr,
            base[0:1, (NT - 1) * NE + e_i:(NT - 1) * NE + e_i + 1], ALU.add, ALU.add))
    b.step(v, lambda e: e.tensor_sub(base[:], base[:], cnt[:]))
    for k_, oh, wt in ((0, oh1, wt1), (1, oh2, wt2)):
        for q in range(4):
            b.step(t, lambda e, oh=oh, q=q: e.matmul(
                [p0, p1, p2, p3][q][:, 0:448], ls[:, :], oh[:, q * 448:(q + 1) * 448], start=True, stop=True))
        for q in range(4):
            b.step(t, lambda e, k_=k_, q=q: e.matmul(
                [p4, p5, p6, p7][q][:, 0:448], ones1f[:, :], base[0:1, k_ * W + q * 448:k_ * W + (q + 1) * 448],
                start=True, stop=True))
        for q in range(4):
            b.step(v, lambda e, wt=wt, q=q: e.tensor_copy(
                wt[:, q * 448:(q + 1) * 448], [p0, p1, p2, p3][q][:, 0:448]))
            b.step(v, lambda e, wt=wt, q=q: e.tensor_add(
                wt[:, q * 448:(q + 1) * 448], wt[:, q * 448:(q + 1) * 448], [p4, p5, p6, p7][q][:, 0:448]))
    for k_, oh, wt, gk, col in ((0, oh1, wt1, g1, 0), (1, oh2, wt2, g2, 1)):
        b.step(v, lambda e, wt=wt: e.tensor_tensor(
            scr[:].rearrange("p (T e) -> p T e", T=NT), wt[:].rearrange("p (T e) -> p T e", T=NT),
            eoff[:].rearrange("p e -> p 1 e" if False else "p (o e) -> p o e", o=1).to_broadcast([128, NT, NE]), ALU.add))
        b.step(v, lambda e, oh=oh: e.tensor_mul(scr[:], scr[:], oh[:]))
        b.step(v, lambda e, col=col: e.tensor_reduce(
            out_s[:, col * NT:(col + 1) * NT].rearrange("p (T o) -> p T o", o=1),
            scr[:].rearrange("p (T e) -> p T e", T=NT), mybir.AxisListType.X, ALU.add))
        b.step(v, lambda e, oh=oh, wt=wt: e.tensor_mul(scr[:], wt[:], oh[:]))
        b.step(v, lambda e: e.tensor_reduce(
            v1[:].rearrange("p (T o) -> p T o", o=1),
            scr[:].rearrange("p (T e) -> p T e", T=NT), mybir.AxisListType.X, ALU.add))
        b.step(v, lambda e: e.tensor_single_scalar(v1[:], v1[:], float(CAP) - 0.5, ALU.is_lt))
        b.step(v, lambda e, gk=gk, col=col: e.tensor_mul(out_s[:, (2 + col) * NT:(3 + col) * NT], gk[:], v1[:]))
        b.step(v, lambda e, col=col: e.tensor_single_scalar(
            out_s[:, col * NT:(col + 1) * NT], out_s[:, col * NT:(col + 1) * NT], -float(NE * CAP), ALU.add))
        b.step(v, lambda e, col=col: e.tensor_mul(
            out_s[:, col * NT:(col + 1) * NT], out_s[:, col * NT:(col + 1) * NT], v1[:]))
        b.step(v, lambda e, col=col: e.tensor_single_scalar(
            out_s[:, col * NT:(col + 1) * NT], out_s[:, col * NT:(col + 1) * NT], float(NE * CAP), ALU.add))
    b.wait(sy)
    b.bump(sy.dma_start(out=info[:, :], in_=out_s[:]), 16)
    b.wait(sy)
    if own:
        stack.__exit__(None, None, None)
    return b


def build_l0slots(nc, stack=None, b=None, dram=None):
    own = stack is None
    if own:
        from contextlib import ExitStack
        stack = ExitStack()
        stack.__enter__()
    if dram is None:
        dram = {}
    hs = nc.dram_tensor("hs_fm", [D, ESLOTS], BF16, kind="ExternalInput").ap()
    gwa = nc.dram_tensor("l0f_gw", [D, 4096], F32, kind="ExternalInput").ap()
    uwa = nc.dram_tensor("l0f_uw", [D, 4096], F32, kind="ExternalInput").ap()
    dwa = nc.dram_tensor("l0f_dw", [4096, D], F32, kind="ExternalInput").ap()
    ye = nc.dram_tensor("ye_fm", [D, ESLOTS], F32, kind="ExternalOutput").ap()
    if b is None:
        b = B(nc, stack)
    sy, g, v, s, t = nc.sync, nc.gpsimd, nc.vector, nc.scalar, nc.tensor

    hb = b.sbuf("d_hb", [128, NCH * TT], BF16)
    wg = b.sbuf("d_wg", [128, NCH * 1024], BF16)
    wu = b.sbuf("d_wu", [128, NCH * 1024], BF16)
    mp = b.sbuf("d_mp", [128, 32 * TT], BF16)
    yt = b.sbuf("d_yt", [128, NCH * TT], F32)
    sil = b.sbuf("d_sil", [128, TT], F32)

    pa = b.psum("d_pa", [128, TT])
    pu = b.psum("d_pu", [128, TT])
    pdn = [b.psum(f"d_pd{i}", [128, TT]) for i in range(6)]

    for tile in range(SLOT_TILES):
        t0 = tile * TT
        b.wait(g)
        b.bump(g.dma_start(out=hb[:].rearrange("p (c t) -> p c t", c=NCH),
                           in_=hs.rearrange("(c p) t -> p c t", p=128)[:, :, t0:t0 + TT]), 16)
        for fg in range(4):
            b.wait(g)
            b.bump(g.dma_start(out=wg[:].rearrange("p (c f) -> p c f", c=NCH),
                               in_=gwa.rearrange("(c p) f -> p c f", p=128)[:, :, fg * 1024:(fg + 1) * 1024]), 16)
            b.wait(g)
            b.bump(g.dma_start(out=wu[:].rearrange("p (c f) -> p c f", c=NCH),
                               in_=uwa.rearrange("(c p) f -> p c f", p=128)[:, :, fg * 1024:(fg + 1) * 1024]), 16)
            for fc in range(NCH):
                b.wait(t)
                for dc in range(NCH):
                    last = t.matmul(pa[:, :], wg[:, dc * 1024 + fc * 128: dc * 1024 + (fc + 1) * 128],
                                    hb[:, dc * TT:(dc + 1) * TT], start=(dc == 0), stop=(dc == NCH - 1))
                b.bump(last)
                b.wait(t)
                for dc in range(NCH):
                    last = t.matmul(pu[:, :], wu[:, dc * 1024 + fc * 128: dc * 1024 + (fc + 1) * 128],
                                    hb[:, dc * TT:(dc + 1) * TT], start=(dc == 0), stop=(dc == NCH - 1))
                b.bump(last)
                b.step(s, lambda e: e.activation(sil[:, :], pa[:, :], AF.Silu))
                b.step(v, lambda e, fg=fg, fc=fc: e.tensor_mul(
                    mp[:, (fg * NCH + fc) * TT: (fg * NCH + fc + 1) * TT], sil[:, :], pu[:, :]))
        for half, dcs in ((0, range(0, 6)), (1, range(6, 8))):
            for fg in range(4):
                b.wait(g)
                b.bump(g.dma_start(out=wg[:].rearrange("p (c f) -> p c f", c=NCH),
                                   in_=dwa.rearrange("(c p) f -> p c f", p=128)[:, fg * NCH:(fg + 1) * NCH, :]), 16)
                b.wait(t)
                for dc in dcs:
                    for fc in range(NCH):
                        last = t.matmul(pdn[dc % 6][:, :], wg[:, fc * 1024 + dc * 128: fc * 1024 + (dc + 1) * 128],
                                        mp[:, (fg * NCH + fc) * TT: (fg * NCH + fc + 1) * TT],
                                        start=(fg == 0 and fc == 0), stop=(fg == 3 and fc == NCH - 1))
                b.bump(last)
            for dc in dcs:
                b.step(v, lambda e, dc=dc: e.tensor_copy(yt[:, dc * TT:(dc + 1) * TT], pdn[dc % 6][:, :]))
        b.wait(sy)
        b.bump(sy.dma_start(out=ye.rearrange("(c p) t -> p c t", p=128)[:, :, t0:t0 + TT], in_=yt[:]), 16)

    b.wait(sy)
    if own:
        stack.__exit__(None, None, None)
    return b


def build_l0combine(nc, io_in="x1_fm", io_out="x2_fm", stack=None, b=None, dram=None):
    own = stack is None
    if own:
        from contextlib import ExitStack
        stack = ExitStack()
        stack.__enter__()
    if dram is None:
        dram = {}

    def din(name, shape, dtype=F32):
        if name not in dram:
            dram[name] = nc.dram_tensor(name, shape, dtype, kind="ExternalInput").ap()
        return dram[name]

    x = din(io_in, [D, TOK])
    ye0 = din("ye0_fm", [D, TOK])
    ye1 = din("ye1_fm", [D, TOK])
    kg = din("kg_rows", [1, 2 * TOK])
    lng = din("l0f_lng", [128, 16])
    w1s_d = din("l0f_w1s", [D, 4096])
    w3s_d = din("l0f_w3s", [D, 4096])
    w2s_d = din("l0f_w2s", [4096, D])
    if io_out not in dram:
        dram[io_out] = nc.dram_tensor(io_out, [D, TOK], F32, kind="ExternalOutput").ap()
    xo = dram[io_out]
    if b is None:
        b = B(nc, stack)
    sy, g, v, s, t = nc.sync, nc.gpsimd, nc.vector, nc.scalar, nc.tensor

    xt = b.sbuf("e_xt", [128, NCH * TT], F32)
    yt0 = b.sbuf("e_yt0", [128, NCH * TT], F32)
    yt1 = b.sbuf("e_yt1", [128, NCH * TT], F32)
    acc = b.sbuf("e_acc", [128, NCH * TT], F32)
    hb = b.sbuf("e_hb", [128, NCH * TT], BF16)
    wg = b.sbuf("e_wg", [128, NCH * 1024], BF16)
    wu = b.sbuf("e_wu", [128, NCH * 1024], BF16)
    mp = b.sbuf("e_mp", [128, 32 * TT], BF16)
    sil = b.sbuf("e_sil", [128, TT], F32)
    kgr = b.sbuf("e_kgr", [1, 2 * TOK], F32)
    lng_s = b.sbuf("e_lng", [128, 16], F32)
    sq = b.sbuf("e_sq", [128, NCH * TT], BF16)
    sbf = b.sbuf("e_sbf", [128, NCH * TT], BF16)
    ones_bf = b.sbuf("e_onesbf", [128, 1], BF16)
    ones1f = b.sbuf("e_ones1f", [1, 128], F32)
    rowscr = b.sbuf("e_rowscr", [1, 4 * TT], F32)
    rowa, rowb = rowscr[0:1, 0:TT], rowscr[0:1, TT:2 * TT]
    row_mu, row_rs = rowscr[0:1, 2 * TT:3 * TT], rowscr[0:1, 3 * TT:4 * TT]

    pa = b.psum("e_pa", [128, TT])
    pu = b.psum("e_pu", [128, TT])
    pb1 = b.psum("e_pb1", [128, TT])
    pb2 = b.psum("e_pb2", [128, TT])
    pdn0 = b.psum("e_pdn0", [128, TT])
    pdn1 = b.psum("e_pdn1", [128, TT])
    pr1 = b.psum("e_pr1", [128, TT])
    pr2 = b.psum("e_pr2", [128, TT])

    b.step(v, lambda e: e.memset(ones_bf[:], 1.0))
    b.step(v, lambda e: e.memset(ones1f[:], 1.0))
    b.wait(sy); b.bump(sy.dma_start(out=kgr[:], in_=kg[:, :]), 16)
    b.wait(sy); b.bump(sy.dma_start(out=lng_s[:], in_=lng[:, :]), 16)

    for tile in range(NTILES):
        t0 = tile * TT
        for dst, srcd in ((xt, x), (yt0, ye0), (yt1, ye1)):
            b.wait(sy)
            b.bump(sy.dma_start(out=dst[:].rearrange("p (c t) -> p c t", c=NCH),
                                in_=srcd.rearrange("(c p) t -> p c t", p=128)[:, :, t0:t0 + TT]), 16)
        b.step(t, lambda e, t0=t0: e.matmul(pr1[:, :], ones1f[:, :], kgr[0:1, t0:t0 + TT], start=True, stop=True))
        b.step(t, lambda e, t0=t0: e.matmul(pr2[:, :], ones1f[:, :], kgr[0:1, TOK + t0:TOK + t0 + TT], start=True, stop=True))
        for c in range(NCH):
            b.step(v, lambda e, c=c: e.tensor_mul(acc[:, c * TT:(c + 1) * TT], yt0[:, c * TT:(c + 1) * TT], pr1[:, :]))
            b.step(v, lambda e, c=c: e.tensor_mul(yt1[:, c * TT:(c + 1) * TT], yt1[:, c * TT:(c + 1) * TT], pr2[:, :]))
            b.step(v, lambda e, c=c: e.tensor_add(acc[:, c * TT:(c + 1) * TT], acc[:, c * TT:(c + 1) * TT],
                                                  yt1[:, c * TT:(c + 1) * TT]))
            b.step(v, lambda e, c=c: e.tensor_add(acc[:, c * TT:(c + 1) * TT], acc[:, c * TT:(c + 1) * TT],
                                                  xt[:, c * TT:(c + 1) * TT]))
        b.step(v, lambda e: e.tensor_copy(sbf[:], xt[:]))
        b.step(s, lambda e: e.activation(sq[:], xt[:], AF.Square))
        _ln_rows_emit(b, t, v, s, ones_bf, ones1f, pa[0:1, :], pu[0:1, :], pb1[:, :], pb2[:, :],
                      sbf, sq, row_mu, row_rs, rowa, rowb)
        for c in range(NCH):
            b.step(v, lambda e, c=c: e.tensor_sub(sil[:, :], xt[:, c * TT:(c + 1) * TT], pb1[:, :]))
            b.step(v, lambda e, c=c: e.tensor_mul(sil[:, :], sil[:, :], pb2[:, :]))
            b.step(v, lambda e, c=c: e.tensor_scalar(hb[:, c * TT:(c + 1) * TT], sil[:, :],
                                                     lng_s[:, c:c + 1], lng_s[:, 8 + c:8 + c + 1], ALU.mult, ALU.add))
        for fg in range(4):
            b.wait(g)
            b.bump(g.dma_start(out=wg[:].rearrange("p (c f) -> p c f", c=NCH),
                               in_=w1s_d.rearrange("(c p) f -> p c f", p=128)[:, :, fg * 1024:(fg + 1) * 1024]), 16)
            b.wait(g)
            b.bump(g.dma_start(out=wu[:].rearrange("p (c f) -> p c f", c=NCH),
                               in_=w3s_d.rearrange("(c p) f -> p c f", p=128)[:, :, fg * 1024:(fg + 1) * 1024]), 16)
            for fc in range(NCH):
                b.wait(t)
                for dc in range(NCH):
                    last = t.matmul(pa[:, :], wg[:, dc * 1024 + fc * 128: dc * 1024 + (fc + 1) * 128],
                                    hb[:, dc * TT:(dc + 1) * TT], start=(dc == 0), stop=(dc == NCH - 1))
                b.bump(last)
                b.wait(t)
                for dc in range(NCH):
                    last = t.matmul(pu[:, :], wu[:, dc * 1024 + fc * 128: dc * 1024 + (fc + 1) * 128],
                                    hb[:, dc * TT:(dc + 1) * TT], start=(dc == 0), stop=(dc == NCH - 1))
                b.bump(last)
                b.step(s, lambda e: e.activation(sil[:, :], pa[:, :], AF.Silu))
                b.step(v, lambda e, fg=fg, fc=fc: e.tensor_mul(
                    mp[:, (fg * NCH + fc) * TT: (fg * NCH + fc + 1) * TT], sil[:, :], pu[:, :]))
        for dc in range(NCH):
            pd_ = pdn0 if dc % 2 == 0 else pdn1
            for fg in range(4):
                b.wait(g)
                if dc == 0 or True:
                    b.bump(g.dma_start(out=wg[:, 0:NCH * 128].rearrange("p (c f) -> p c f", c=NCH),
                                       in_=w2s_d.rearrange("(c p) f -> p c f", p=128)[:, fg * NCH:(fg + 1) * NCH,
                                                                                      dc * 128:(dc + 1) * 128]), 16)
                b.wait(t)
                for fc in range(NCH):
                    last = t.matmul(pd_[:, :], wg[:, fc * 128:(fc + 1) * 128],
                                    mp[:, (fg * NCH + fc) * TT: (fg * NCH + fc + 1) * TT],
                                    start=(fg == 0 and fc == 0), stop=(fg == 3 and fc == NCH - 1))
                b.bump(last)
            b.step(v, lambda e, dc=dc: e.tensor_add(acc[:, dc * TT:(dc + 1) * TT], acc[:, dc * TT:(dc + 1) * TT],
                                                    pd_[:, :]))
        b.step(v, lambda e: e.tensor_copy(sbf[:], acc[:]))
        b.step(s, lambda e: e.activation(sq[:], acc[:], AF.Square))
        _ln_rows_emit(b, t, v, s, pa[0:1, :], pu[0:1, :], pb1[:, :], pb2[:, :], sbf, sq,
                      row_mu, row_rs, rowa, rowb) if False else \
            _ln_rows_emit(b, t, v, s, ones_bf, ones1f, pa[0:1, :], pu[0:1, :], pb1[:, :], pb2[:, :],
                          sbf, sq, row_mu, row_rs, rowa, rowb)
        for c in range(NCH):
            b.step(v, lambda e, c=c: e.tensor_sub(acc[:, c * TT:(c + 1) * TT], acc[:, c * TT:(c + 1) * TT], pb1[:, :]))
            b.step(v, lambda e, c=c: e.tensor_mul(acc[:, c * TT:(c + 1) * TT], acc[:, c * TT:(c + 1) * TT], pb2[:, :]))
        b.wait(sy)
        b.bump(sy.dma_start(out=xo.rearrange("(c p) t -> p c t", p=128)[:, :, t0:t0 + TT], in_=acc[:]), 16)

    b.wait(sy)
    if own:
        stack.__exit__(None, None, None)
    return b


# version 04
# speedup vs baseline: 1.0167x; 1.0167x over previous
import numpy as np
import concourse.bass as bass
import concourse.mybir as mybir
from concourse.bass_utils import run_bass_kernel_spmd
from concourse.bass_core_rust import IndirectOffsetOnAxis

F32 = mybir.dt.float32
BF16 = mybir.dt.bfloat16
U32 = mybir.dt.uint32
AF = mybir.ActivationFunctionType
ALU = mybir.AluOpType

D = 1024
NCH = 8
SEQ = 2048
NSEQ = 2
TOK = SEQ * NSEQ
TT = 512
NTILES = TOK // TT
FEPS = float(np.finfo(np.float32).eps)


class B:

    def __init__(self, nc, stack):
        self.nc = nc
        self.stack = stack
        self.pcsem = stack.enter_context(nc.semaphore("pc"))
        self.pc = 0
        self._zero_cache = {}

    def wait(self, eng):
        if self.pc > 0:
            eng.wait_ge(self.pcsem, self.pc)

    def bump(self, inst, n=1):
        inst.then_inc(self.pcsem, n)
        self.pc += n

    def step(self, eng, fn, inc=1):
        self.wait(eng)
        last = fn(eng)
        self.bump(last, inc)

    def sbuf(self, name, shape, dtype):
        return self.stack.enter_context(self.nc.sbuf_tensor(name, shape, dtype))

    def psum(self, name, shape, dtype=F32):
        return self.stack.enter_context(self.nc.psum_tensor(name, shape, dtype))



def softplus_emit(b, v, s, out_f32, in_ap, bias_pp, tmp1, tmp2):
    b.step(s, lambda e: e.activation(tmp1, in_ap, AF.Identity, bias=bias_pp, scale=1.0))
    b.step(s, lambda e: e.activation(tmp2, tmp1, AF.Abs))
    b.step(s, lambda e: e.activation(tmp2, tmp2, AF.Exp, bias=0.0, scale=-1.0))
    b.step(s, lambda e: e.activation(tmp2, tmp2, AF.Ln, bias=1.0, scale=1.0))
    b.step(v, lambda e: e.tensor_single_scalar(tmp1, tmp1, 0.0, ALU.max))
    b.step(v, lambda e: e.tensor_add(out_f32, tmp1, tmp2))



def build_l0attn(nc, io_in="x_fm", io_out="x1_fm", stack=None, b=None, dram=None):
    own = stack is None
    if own:
        from contextlib import ExitStack
        stack = ExitStack()
        stack.__enter__()
    if dram is None:
        dram = {}

    def din(name, shape, dtype=F32):
        if name not in dram:
            dram[name] = nc.dram_tensor(name, shape, dtype, kind="ExternalInput").ap()
        return dram[name]

    x = din(io_in, [D, TOK])
    w1t = din("l0a_w1t", [D, D])
    w2t = din("l0a_w2t", [D, D])
    dwt_big = din("l0a_dwtBig", [D, D])
    wlt = din("l0a_wlt", [D, D])
    cwt = din("l0a_cwt", [64, D])
    dwsm = din("l0a_dwsm", [D, 64])
    bwsm = din("l0a_bwsm", [D, 64])
    convw = din("l0a_convw", [D, 3])
    vecs8 = din("l0a_vecs8", [128, 8 * 5])
    vecs64 = din("l0a_vecs64", [64, 3])
    if io_out not in dram:
        dram[io_out] = nc.dram_tensor(io_out, [D, TOK], F32, kind="ExternalOutput").ap()
    xo = dram[io_out]

    if b is None:
        b = B(nc, stack)
    sy, g, v, s, t = nc.sync, nc.gpsimd, nc.vector, nc.scalar, nc.tensor

    w1s = b.sbuf("l0_w1s", [128, NCH * D], BF16)
    w2s = b.sbuf("l0_w2s", [128, NCH * D], BF16)
    dws = b.sbuf("l0_dws", [128, NCH * D], BF16)
    wls = b.sbuf("l0_wls", [128, NCH * D], BF16)
    cws = b.sbuf("l0_cws", [64, D], BF16)
    dwsms = b.sbuf("l0_dwsms", [128, NCH * 64], BF16)
    bwsms = b.sbuf("l0_bwsms", [128, NCH * 64], BF16)
    convs = b.sbuf("l0_convs", [128, NCH * 3], F32)
    v8 = b.sbuf("l0_v8", [128, 40], F32)
    v64 = b.sbuf("l0_v64", [64, 3], F32)
    alogneg = b.sbuf("l0_alogneg", [64, 1], F32)
    cbdb = b.sbuf("l0_cbdb", [128, 8], F32)

    ones_bf = b.sbuf("l0_onesbf", [128, 1], BF16)
    ones1f = b.sbuf("l0_ones1f", [1, 128], F32)
    zeros64 = b.sbuf("l0_zeros64", [64, TT], F32)

    xt = b.sbuf("l0_xt", [128, NCH * TT], F32)
    sq = b.sbuf("l0_sq", [128, NCH * TT], BF16)
    xnt = b.sbuf("l0_xnt", [128, NCH * TT], BF16)
    xbt = b.sbuf("l0_xbt", [128, NCH * (TT + 2)], BF16)
    zt = b.sbuf("l0_zt", [128, NCH * TT], BF16)
    xct = b.sbuf("l0_xct", [128, NCH * TT], BF16)
    st_ = b.sbuf("l0_st", [128, NCH * TT], F32)
    sbf = b.sbuf("l0_sbf", [128, NCH * TT], BF16)
    x1t = b.sbuf("l0_x1t", [128, NCH * TT], F32)
    rowa = b.sbuf("l0_rowa", [1, TT], F32)
    rowb = b.sbuf("l0_rowb", [1, TT], F32)
    rowbf = b.sbuf("l0_rowbf", [1, 2 * TT], F32)

    c1 = b.sbuf("l0_c1", [64, TT], F32)
    c2 = b.sbuf("l0_c2", [64, TT], F32)
    delta = b.sbuf("l0_delta", [64, TT], F32)
    bbar = b.sbuf("l0_bbar", [64, TT], F32)
    acum = b.sbuf("l0_acum", [64, TT], F32)
    acumc = b.sbuf("l0_acumc", [64, TT], F32)
    ashift = b.sbuf("l0_ashift", [64, TT], F32)
    csum = b.sbuf("l0_csum", [64, TT], F32)
    hbf = b.sbuf("l0_hbf", [64, TT], BF16)
    sztile = b.sbuf("l0_sz", [128, TT], F32)
    carry = b.sbuf("l0_carry", [64, 4], F32)
    sp1 = b.sbuf("l0_sp1", [64, TT], F32)
    sp2 = b.sbuf("l0_sp2", [64, TT], F32)

    pA = b.psum("l0_pA", [128, TT])
    pB = b.psum("l0_pB", [128, TT])
    pd = b.psum("l0_pd", [64, TT])
    pb2 = b.psum("l0_pb2", [64, TT])
    prow = b.psum("l0_prow", [1, TT])
    prow2 = b.psum("l0_prow2", [1, TT])
    pbc1 = b.psum("l0_pbc1", [128, TT])
    pbc2 = b.psum("l0_pbc2", [128, TT])

    def ldw(dst, src, width):
        b.wait(g)
        b.bump(g.dma_start(out=dst[:].rearrange("p (c j) -> p c j", c=NCH),
                           in_=src.rearrange("(c p) j -> p c j", p=128)), 16)

    ldw(w1s, w1t, D)
    ldw(w2s, w2t, D)
    ldw(dws, dwt_big, D)
    ldw(wls, wlt, D)
    b.wait(g)
    b.bump(g.dma_start(out=cws[:], in_=cwt[:, :]), 16)
    ldw(dwsms, dwsm, 64)
    ldw(bwsms, bwsm, 64)
    b.wait(sy)
    b.bump(sy.dma_start(out=convs[:].rearrange("p (c k) -> p c k", c=NCH),
                        in_=convw.rearrange("(c p) k -> p c k", p=128)), 16)
    b.wait(sy)
    b.bump(sy.dma_start(out=v8[:], in_=vecs8[:, :]), 16)
    b.wait(sy)
    b.bump(sy.dma_start(out=v64[:], in_=vecs64[:, :]), 16)

    b.step(v, lambda e: e.memset(ones_bf[:], 1.0))
    b.step(v, lambda e: e.memset(ones1f[:], 1.0))
    b.step(v, lambda e: e.memset(zeros64[:], 0.0))
    b.step(v, lambda e: e.tensor_add(cbdb[:], v8[:, 16:24], v8[:, 24:32]))
    softplus_emit(b, v, s, alogneg[:, 0:1], v64[:, 0:1], 0.0, c1[:, 0:1], c2[:, 0:1])

    NW, CB = v8[:, 0:8], v8[:, 8:16]
    BB, DBSM = v64[:, 1:2], v64[:, 2:3]

    for tile in range(NTILES):
        t0 = tile * TT
        seq_start = (tile % (NTILES // NSEQ)) == 0
        b.wait(sy)
        b.bump(sy.dma_start(
            out=xt[:], in_=x.rearrange("(c p) t -> p c t", p=128)[:, :, t0:t0 + TT]), 16)

        b.step(s, lambda e: e.activation(sq[:], xt[:], AF.Square))
        b.wait(t)
        for c in range(NCH):
            last = t.matmul(prow[:, :], ones_bf[:, :], sq[:, c * TT:(c + 1) * TT],
                            start=(c == 0), stop=(c == NCH - 1))
        b.bump(last)
        b.step(v, lambda e: e.tensor_scalar(rowa[:], prow[:, :], 1.0 / D, FEPS, ALU.mult, ALU.add))
        b.step(v, lambda e: e.reciprocal(rowb[:], rowa[:]))
        b.step(s, lambda e: e.activation(rowbf[:, TT:2 * TT], rowb[:], AF.Sqrt))
        b.step(t, lambda e: e.matmul(pbc1[:, :], ones1f[:, :], rowbf[:, TT:2 * TT], start=True, stop=True))
        for c in range(NCH):
            b.step(v, lambda e, c=c: e.scalar_tensor_tensor(
                xnt[:, c * TT:(c + 1) * TT], xt[:, c * TT:(c + 1) * TT], NW[:, c:c + 1], pbc1[:, :],
                ALU.mult, ALU.mult))
        for jc in range(NCH):
            b.wait(t)
            for dc in range(NCH):
                last = t.matmul(pA[:, :], w1s[:, dc * D + jc * 128: dc * D + (jc + 1) * 128],
                                xnt[:, dc * TT:(dc + 1) * TT], start=(dc == 0), stop=(dc == NCH - 1))
            b.bump(last)
            b.step(v, lambda e, jc=jc: e.tensor_copy(xbt[:, jc * (TT + 2) + 2: (jc + 1) * (TT + 2)], pA[:, :]))
            b.wait(t)
            for dc in range(NCH):
                last = t.matmul(pB[:, :], w2s[:, dc * D + jc * 128: dc * D + (jc + 1) * 128],
                                xnt[:, dc * TT:(dc + 1) * TT], start=(dc == 0), stop=(dc == NCH - 1))
            b.bump(last)
            b.step(s, lambda e, jc=jc: e.copy(zt[:, jc * TT:(jc + 1) * TT], pB[:, :]))
        for jc in range(NCH):
            base = jc * (TT + 2)
            if seq_start:
                b.step(v, lambda e, base=base: e.memset(xbt[:, base:base + 2], 0.0))
            b.step(v, lambda e, jc=jc, base=base: e.tensor_single_scalar(
                sztile[:, :], xbt[:, base:base + TT], convs[:, jc * 3:jc * 3 + 1], ALU.mult))
            b.step(v, lambda e, jc=jc, base=base: e.scalar_tensor_tensor(
                sztile[:, :], xbt[:, base + 1:base + 1 + TT], convs[:, jc * 3 + 1:jc * 3 + 2],
                sztile[:, :], ALU.mult, ALU.add))
            b.step(v, lambda e, jc=jc, base=base: e.scalar_tensor_tensor(
                sztile[:, :], xbt[:, base + 2:base + 2 + TT], convs[:, jc * 3 + 2:jc * 3 + 3],
                sztile[:, :], ALU.mult, ALU.add))
            b.step(s, lambda e, jc=jc: e.activation(
                xct[:, jc * TT:(jc + 1) * TT], sztile[:, :], AF.Silu, bias=CB[:, jc:jc + 1], scale=1.0))
            b.step(v, lambda e, base=base: e.tensor_copy(xbt[:, base:base + 2], xbt[:, base + TT:base + TT + 2]))
        b.wait(t)
        for dc in range(NCH):
            last = t.matmul(pd[:, :], dwsms[:, dc * 64:(dc + 1) * 64], xct[:, dc * TT:(dc + 1) * TT],
                            start=(dc == 0), stop=(dc == NCH - 1))
        b.bump(last)
        softplus_emit(b, v, s, delta[:, :], pd[:, :], DBSM, sp1[:, :], sp2[:, :])
        b.wait(t)
        for dc in range(NCH):
            last = t.matmul(pb2[:, :], bwsms[:, dc * 64:(dc + 1) * 64], xct[:, dc * TT:(dc + 1) * TT],
                            start=(dc == 0), stop=(dc == NCH - 1))
        b.bump(last)
        b.step(s, lambda e: e.activation(c1[:, :], pb2[:, :], AF.Identity, bias=BB, scale=1.0))
        b.step(v, lambda e: e.tensor_mul(bbar[:, :], delta[:, :], c1[:, :]))
        b.step(v, lambda e: e.tensor_single_scalar(c2[:, :], delta[:, :], alogneg[:, 0:1], ALU.mult))
        if seq_start:
            b.step(v, lambda e: e.memset(carry[:, :], 0.0))
        b.step(v, lambda e: e.tensor_single_scalar(c2[:, :], c2[:, :], -1.0, ALU.mult))
        b.step(v, lambda e: e.tensor_tensor_scan(acum[:, :], c2[:, :], zeros64[:, :], carry[:, 0:1], ALU.add, ALU.add))
        b.step(v, lambda e: e.tensor_copy(carry[:, 0:1], acum[:, TT - 1:TT]))
        b.step(v, lambda e: e.tensor_scalar(acumc[:, :], acum[:, :], 20.0, -20.0, ALU.min, ALU.max))
        b.step(v, lambda e: e.tensor_copy(ashift[:, 1:TT], acumc[:, 0:TT - 1]))
        b.step(v, lambda e: e.tensor_copy(ashift[:, 0:1], carry[:, 1:2]))
        b.step(v, lambda e: e.tensor_copy(carry[:, 1:2], acumc[:, TT - 1:TT]))
        b.step(s, lambda e: e.activation(ashift[:, :], ashift[:, :], AF.Exp, bias=0.0, scale=-1.0))
        b.step(v, lambda e: e.tensor_mul(c1[:, :], bbar[:, :], ashift[:, :]))
        b.step(v, lambda e: e.tensor_tensor_scan(csum[:, :], c1[:, :], zeros64[:, :], carry[:, 2:3], ALU.add, ALU.add))
        b.step(v, lambda e: e.tensor_copy(carry[:, 2:3], csum[:, TT - 1:TT]))
        b.step(s, lambda e: e.activation(acumc[:, :], acumc[:, :], AF.Exp))
        b.step(v, lambda e: e.tensor_mul(hbf[:, :], csum[:, :], acumc[:, :]))
        for jc in range(NCH):
            b.wait(t)
            last = t.matmul(pA[:, :], cws[:, jc * 128:(jc + 1) * 128], hbf[:, :], start=True, stop=False)
            for dc in range(NCH):
                last = t.matmul(pA[:, :], dws[:, dc * D + jc * 128: dc * D + (jc + 1) * 128],
                                xct[:, dc * TT:(dc + 1) * TT], start=False, stop=(dc == NCH - 1))
            b.bump(last)
            b.step(s, lambda e, jc=jc: e.activation(sztile[:, :], zt[:, jc * TT:(jc + 1) * TT], AF.Silu))
            b.step(v, lambda e, jc=jc: e.scalar_tensor_tensor(
                xnt[:, jc * TT:(jc + 1) * TT], pA[:, :], cbdb[:, jc:jc + 1], sztile[:, :],
                ALU.add, ALU.mult))
        for jc in range(NCH):
            b.wait(t)
            for dc in range(NCH):
                last = t.matmul(pA[:, :], wls[:, dc * D + jc * 128: dc * D + (jc + 1) * 128],
                                xnt[:, dc * TT:(dc + 1) * TT], start=(dc == 0), stop=(dc == NCH - 1))
            b.bump(last)
            b.step(v, lambda e, jc=jc: e.scalar_tensor_tensor(
                st_[:, jc * TT:(jc + 1) * TT], xt[:, jc * TT:(jc + 1) * TT], 2.0, pA[:, :],
                ALU.mult, ALU.add))
        b.step(v, lambda e: e.tensor_copy(sbf[:], st_[:]))
        b.step(s, lambda e: e.activation(sq[:], st_[:], AF.Square))
        b.wait(t)
        for c in range(NCH):
            last = t.matmul(prow[:, :], ones_bf[:, :], sbf[:, c * TT:(c + 1) * TT],
                            start=(c == 0), stop=(c == NCH - 1))
        b.bump(last)
        b.wait(t)
        for c in range(NCH):
            last = t.matmul(prow2[:, :], ones_bf[:, :], sq[:, c * TT:(c + 1) * TT],
                            start=(c == 0), stop=(c == NCH - 1))
        b.bump(last)
        b.step(v, lambda e: e.tensor_single_scalar(rowbf[:, 0:TT], prow[:, :], 1.0 / D, ALU.mult))
        b.step(s, lambda e: e.activation(rowa[:], rowbf[:, 0:TT], AF.Square))
        b.step(v, lambda e: e.scalar_tensor_tensor(rowa[:], prow2[:, :], 1.0 / D, rowa[:], ALU.mult, ALU.subtract))
        b.step(v, lambda e: e.tensor_single_scalar(rowa[:], rowa[:], 1e-5, ALU.add))
        b.step(v, lambda e: e.reciprocal(rowb[:], rowa[:]))
        b.step(s, lambda e: e.activation(rowbf[:, TT:2 * TT], rowb[:], AF.Sqrt))
        b.step(t, lambda e: e.matmul(pbc1[:, :], ones1f[:, :], rowbf[:, 0:TT], start=True, stop=True))
        b.step(t, lambda e: e.matmul(pbc2[:, :], ones1f[:, :], rowbf[:, TT:2 * TT], start=True, stop=True))
        for c in range(NCH):
            b.step(v, lambda e, c=c: e.tensor_sub(x1t[:, c * TT:(c + 1) * TT], st_[:, c * TT:(c + 1) * TT], pbc1[:, :]))
            b.step(v, lambda e, c=c: e.tensor_mul(x1t[:, c * TT:(c + 1) * TT], x1t[:, c * TT:(c + 1) * TT], pbc2[:, :]))
        b.wait(sy)
        b.bump(sy.dma_start(out=xo.rearrange("(c p) t -> p c t", p=128)[:, :, t0:t0 + TT], in_=x1t[:]), 16)

    b.wait(sy)
    if own:
        stack.__exit__(None, None, None)
    return b



def fm(a):
    a2 = np.asarray(a).reshape(-1, a.shape[-1])
    return np.ascontiguousarray(a2.T)


def l0attn_inputs(inp, core):
    x = inp["L0_attn_x"][2 * core:2 * core + 2]
    v8 = np.zeros((128, 40), np.float32)
    v8[:, 0:8] = inp["L0_attn_norm_w"].reshape(8, 128).T
    v8[:, 8:16] = inp["L0_attn_conv_b"].reshape(8, 128).T
    v8[:, 16:24] = inp["L0_attn_Cb"].reshape(8, 128).T
    v8[:, 24:32] = inp["L0_attn_Db"].reshape(8, 128).T
    v64 = np.zeros((64, 3), np.float32)
    v64[:, 0] = inp["L0_attn_A"]
    v64[:, 1] = inp["L0_attn_Bb"]
    v64[:, 2] = inp["L0_attn_db"]
    return {
        "x_fm": fm(x),
        "l0a_w1t": np.ascontiguousarray(inp["L0_attn_W1"].T),
        "l0a_w2t": np.ascontiguousarray(inp["L0_attn_W2"].T),
        "l0a_dwtBig": np.ascontiguousarray(inp["L0_attn_Dw"].T),
        "l0a_wlt": np.ascontiguousarray(inp["L0_attn_Wlast"].T),
        "l0a_cwt": np.ascontiguousarray(inp["L0_attn_Cw"].T),
        "l0a_dwsm": np.ascontiguousarray(inp["L0_attn_dw"].T),
        "l0a_bwsm": np.ascontiguousarray(inp["L0_attn_Bw"].T),
        "l0a_convw": np.ascontiguousarray(inp["L0_attn_conv_w"]),
        "l0a_vecs8": v8,
        "l0a_vecs64": v64,
    }


def _lng_layout(inp):
    lng = np.zeros((128, 16), np.float32)
    lng[:, 0:8] = inp["L0_ffn_ln_g"].reshape(8, 128).T
    lng[:, 8:16] = inp["L0_ffn_ln_b"].reshape(8, 128).T
    return lng


def _run(build_fn, in_maps, **bkw):
    nc = bass.Bass("TRN2", target_bir_lowering=False, num_devices=8)
    build_fn(nc, **bkw)
    return run_bass_kernel_spmd(nc, in_maps, core_ids=list(range(8))).results


def kernel(**inputs):
    import ml_dtypes
    inp = inputs
    NCORE = 8
    maps = [l0attn_inputs(inp, c) for c in range(NCORE)]
    resA = _run(build_l0attn, maps)
    x1 = [resA[c]["x1_fm"] for c in range(NCORE)]
    lng = _lng_layout(inp)
    wr = np.ascontiguousarray(inp["L0_ffn_Wr"])
    maps = [{"x1_fm": x1[c], "l0f_lng": lng, "l0f_wr": wr} for c in range(NCORE)]
    resB = _run(build_l0route, maps)
    h_all = np.concatenate([np.asarray(resB[c]["h_fm"]).T for c in range(NCORE)], axis=0)
    logits_all = np.ascontiguousarray(np.concatenate([resB[c]["logits_tm"] for c in range(NCORE)], axis=0))
    maps = [{"logits_all": logits_all} for _ in range(NCORE)]
    resC = _run(build_l0rank, maps)
    info = resC[0]["route_info"]
    slot0 = info[:, 0:256].T.ravel().astype(np.int64)
    slot1 = info[:, 256:512].T.ravel().astype(np.int64)
    kg0 = info[:, 512:768].T.ravel().astype(np.float32)
    kg1 = info[:, 768:1024].T.ravel().astype(np.float32)
    NE7 = 7
    table = np.zeros(TRASH + 1, np.int64)
    for sl in (slot0, slot1):
        valid = sl < TRASH
        table[sl[valid]] = np.nonzero(valid)[0] % 32768
    maps = []
    for c in range(NCORE):
        e = min(c, NE7 - 1)
        toks = table[e * ESLOTS:e * ESLOTS + CAP]
        hs = np.zeros((ESLOTS, D), ml_dtypes.bfloat16)
        hs[0:CAP] = h_all[toks]
        maps.append({
            "hs_fm": np.ascontiguousarray(hs.T),
            "l0f_gw": np.ascontiguousarray(inp["L0_ffn_W1"][e]),
            "l0f_uw": np.ascontiguousarray(inp["L0_ffn_W3"][e]),
            "l0f_dw": np.ascontiguousarray(inp["L0_ffn_W2"][e]),
        })
    resD = _run(build_l0slots, maps)
    ye_all = np.zeros((TRASH + 1, D), np.float32)
    for e in range(NE7):
        ye_all[e * ESLOTS:e * ESLOTS + CAP] = np.asarray(resD[e]["ye_fm"]).T[0:CAP]
    ye0 = ye_all[slot0]
    ye1 = ye_all[slot1]
    maps = []
    for c in range(NCORE):
        sl = slice(c * TOK, (c + 1) * TOK)
        maps.append({
            "x1_fm": x1[c],
            "ye0_fm": np.ascontiguousarray(ye0[sl].T),
            "ye1_fm": np.ascontiguousarray(ye1[sl].T),
            "kg_rows": np.ascontiguousarray(np.concatenate([kg0[sl], kg1[sl]])[None, :]),
            "l0f_lng": lng,
            "l0f_w1s": np.ascontiguousarray(inp["L0_ffn_W1s"]),
            "l0f_w3s": np.ascontiguousarray(inp["L0_ffn_W3s"]),
            "l0f_w2s": np.ascontiguousarray(inp["L0_ffn_W2s"]),
        })
    resE = _run(build_l0combine, maps)
    x2 = [resE[c]["x2_fm"] for c in range(NCORE)]
    maps = []
    for c in range(NCORE):
        m = l1attn_inputs(inp, c)
        m["x2_fm"] = x2[c]
        maps.append(m)
    resF = _run(build_l1attn, maps)
    x3 = [resF[c]["x3_fm"] for c in range(NCORE)]
    maps = []
    for c in range(NCORE):
        m = l1ffn_inputs(inp, c)
        m["x3_fm"] = x3[c]
        maps.append(m)
    resG = _run(build_l1ffn, maps)
    out = np.empty((16, 2048, 1024), np.float32)
    for c in range(NCORE):
        out[2 * c:2 * c + 2] = np.asarray(resG[c]["x4_fm"]).T.reshape(2, SEQ, D)
    return out



def build_l1ffn(nc, io_in="x3_fm", io_out="x4_fm", stack=None, b=None, dram=None):
    own = stack is None
    if own:
        from contextlib import ExitStack
        stack = ExitStack()
        stack.__enter__()
    if dram is None:
        dram = {}

    def din(name, shape, dtype=F32):
        if name not in dram:
            dram[name] = nc.dram_tensor(name, shape, dtype, kind="ExternalInput").ap()
        return dram[name]

    x = din(io_in, [D, TOK])
    rwt = din("l1f_rwt", [D, 16])
    biasr = din("l1f_bias", [1, 16])
    gw = din("l1f_gw", [16, D, D])
    uw = din("l1f_uw", [16, D, D])
    dw = din("l1f_dw", [16, D, D])
    sgw = din("l1f_sgw", [D, 2048])
    suw = din("l1f_suw", [D, 2048])
    sdw = din("l1f_sdw", [2048, D])
    if io_out not in dram:
        dram[io_out] = nc.dram_tensor(io_out, [D, TOK], F32, kind="ExternalOutput").ap()
    xo = dram[io_out]

    if b is None:
        b = B(nc, stack)
    sy, g, v, s, t = nc.sync, nc.gpsimd, nc.vector, nc.scalar, nc.tensor

    MEGA = 1024
    NMEGA = TOK // MEGA
    TPM = MEGA // TT

    x3b = b.sbuf("f_x3b", [128, NCH * MEGA], BF16)
    xt = b.sbuf("f_xt", [128, NCH * TT], F32)
    racc = b.sbuf("f_racc", [128, NCH * MEGA], F32)
    we_g = b.sbuf("f_weg", [128, NCH * D], BF16)
    we_u = b.sbuf("f_weu", [128, NCH * D], BF16)
    we_d = b.sbuf("f_wed", [128, NCH * D], BF16)
    sh_g = b.sbuf("f_shg", [128, NCH * 128], BF16)
    sh_u = b.sbuf("f_shu", [128, NCH * 128], BF16)
    sh_d = b.sbuf("f_shd", [128, 2 * NCH * 128], BF16)
    mprime = b.sbuf("f_mp", [128, NCH * TT], BF16)
    msh = b.sbuf("f_msh", [128, 2 * NCH * TT], BF16)
    rws = b.sbuf("f_rws", [128, NCH * 16], F32)
    biasb = b.sbuf("f_biasb", [128, 16], F32)
    eall = b.sbuf("f_eall", [16, 16 * 128], BF16)
    iden_sc = b.sbuf("f_idsc", [16, 16 * 128], F32)
    scores = b.sbuf("f_scores", [128, 16], F32)
    splus = b.sbuf("f_splus", [128, 16], F32)
    r1 = b.sbuf("f_r1", [128, 4], F32)
    r2 = b.sbuf("f_r2", [128, 4], F32)
    gsc = b.sbuf("f_gsc", [128, 4], F32)
    grow = b.sbuf("f_grow", [128, 2], F32)
    msel = b.sbuf("f_msel", [128, 16], F32)
    mx8 = b.sbuf("f_mx8", [128, 8], F32)
    ohall = b.sbuf("f_ohall", [128, 16], F32)
    comb = b.sbuf("f_comb", [128, 16], F32)
    combT = b.sbuf("f_combT", [16, MEGA], BF16)
    iden = b.sbuf("f_iden", [128, 128], F32)
    idsc2 = b.sbuf("f_idsc2", [128, 128], F32)
    ones_bf = b.sbuf("f_onesbf", [128, 1], BF16)
    ones1f = b.sbuf("f_ones1f", [1, 128], F32)
    sq = b.sbuf("f_sq", [128, NCH * TT], BF16)
    sbf = b.sbuf("f_sbf", [128, NCH * TT], BF16)
    st_ = b.sbuf("f_st", [128, 2 * TT], F32)
    x4t = b.sbuf("f_x4t", [128, NCH * TT], F32)
    rowa = b.sbuf("f_rowa", [1, TT], F32)
    rowb = b.sbuf("f_rowb", [1, TT], F32)
    rowbf = b.sbuf("f_rowbf", [1, 2 * TT], F32)

    pa = b.psum("f_pa", [128, TT])
    pu = b.psum("f_pu", [128, TT])
    pc = b.psum("f_pc", [128, TT])
    pdn0 = b.psum("f_pdn0", [128, TT])
    pdn1 = b.psum("f_pdn1", [128, TT])
    pbc1 = b.psum("f_pbc1", [128, TT])
    pbc2 = b.psum("f_pbc2", [128, TT])

    b.step(v, lambda e: e.memset(ones_bf[:], 1.0))
    b.step(v, lambda e: e.memset(ones1f[:], 1.0))
    b.step(g, lambda e: e.iota(iden_sc[:].rearrange("p (e i) -> p e i", e=16), pattern=[[-1, 16], [0, 128]],
                               base=0, channel_multiplier=1, allow_small_or_imprecise_dtypes=True))
    b.step(v, lambda e: e.tensor_single_scalar(eall[:], iden_sc[:], 0.0, ALU.is_equal))
    b.step(g, lambda e: e.iota(idsc2[:], pattern=[[-1, 128]], base=0, channel_multiplier=1,
                               allow_small_or_imprecise_dtypes=True))
    b.step(v, lambda e: e.tensor_single_scalar(iden[:], idsc2[:], 0.0, ALU.is_equal))
    b.wait(sy)
    b.bump(sy.dma_start(out=rws[:].rearrange("p (c j) -> p c j", c=NCH),
                        in_=rwt.rearrange("(c p) j -> p c j", p=128)), 16)
    biasrow = b.sbuf("f_biasrow", [1, 16], F32)
    b.wait(sy)
    b.bump(sy.dma_start(out=biasrow[:], in_=biasr[:, :]), 16)
    b.step(t, lambda e: e.matmul(pc[:, 0:16], ones1f[:, :], biasrow[:, :], start=True, stop=True))
    b.step(v, lambda e: e.tensor_copy(biasb[:], pc[:, 0:16]))

    for mega in range(NMEGA):
        m0 = mega * MEGA
        b.wait(g)
        b.bump(g.dma_start(out=x3b[:].rearrange("p (c t) -> p c t", c=NCH),
                           in_=x.rearrange("(c p) t -> p c t", p=128)[:, :, m0:m0 + MEGA]), 16)
        for ti in range(TPM):
            t0 = m0 + ti * TT
            b.wait(sy)
            b.bump(sy.dma_start(out=xt[:].rearrange("p (c t) -> p c t", c=NCH),
                                in_=x.rearrange("(c p) t -> p c t", p=128)[:, :, t0:t0 + TT]), 16)
            for sub in range(TT // 128):
                so = sub * 128
                b.wait(t)
                for dc in range(NCH):
                    last = t.matmul(pc[:, 0:16], xt[:, dc * TT + so: dc * TT + so + 128],
                                    rws[:, dc * 16:(dc + 1) * 16], start=(dc == 0), stop=(dc == NCH - 1))
                b.bump(last)
                b.step(s, lambda e: e.activation(scores[:], pc[:, 0:16], AF.Sigmoid))
                b.step(v, lambda e: e.tensor_add(splus[:], scores[:], biasb[:]))
                b.step(v, lambda e: e.tensor_reduce(r1[:].rearrange("p (g o) -> p g o", o=1),
                                                    splus[:].rearrange("p (g e) -> p g e", g=4),
                                                    mybir.AxisListType.X, ALU.max))
                b.step(v, lambda e: e.tensor_tensor(msel[:].rearrange("p (g e) -> p g e", g=4),
                                                    splus[:].rearrange("p (g e) -> p g e", g=4),
                                                    r1[:].rearrange("p (g o) -> p g o", o=1).to_broadcast([128, 4, 4]),
                                                    ALU.is_equal))
                b.step(v, lambda e: e.scalar_tensor_tensor(msel[:], msel[:], -1e30, splus[:], ALU.mult, ALU.add))
                b.step(v, lambda e: e.tensor_reduce(r2[:].rearrange("p (g o) -> p g o", o=1),
                                                    msel[:].rearrange("p (g e) -> p g e", g=4),
                                                    mybir.AxisListType.X, ALU.max))
                b.step(v, lambda e: e.tensor_add(gsc[:], r1[:], r2[:]))
                b.step(v, lambda e: e.tensor_reduce(grow[:, 0:1], gsc[:], mybir.AxisListType.X, ALU.max))
                b.step(v, lambda e: e.tensor_single_scalar(r2[:], gsc[:], grow[:, 0:1], ALU.is_equal))
                b.step(v, lambda e: e.scalar_tensor_tensor(r2[:], r2[:], -1e30, gsc[:], ALU.mult, ALU.add))
                b.step(v, lambda e: e.tensor_reduce(grow[:, 1:2], r2[:], mybir.AxisListType.X, ALU.max))
                b.step(v, lambda e: e.tensor_single_scalar(gsc[:], gsc[:], grow[:, 1:2], ALU.is_ge))
                b.step(v, lambda e: e.tensor_single_scalar(splus[:], splus[:], 1.0, ALU.add))
                b.step(v, lambda e: e.tensor_tensor(msel[:].rearrange("p (g e) -> p g e", g=4),
                                                    splus[:].rearrange("p (g e) -> p g e", g=4),
                                                    gsc[:].rearrange("p (g o) -> p g o", o=1).to_broadcast([128, 4, 4]),
                                                    ALU.mult))
                b.step(v, lambda e: e.tensor_single_scalar(msel[:], msel[:], -1.0, ALU.add))
                b.step(v, lambda e: e.max(out=mx8[:], in_=msel[:]))
                b.step(v, lambda e: e.tensor_single_scalar(ohall[:], msel[:], mx8[:, 3:4], ALU.is_ge))
                b.step(v, lambda e: e.tensor_mul(ohall[:], ohall[:], scores[:]))
                b.step(v, lambda e: e.tensor_reduce(grow[:, 0:1], ohall[:], mybir.AxisListType.X, ALU.add))
                b.step(v, lambda e: e.tensor_single_scalar(grow[:, 0:1], grow[:, 0:1], 1e-20, ALU.add))
                b.step(v, lambda e: e.reciprocal(grow[:, 1:2], grow[:, 0:1]))
                b.step(v, lambda e: e.tensor_scalar(comb[:], ohall[:], grow[:, 1:2], 2.5, ALU.mult, ALU.mult))
                b.step(t, lambda e: e.transpose(pc[0:16, 0:128], comb[:], iden[:]))
                b.step(v, lambda e, ti=ti, so=so: e.tensor_copy(combT[:, ti * TT + so: ti * TT + so + 128],
                                                                pc[0:16, 0:128]))
        for e_i in range(16):
            b.wait(g)
            b.bump(g.dma_start(out=we_g[:].rearrange("p (c f) -> p c f", c=NCH),
                               in_=gw[e_i].rearrange("(c p) f -> p c f", p=128)), 16)
            b.wait(g)
            b.bump(g.dma_start(out=we_u[:].rearrange("p (c f) -> p c f", c=NCH),
                               in_=uw[e_i].rearrange("(c p) f -> p c f", p=128)), 16)
            b.wait(g)
            b.bump(g.dma_start(out=we_d[:].rearrange("p (c f) -> p c f", c=NCH),
                               in_=dw[e_i].rearrange("(c p) f -> p c f", p=128)), 16)
            for ti in range(TPM):
                ts0 = ti * TT
                b.step(t, lambda e, e_i=e_i, ts0=ts0: e.matmul(
                    pc[:, :], eall[:, e_i * 128:(e_i + 1) * 128], combT[:, ts0:ts0 + TT], start=True, stop=True))
                for fc in range(NCH):
                    b.wait(t)
                    for dc in range(NCH):
                        last = t.matmul(pa[:, :], we_g[:, dc * D + fc * 128: dc * D + (fc + 1) * 128],
                                        x3b[:, dc * MEGA + ts0: dc * MEGA + ts0 + TT],
                                        start=(dc == 0), stop=(dc == NCH - 1))
                    b.bump(last)
                    b.wait(t)
                    for dc in range(NCH):
                        last = t.matmul(pu[:, :], we_u[:, dc * D + fc * 128: dc * D + (fc + 1) * 128],
                                        x3b[:, dc * MEGA + ts0: dc * MEGA + ts0 + TT],
                                        start=(dc == 0), stop=(dc == NCH - 1))
                    b.bump(last)
                    b.step(s, lambda e, fc=fc: e.activation(st_[:, 0:TT], pa[:, :], AF.Silu))
                    b.step(v, lambda e, fc=fc: e.tensor_mul(st_[:, TT:2 * TT], st_[:, 0:TT], pu[:, :]))
                    b.step(v, lambda e, fc=fc: e.tensor_mul(mprime[:, fc * TT:(fc + 1) * TT], st_[:, TT:2 * TT], pc[:, :]))
                for dc in range(NCH):
                    pd_ = pdn0 if dc % 2 == 0 else pdn1
                    b.wait(t)
                    for fc in range(NCH):
                        last = t.matmul(pd_[:, :], we_d[:, fc * D + dc * 128: fc * D + (dc + 1) * 128],
                                        mprime[:, fc * TT:(fc + 1) * TT],
                                        start=(fc == 0), stop=(fc == NCH - 1))
                    b.bump(last)
                    if e_i == 0:
                        b.step(v, lambda e, dc=dc, ts0=ts0: e.tensor_copy(
                            racc[:, dc * MEGA + ts0: dc * MEGA + ts0 + TT], pd_[:, :]))
                    else:
                        b.step(v, lambda e, dc=dc, ts0=ts0: e.tensor_add(
                            racc[:, dc * MEGA + ts0: dc * MEGA + ts0 + TT],
                            racc[:, dc * MEGA + ts0: dc * MEGA + ts0 + TT], pd_[:, :]))
        for ti in range(TPM):
            ts0 = ti * TT
            for fc in range(2 * NCH):
                if ti == 0:
                    b.wait(g)
                    b.bump(g.dma_start(out=sh_g[:].rearrange("p (c f) -> p c f", c=NCH),
                                       in_=sgw.rearrange("(c p) f -> p c f", p=128)[:, :, fc * 128:(fc + 1) * 128]), 16)
                    b.wait(g)
                    b.bump(g.dma_start(out=sh_u[:].rearrange("p (c f) -> p c f", c=NCH),
                                       in_=suw.rearrange("(c p) f -> p c f", p=128)[:, :, fc * 128:(fc + 1) * 128]), 16)
                elif True:
                    b.wait(g)
                    b.bump(g.dma_start(out=sh_g[:].rearrange("p (c f) -> p c f", c=NCH),
                                       in_=sgw.rearrange("(c p) f -> p c f", p=128)[:, :, fc * 128:(fc + 1) * 128]), 16)
                    b.wait(g)
                    b.bump(g.dma_start(out=sh_u[:].rearrange("p (c f) -> p c f", c=NCH),
                                       in_=suw.rearrange("(c p) f -> p c f", p=128)[:, :, fc * 128:(fc + 1) * 128]), 16)
                b.wait(t)
                for dc in range(NCH):
                    last = t.matmul(pa[:, :], sh_g[:, dc * 128: (dc + 1) * 128],
                                    x3b[:, dc * MEGA + ts0: dc * MEGA + ts0 + TT],
                                    start=(dc == 0), stop=(dc == NCH - 1))
                b.bump(last)
                b.wait(t)
                for dc in range(NCH):
                    last = t.matmul(pu[:, :], sh_u[:, dc * 128: (dc + 1) * 128],
                                    x3b[:, dc * MEGA + ts0: dc * MEGA + ts0 + TT],
                                    start=(dc == 0), stop=(dc == NCH - 1))
                b.bump(last)
                b.step(s, lambda e: e.activation(st_[:, 0:TT], pa[:, :], AF.Silu))
                b.step(v, lambda e, fc=fc: e.tensor_mul(msh[:, fc * TT:(fc + 1) * TT], st_[:, 0:TT], pu[:, :]))
            for dc in range(NCH):
                pd_ = pdn0 if dc % 2 == 0 else pdn1
                b.wait(g)
                b.bump(g.dma_start(out=sh_d[:].rearrange("p (c f) -> p c f", c=2 * NCH),
                                   in_=sdw.rearrange("(c p) f -> p c f", p=128)[:, :, dc * 128:(dc + 1) * 128]), 16)
                b.wait(t)
                for fc in range(2 * NCH):
                    last = t.matmul(pd_[:, :], sh_d[:, fc * 128: (fc + 1) * 128],
                                    msh[:, fc * TT:(fc + 1) * TT],
                                    start=(fc == 0), stop=(fc == 2 * NCH - 1))
                b.bump(last)
                b.step(v, lambda e, dc=dc, ts0=ts0: e.tensor_add(
                    racc[:, dc * MEGA + ts0: dc * MEGA + ts0 + TT],
                    racc[:, dc * MEGA + ts0: dc * MEGA + ts0 + TT], pd_[:, :]))
        for ti in range(TPM):
            t0 = m0 + ti * TT
            ts0 = ti * TT
            b.wait(sy)
            b.bump(sy.dma_start(out=xt[:].rearrange("p (c t) -> p c t", c=NCH),
                                in_=x.rearrange("(c p) t -> p c t", p=128)[:, :, t0:t0 + TT]), 16)
            for c in range(NCH):
                b.step(v, lambda e, c=c, ts0=ts0: e.tensor_add(
                    x4t[:, c * TT:(c + 1) * TT], xt[:, c * TT:(c + 1) * TT], racc[:, c * MEGA + ts0: c * MEGA + ts0 + TT]))
            b.step(v, lambda e: e.tensor_copy(sbf[:], x4t[:]))
            b.step(s, lambda e: e.activation(sq[:], x4t[:], AF.Square))
            b.wait(t)
            for c in range(NCH):
                last = t.matmul(pa[0:1, :], ones_bf[:, :], sbf[:, c * TT:(c + 1) * TT],
                                start=(c == 0), stop=(c == NCH - 1))
            b.bump(last)
            b.wait(t)
            for c in range(NCH):
                last = t.matmul(pu[0:1, :], ones_bf[:, :], sq[:, c * TT:(c + 1) * TT],
                                start=(c == 0), stop=(c == NCH - 1))
            b.bump(last)
            b.step(v, lambda e: e.tensor_single_scalar(rowbf[:, 0:TT], pa[0:1, :], 1.0 / D, ALU.mult))
            b.step(s, lambda e: e.activation(rowa[:], rowbf[:, 0:TT], AF.Square))
            b.step(v, lambda e: e.scalar_tensor_tensor(rowa[:], pu[0:1, :], 1.0 / D, rowa[:], ALU.mult, ALU.subtract))
            b.step(v, lambda e: e.tensor_single_scalar(rowa[:], rowa[:], 1e-5, ALU.add))
            b.step(v, lambda e: e.reciprocal(rowb[:], rowa[:]))
            b.step(s, lambda e: e.activation(rowbf[:, TT:2 * TT], rowb[:], AF.Sqrt))
            b.step(t, lambda e: e.matmul(pbc1[:, :], ones1f[:, :], rowbf[:, 0:TT], start=True, stop=True))
            b.step(t, lambda e: e.matmul(pbc2[:, :], ones1f[:, :], rowbf[:, TT:2 * TT], start=True, stop=True))
            for c in range(NCH):
                b.step(v, lambda e, c=c: e.tensor_sub(x4t[:, c * TT:(c + 1) * TT], x4t[:, c * TT:(c + 1) * TT], pbc1[:, :]))
                b.step(v, lambda e, c=c: e.tensor_mul(x4t[:, c * TT:(c + 1) * TT], x4t[:, c * TT:(c + 1) * TT], pbc2[:, :]))
            b.wait(sy)
            b.bump(sy.dma_start(out=xo.rearrange("(c p) t -> p c t", p=128)[:, :, t0:t0 + TT], in_=x4t[:]), 16)

    b.wait(sy)
    if own:
        stack.__exit__(None, None, None)
    return b


def l1ffn_inputs(inp, core):
    return {
        "l1f_rwt": np.ascontiguousarray(inp["L1_ffn_router_w"].T),
        "l1f_bias": np.ascontiguousarray(inp["L1_ffn_bias"][None, :]),
        "l1f_gw": np.ascontiguousarray(inp["L1_ffn_gate_w"]),
        "l1f_uw": np.ascontiguousarray(inp["L1_ffn_up_w"]),
        "l1f_dw": np.ascontiguousarray(inp["L1_ffn_down_w"]),
        "l1f_sgw": np.ascontiguousarray(inp["L1_ffn_sh_gate_w"]),
        "l1f_suw": np.ascontiguousarray(inp["L1_ffn_sh_up_w"]),
        "l1f_sdw": np.ascontiguousarray(inp["L1_ffn_sh_down_w"]),
    }



def build_l1attn(nc, io_in="x2_fm", io_out="x3_fm", stack=None, b=None, dram=None, dbg=False):
    own = stack is None
    if own:
        from contextlib import ExitStack
        stack = ExitStack()
        stack.__enter__()
    if dram is None:
        dram = {}

    def din(name, shape, dtype=F32):
        if name not in dram:
            dram[name] = nc.dram_tensor(name, shape, dtype, kind="ExternalInput").ap()
        return dram[name]

    x = din(io_in, [D, TOK])
    wx1t = din("l1a_wx1t", [D, 160])
    wx2r = din("l1a_wx2r", [32, 5 * D])
    xbias = din("l1a_xbias", [128, 5 * 8])
    wrt = din("l1a_wrt", [D, 512])
    wkt = din("l1a_wkt", [D, 512])
    wvt = din("l1a_wvt", [D, D])
    wgt = din("l1a_wgt", [D, D])
    awt = din("l1a_awt", [D, 64])
    bwt = din("l1a_bwt", [64, 512])
    bw_b = din("l1a_bwb", [128, 4])
    bonus = din("l1a_bonus", [128, 4])
    gnwb = din("l1a_gnwb", [1, 2 * D])
    wot = din("l1a_wot", [D, D])
    mux = din("l1a_mux", [128, 8])
    if io_out not in dram:
        dram[io_out] = nc.dram_tensor(io_out, [D, TOK], F32, kind="ExternalOutput").ap()
    xo = dram[io_out]

    if b is None:
        b = B(nc, stack)
    sy, g, v, s, t = nc.sync, nc.gpsimd, nc.vector, nc.scalar, nc.tensor
    HD = 4
    CK = 64
    NCHK = TT // CK

    wx1s = b.sbuf("a_wx1s", [128, NCH * 160], BF16)
    wx2s = b.sbuf("a_wx2s", [32, 5 * D], BF16)
    xb_s = b.sbuf("a_xbs", [128, 40], F32)
    wvs = b.sbuf("a_wvs", [128, NCH * D], BF16)
    wgs = b.sbuf("a_wgs", [128, NCH * 512], BF16)
    wrs = wgs
    wks = wgs
    aws = b.sbuf("a_aws", [128, NCH * 64], BF16)
    bws = b.sbuf("a_bws", [64, 512], BF16)
    bwb_s = b.sbuf("a_bwbs", [128, 4], F32)
    u_s = b.sbuf("a_us", [128, 4], F32)
    mux_s = b.sbuf("a_muxs", [128, 8], F32)
    wos = b.sbuf("a_wos", [128, NCH * 128], BF16)
    gnw_b = b.sbuf("a_gnwb", [128, D], F32)
    gnb_b = b.sbuf("a_gnbb", [128, D], F32)
    gnrow = b.sbuf("a_gnrow", [1, 2 * D], F32)
    ones_bf = b.sbuf("a_onesbf", [128, 1], BF16)
    ones1f = b.sbuf("a_ones1f", [1, 128], F32)
    zerosf = b.sbuf("a_zerosf", [128, TT], F32)
    maskLT = b.sbuf("a_maskLT", [64, 64], mybir.dt.uint8)

    xt = b.sbuf("a_xt", [128, NCH * (TT + 1)], BF16)
    dlt = b.sbuf("a_dlt", [128, NCH * TT], BF16)
    dltf = b.sbuf("a_dltf", [128, TT], F32)
    xm = b.sbuf("a_xm", [128, NCH * TT], BF16)
    zall = b.sbuf("a_zall", [32, 5 * TT], BF16)
    zsl5 = [zall[:, n * TT:(n + 1) * TT] for n in range(5)]
    mubuf = b.sbuf("a_mubuf", [128, TT], F32)
    r_bf = b.sbuf("a_rbf", [128, 4 * TT], BF16)
    k_bf = b.sbuf("a_kbf", [128, 4 * TT], BF16)
    wp = b.sbuf("a_wp", [128, 4 * TT], F32)
    cumw = b.sbuf("a_cumw", [128, 4 * TT], F32)
    v_tm = b.sbuf("a_vtm", [64, NCHK * D], BF16)
    g_tm = b.sbuf("a_gtm", [64, NCHK * D], BF16)
    qmid = b.sbuf("a_qmid", [128, TT], BF16)
    kmid = b.sbuf("a_kmid", [128, TT], BF16)
    qst = b.sbuf("a_qst", [128, TT], BF16)
    kendf = b.sbuf("a_kendf", [128, TT], F32)
    kend_tm = b.sbuf("a_kendtm", [64, 128], BF16)
    at_bf = b.sbuf("a_atbf", [64, 64], BF16)
    o_buf = b.sbuf("a_obuf", [64, NCHK * D], BF16)
    sdiag = b.sbuf("a_sdiag", [64, 4 * NCHK], F32)
    scr1 = b.sbuf("a_scr1", [128, TT], F32)
    scr2 = b.sbuf("a_scr2", [128, TT], F32)
    scr3 = b.sbuf("a_scr3", [128, TT], BF16)
    bnd = b.sbuf("a_bnd", [128, 4 * 24], F32)
    dend8 = b.sbuf("a_dend8", [128, 4 * 8], F32)
    carry_w = b.sbuf("a_carryw", [128, 4], F32)
    prev_carry = b.sbuf("a_prevc", [128, 4], F32)
    S_s = b.sbuf("a_Ss", [128, 4 * 256], F32)
    S_bf = b.sbuf("a_Sbf", [128, 4 * 256], BF16)
    iden_a = b.sbuf("a_idena", [128, 128], F32)
    iden_b = b.sbuf("a_idenb", [64, 64], BF16)
    gnm = b.sbuf("a_gnm", [64, 2 * HD], F32)
    st_ = b.sbuf("a_st", [128, NCH * TT], F32)
    sq = xm
    sbf = dlt
    rowa = scr1[0:1, :]
    rowb = scr2[0:1, :]
    row_mu = dltf[0:1, :]
    row_rs = kendf[0:1, :]

    p0 = b.psum("a_p0", [128, TT])
    p1 = b.psum("a_p1", [128, TT])
    p2 = b.psum("a_p2", [128, TT])
    p3 = b.psum("a_p3", [128, TT])
    p4 = b.psum("a_p4", [128, TT])
    p5 = b.psum("a_p5", [128, TT])
    p6 = b.psum("a_p6", [128, TT])
    pbf = b.psum("a_pbf", [128, 256], BF16)

    def ldw(dst, src, c=NCH):
        b.wait(g)
        b.bump(g.dma_start(out=dst[:].rearrange("p (c j) -> p c j", c=c),
                           in_=src.rearrange("(c p) j -> p c j", p=128)), 16)

    ldw(wx1s, wx1t)
    b.wait(g); b.bump(g.dma_start(out=wx2s[:], in_=wx2r[:, :]), 16)
    ldw(wvs, wvt); ldw(aws, awt)
    b.wait(g); b.bump(g.dma_start(out=bws[:], in_=bwt[:, :]), 16)
    b.wait(sy); b.bump(sy.dma_start(out=xb_s[:], in_=xbias[:, :]), 16)
    b.wait(sy); b.bump(sy.dma_start(out=bwb_s[:], in_=bw_b[:, :]), 16)
    b.wait(sy); b.bump(sy.dma_start(out=u_s[:], in_=bonus[:, :]), 16)
    b.wait(sy); b.bump(sy.dma_start(out=mux_s[:], in_=mux[:, :]), 16)
    b.wait(sy); b.bump(sy.dma_start(out=gnrow[:], in_=gnwb[:, :]), 16)

    b.step(v, lambda e: e.memset(ones_bf[:], 1.0))
    b.step(v, lambda e: e.memset(ones1f[:], 1.0))
    b.step(v, lambda e: e.memset(zerosf[:], 0.0))
    b.step(g, lambda e: e.iota(scr1[0:64, 0:64], pattern=[[1, 64]], base=0, channel_multiplier=-1,
                               allow_small_or_imprecise_dtypes=True))
    b.step(v, lambda e: e.tensor_single_scalar(maskLT[:], scr1[0:64, 0:64], 0.0, ALU.is_gt))
    b.step(g, lambda e: e.iota(scr1[:, 0:128], pattern=[[-1, 128]], base=0, channel_multiplier=1,
                               allow_small_or_imprecise_dtypes=True))
    b.step(v, lambda e: e.tensor_single_scalar(iden_a[:], scr1[:, 0:128], 0.0, ALU.is_equal))
    b.step(v, lambda e: e.tensor_single_scalar(iden_b[:], scr1[0:64, 0:64], 0.0, ALU.is_equal))
    b.step(t, lambda e: e.matmul(p0[:, :], ones1f[:, :], gnrow[:, 0:512], start=True, stop=True))
    b.step(t, lambda e: e.matmul(p1[:, :], ones1f[:, :], gnrow[:, 512:1024], start=True, stop=True))
    b.step(v, lambda e: e.tensor_copy(gnw_b[:, 0:512], p0[:, :]))
    b.step(v, lambda e: e.tensor_copy(gnw_b[:, 512:1024], p1[:, :]))
    b.step(t, lambda e: e.matmul(p0[:, :], ones1f[:, :], gnrow[:, 1024:1536], start=True, stop=True))
    b.step(t, lambda e: e.matmul(p1[:, :], ones1f[:, :], gnrow[:, 1536:2048], start=True, stop=True))
    b.step(v, lambda e: e.tensor_copy(gnb_b[:, 0:512], p0[:, :]))
    b.step(v, lambda e: e.tensor_copy(gnb_b[:, 512:1024], p1[:, :]))

    for tile in range(NTILES):
        t0 = tile * TT
        seq_start = (tile % (NTILES // NSEQ)) == 0
        xv = xt[:].rearrange("p (c t) -> p c t", c=NCH)
        b.wait(g)
        if seq_start:
            b.bump(g.dma_start(out=xv[:, :, 1:TT + 1],
                               in_=x.rearrange("(c p) t -> p c t", p=128)[:, :, t0:t0 + TT]), 16)
            b.step(v, lambda e: e.memset(xv[:, :, 0:1], 0.0))
        else:
            b.bump(g.dma_start(out=xv,
                               in_=x.rearrange("(c p) t -> p c t", p=128)[:, :, t0 - 1:t0 + TT]), 16)
        b.step(v, lambda e: e.tensor_sub(dlt[:].rearrange("p (c t) -> p c t", c=NCH),
                                         xv[:, :, 0:TT], xv[:, :, 1:TT + 1]))
        for c in range(NCH):
            b.step(v, lambda e, c=c: e.scalar_tensor_tensor(
                xm[:, c * TT:(c + 1) * TT], dlt[:, c * TT:(c + 1) * TT], mux_s[:, c:c + 1],
                xv[:, c, 1:TT + 1], ALU.mult, ALU.add))
        zp = [p0, p1, p2, p3, p4]
        for n in range(5):
            b.wait(t)
            for dc in range(NCH):
                last = t.matmul(zp[n][0:32, :], wx1s[:, dc * 160 + n * 32: dc * 160 + (n + 1) * 32],
                                xm[:, dc * TT:(dc + 1) * TT], start=(dc == 0), stop=(dc == NCH - 1))
            b.bump(last)
        for n in range(5):
            b.step(s, lambda e, n=n: e.activation(zsl5[n], zp[n][0:32, :], AF.Tanh))
        zsl = zsl5

        def make_xm(n):
            for dc in range(NCH):
                b.step(t, lambda e, n=n, dc=dc: e.matmul(
                    p3[:, :], wx2s[:, n * D + dc * 128:n * D + (dc + 1) * 128], zsl[n],
                    start=True, stop=True))
                b.step(s, lambda e, n=n, dc=dc: e.activation(
                    mubuf[:, :], p3[:, :], AF.Identity, bias=xb_s[:, n * 8 + dc:n * 8 + dc + 1], scale=1.0))
                b.step(v, lambda e, dc=dc: e.tensor_mul(dltf[:, :], dlt[:, dc * TT:(dc + 1) * TT], mubuf[:, :]))
                b.step(v, lambda e, dc=dc: e.tensor_add(
                    xm[:, dc * TT:(dc + 1) * TT], dltf[:, :], xv[:, dc, 1:TT + 1]))

        make_xm(0)
        b.wait(g)
        b.bump(g.dma_start(out=wgs[:].rearrange("p (c j) -> p c j", c=NCH),
                           in_=wrt.rearrange("(c p) j -> p c j", p=128)), 16)
        for jc in range(4):
            b.wait(t)
            for dc in range(NCH):
                last = t.matmul(p4[:, :], wrs[:, dc * 512 + jc * 128: dc * 512 + (jc + 1) * 128],
                                xm[:, dc * TT:(dc + 1) * TT], start=(dc == 0), stop=(dc == NCH - 1))
            b.bump(last)
            b.step(v, lambda e, jc=jc: e.tensor_copy(r_bf[:, jc * TT:(jc + 1) * TT], p4[:, :]))
        make_xm(1)
        b.wait(t)
        for dc in range(NCH):
            last = t.matmul(p4[0:64, :], aws[:, dc * 64:(dc + 1) * 64],
                            xm[:, dc * TT:(dc + 1) * TT], start=(dc == 0), stop=(dc == NCH - 1))
        b.bump(last)
        b.step(s, lambda e: e.activation(scr3[0:64, :], p4[0:64, :], AF.Tanh))
        for jc in range(4):
            b.step(t, lambda e, jc=jc: e.matmul(p5[:, :], bws[:, jc * 128:(jc + 1) * 128],
                                                scr3[0:64, :], start=True, stop=True))
            b.step(s, lambda e, jc=jc: e.activation(scr1[:, :], p5[:, :], AF.Identity,
                                                    bias=bwb_s[:, jc:jc + 1], scale=1.0))
            b.step(s, lambda e, jc=jc: e.activation(wp[:, jc * TT:(jc + 1) * TT], scr1[:, :], AF.Exp))
        make_xm(2)
        b.wait(g)
        b.bump(g.dma_start(out=wgs[:].rearrange("p (c j) -> p c j", c=NCH),
                           in_=wkt.rearrange("(c p) j -> p c j", p=128)), 16)
        for jc in range(4):
            b.wait(t)
            for dc in range(NCH):
                last = t.matmul(p4[:, :], wks[:, dc * 512 + jc * 128: dc * 512 + (jc + 1) * 128],
                                xm[:, dc * TT:(dc + 1) * TT], start=(dc == 0), stop=(dc == NCH - 1))
            b.bump(last)
            b.step(v, lambda e, jc=jc: e.tensor_copy(k_bf[:, jc * TT:(jc + 1) * TT], p4[:, :]))
        make_xm(3)
        for cc in range(NCHK):
            for half in range(2):
                b.wait(t)
                for dc in range(NCH):
                    last = t.matmul(p4[0:64, :], xm[:, dc * TT + cc * 64: dc * TT + (cc + 1) * 64],
                                    wvs[:, dc * D + half * 512: dc * D + (half + 1) * 512],
                                    start=(dc == 0), stop=(dc == NCH - 1))
                b.bump(last)
                b.step(v, lambda e, cc=cc, half=half: e.tensor_copy(
                    v_tm[:, cc * D + half * 512: cc * D + (half + 1) * 512], p4[0:64, :]))
        make_xm(4)
        for half in range(2):
            b.wait(g)
            b.bump(g.dma_start(out=wgs[:].rearrange("p (c j) -> p c j", c=NCH),
                               in_=wgt.rearrange("(c p) j -> p c j", p=128)[:, :, half * 512:(half + 1) * 512]), 16)
            for cc in range(NCHK):
                b.wait(t)
                for dc in range(NCH):
                    last = t.matmul(p4[0:64, :], xm[:, dc * TT + cc * 64: dc * TT + (cc + 1) * 64],
                                    wgs[:, dc * 512: (dc + 1) * 512],
                                    start=(dc == 0), stop=(dc == NCH - 1))
                b.bump(last)
                b.step(v, lambda e, cc=cc, half=half: e.tensor_copy(
                    g_tm[:, cc * D + half * 512: cc * D + (half + 1) * 512], p4[0:64, :]))

        if seq_start:
            b.step(v, lambda e: e.memset(carry_w[:], 0.0))
            b.step(v, lambda e: e.memset(S_s[:], 0.0))
            b.step(v, lambda e: e.memset(S_bf[:], 0.0))
        b.step(v, lambda e: e.tensor_copy(prev_carry[:], carry_w[:]))
        for hc in range(HD):
            hsl = slice(hc * TT, (hc + 1) * TT)
            b.step(v, lambda e, hsl=hsl, hc=hc: e.tensor_tensor_scan(
                cumw[:, hsl], wp[:, hsl], zerosf[:, :], carry_w[:, hc:hc + 1], ALU.add, ALU.add))
            b.step(v, lambda e, hc=hc: e.tensor_copy(carry_w[:, hc:hc + 1], cumw[:, hc * TT + TT - 1: hc * TT + TT]))
        for hc in range(HD):
            bb = hc * 24
            b.step(v, lambda e, hc=hc, bb=bb: e.tensor_copy(
                bnd[:, bb:bb + 8], cumw[:, hc * TT + CK // 2 - 1: (hc + 1) * TT: CK]))
            b.step(v, lambda e, hc=hc, bb=bb: e.tensor_copy(
                bnd[:, bb + 8:bb + 16], cumw[:, hc * TT + CK - 1: (hc + 1) * TT: CK]))
            if seq_start:
                b.step(v, lambda e, bb=bb: e.memset(bnd[:, bb + 16:bb + 17], 0.0))
            else:
                b.step(v, lambda e, bb=bb, hc=hc: e.tensor_copy(bnd[:, bb + 16:bb + 17], prev_carry[:, hc:hc + 1]))
            b.step(v, lambda e, bb=bb: e.tensor_copy(bnd[:, bb + 17:bb + 24], bnd[:, bb + 8:bb + 15]))
            b.step(v, lambda e, hc=hc, bb=bb: e.tensor_sub(dend8[:, hc * 8:(hc + 1) * 8],
                                                           bnd[:, bb + 16:bb + 24], bnd[:, bb + 8:bb + 16]))
            b.step(s, lambda e, hc=hc: e.activation(dend8[:, hc * 8:(hc + 1) * 8],
                                                    dend8[:, hc * 8:(hc + 1) * 8], AF.Exp))

        for hc in range(HD):
            hsl = slice(hc * TT, (hc + 1) * TT)
            bb = hc * 24
            cview = cumw[:, hsl].rearrange("p (n c) -> p n c", n=NCHK)
            b.step(v, lambda e, hsl=hsl: e.tensor_sub(scr1[:, :], cumw[:, hsl], wp[:, hsl]))
            ecview = scr1[:, :].rearrange("p (n c) -> p n c", n=NCHK)
            scv = scr2[:, :].rearrange("p (n c) -> p n c", n=NCHK)
            b.step(v, lambda e, bb=bb, ecview=ecview, scv=scv: e.tensor_sub(
                scv, ecview, bnd[:, bb:bb + 8].rearrange("p (n o) -> p n o", o=1).to_broadcast([128, NCHK, CK])))
            b.step(s, lambda e: e.activation(scr2[:, :], scr2[:, :], AF.Exp, bias=0.0, scale=-1.0))
            b.step(v, lambda e, hsl=hsl: e.tensor_mul(qmid[:, :], r_bf[:, hsl], scr2[:, :]))
            b.step(v, lambda e, bb=bb, cview=cview, scv=scv: e.tensor_sub(
                scv, bnd[:, bb:bb + 8].rearrange("p (n o) -> p n o", o=1).to_broadcast([128, NCHK, CK]), cview))
            b.step(s, lambda e: e.activation(scr2[:, :], scr2[:, :], AF.Exp, bias=0.0, scale=-1.0))
            b.step(v, lambda e, hsl=hsl: e.tensor_mul(kmid[:, :], k_bf[:, hsl], scr2[:, :]))
            b.step(v, lambda e, bb=bb, ecview=ecview, scv=scv: e.tensor_sub(
                scv, ecview, bnd[:, bb + 16:bb + 24].rearrange("p (n o) -> p n o", o=1).to_broadcast([128, NCHK, CK])))
            b.step(s, lambda e: e.activation(scr2[:, :], scr2[:, :], AF.Exp, bias=0.0, scale=-1.0))
            b.step(v, lambda e, hsl=hsl: e.tensor_mul(qst[:, :], r_bf[:, hsl], scr2[:, :]))
            b.step(v, lambda e, bb=bb, cview=cview, scv=scv: e.tensor_sub(
                scv, cview, bnd[:, bb + 8:bb + 16].rearrange("p (n o) -> p n o", o=1).to_broadcast([128, NCHK, CK])))
            b.step(s, lambda e: e.activation(scr2[:, :], scr2[:, :], AF.Exp))
            b.step(v, lambda e, hsl=hsl: e.tensor_mul(kendf[:, :], k_bf[:, hsl], scr2[:, :]))
            b.step(v, lambda e, hsl=hsl: e.tensor_mul(scr2[:, :], r_bf[:, hsl], k_bf[:, hsl]))
            b.step(v, lambda e, hc=hc: e.tensor_single_scalar(scr3[:, :], scr2[:, :], u_s[:, hc:hc + 1], ALU.mult))
            for cc in range(NCHK):
                b.step(t, lambda e, cc=cc: e.matmul(p6[0:64, 0:1], scr3[:, cc * 64:(cc + 1) * 64],
                                                    ones_bf[:, :], start=True, stop=True))
                b.step(v, lambda e, hc=hc, cc=cc: e.tensor_copy(sdiag[:, hc * 8 + cc:hc * 8 + cc + 1], p6[0:64, 0:1]))
            for cc in range(NCHK):
                csl = slice(cc * CK, (cc + 1) * CK)
                vsl = v_tm[:, cc * D + hc * 256: cc * D + (hc + 1) * 256]
                b.step(t, lambda e, csl=csl: e.transpose(p6[0:64, 0:128], kendf[:, csl], iden_a[:]))
                b.step(v, lambda e: e.tensor_copy(kend_tm[:, :], p6[0:64, 0:128]))
                b.step(t, lambda e, csl=csl: e.matmul(p0[0:64, 0:64], kmid[:, csl], qmid[:, csl], start=True, stop=True))
                b.step(v, lambda e: e.select(at_bf[:, :], maskLT[:, :], p0[0:64, 0:64], zerosf[0:64, 0:64]))
                b.wait(t)
                t.matmul(p1[0:64, 0:256], at_bf[:, :], vsl, start=True, stop=False)
                last = t.matmul(p1[0:64, 0:256], qst[:, csl], S_bf[:, hc * 256:(hc + 1) * 256],
                                start=False, stop=True)
                b.bump(last)
                b.step(v, lambda e, hc=hc, cc=cc, vsl=vsl: e.scalar_tensor_tensor(
                    o_buf[:, cc * D + hc * 256: cc * D + (hc + 1) * 256],
                    vsl, sdiag[:, hc * 8 + cc:hc * 8 + cc + 1],
                    p1[0:64, 0:256], ALU.mult, ALU.add))
                b.step(t, lambda e, vsl=vsl: e.matmul(p2[:, 0:256], kend_tm[:, :], vsl, start=True, stop=True))
                b.step(v, lambda e, cc=cc, hc=hc: e.scalar_tensor_tensor(
                    S_s[:, hc * 256:(hc + 1) * 256], S_s[:, hc * 256:(hc + 1) * 256],
                    dend8[:, hc * 8 + cc:hc * 8 + cc + 1], p2[:, 0:256], ALU.mult, ALU.add))
                b.step(v, lambda e, hc=hc: e.tensor_copy(S_bf[:, hc * 256:(hc + 1) * 256],
                                                         S_s[:, hc * 256:(hc + 1) * 256]))

        for cc in range(NCHK):
            osl = o_buf[:, cc * D:(cc + 1) * D]
            b.step(v, lambda e, osl=osl: e.tensor_reduce(
                gnm[:, 0:HD].rearrange("p (h o) -> p h o", o=1),
                osl.rearrange("p (h v) -> p h v", h=HD), mybir.AxisListType.X, ALU.add))
            b.step(s, lambda e, osl=osl: e.activation(st_[0:64, 0:D], osl, AF.Square))
            b.step(v, lambda e: e.tensor_reduce(
                gnm[:, HD:2 * HD].rearrange("p (h o) -> p h o", o=1),
                st_[0:64, 0:D].rearrange("p (h v) -> p h v", h=HD), mybir.AxisListType.X, ALU.add))
            b.step(v, lambda e: e.tensor_single_scalar(gnm[:, 0:HD], gnm[:, 0:HD], 1.0 / 256, ALU.mult))
            b.step(v, lambda e: e.tensor_single_scalar(gnm[:, HD:2 * HD], gnm[:, HD:2 * HD], 1.0 / 256, ALU.mult))
            b.step(s, lambda e: e.activation(st_[0:64, 0:HD], gnm[:, 0:HD], AF.Square))
            b.step(v, lambda e: e.tensor_sub(gnm[:, HD:2 * HD], gnm[:, HD:2 * HD], st_[0:64, 0:HD]))
            b.step(v, lambda e: e.tensor_single_scalar(gnm[:, HD:2 * HD], gnm[:, HD:2 * HD], 1e-5, ALU.add))
            b.step(v, lambda e: e.reciprocal(gnm[:, HD:2 * HD], gnm[:, HD:2 * HD]))
            b.step(s, lambda e: e.activation(gnm[:, HD:2 * HD], gnm[:, HD:2 * HD], AF.Sqrt))
            for hc in range(HD):
                oslh = o_buf[:, cc * D + hc * 256: cc * D + (hc + 1) * 256]
                b.step(v, lambda e, oslh=oslh, hc=hc: e.tensor_single_scalar(
                    st_[0:64, 0:256], oslh, gnm[:, hc:hc + 1], ALU.subtract))
                b.step(v, lambda e, hc=hc: e.tensor_single_scalar(
                    st_[0:64, 0:256], st_[0:64, 0:256], gnm[:, HD + hc:HD + hc + 1], ALU.mult))
                b.step(v, lambda e, hc=hc: e.tensor_mul(st_[0:64, 0:256], st_[0:64, 0:256], gnw_b[0:64, hc * 256:(hc + 1) * 256]))
                b.step(v, lambda e, hc=hc: e.tensor_add(st_[0:64, 0:256], st_[0:64, 0:256], gnb_b[0:64, hc * 256:(hc + 1) * 256]))
                b.step(s, lambda e, cc=cc, hc=hc: e.activation(
                    st_[0:64, 256:512], g_tm[:, cc * D + hc * 256: cc * D + (hc + 1) * 256], AF.Silu))
                b.step(v, lambda e, oslh=oslh: e.tensor_mul(oslh, st_[0:64, 0:256], st_[0:64, 256:512]))
            for vc in range(NCH):
                b.step(t, lambda e, cc=cc, vc=vc: e.transpose(
                    pbf[:, 0:64], o_buf[:, cc * D + vc * 128: cc * D + (vc + 1) * 128], iden_b[:]))
                b.step(v, lambda e, cc=cc, vc=vc: e.tensor_copy(
                    xm[:, vc * TT + cc * 64: vc * TT + (cc + 1) * 64], pbf[:, 0:64]))
        for jc in range(NCH):
            b.wait(g)
            b.bump(g.dma_start(out=wos[:].rearrange("p (c j) -> p c j", c=NCH),
                               in_=wot.rearrange("(c p) j -> p c j", p=128)[:, :, jc * 128:(jc + 1) * 128]), 16)
            b.wait(t)
            for dc in range(NCH):
                last = t.matmul(p4[:, :], wos[:, dc * 128: (dc + 1) * 128],
                                xm[:, dc * TT:(dc + 1) * TT], start=(dc == 0), stop=(dc == NCH - 1))
            b.bump(last)
            b.step(v, lambda e, jc=jc: e.tensor_add(
                st_[:, jc * TT:(jc + 1) * TT], xv[:, jc, 1:TT + 1], p4[:, :]))
        b.step(v, lambda e: e.tensor_copy(sbf[:], st_[:]))
        b.step(s, lambda e: e.activation(sq[:], st_[:], AF.Square))
        b.wait(t)
        for c in range(NCH):
            last = t.matmul(p5[0:1, :], ones_bf[:, :], sbf[:, c * TT:(c + 1) * TT],
                            start=(c == 0), stop=(c == NCH - 1))
        b.bump(last)
        b.wait(t)
        for c in range(NCH):
            last = t.matmul(p6[0:1, :], ones_bf[:, :], sq[:, c * TT:(c + 1) * TT],
                            start=(c == 0), stop=(c == NCH - 1))
        b.bump(last)
        b.step(v, lambda e: e.tensor_single_scalar(row_mu, p5[0:1, :], 1.0 / D, ALU.mult))
        b.step(s, lambda e: e.activation(rowa, row_mu, AF.Square))
        b.step(v, lambda e: e.scalar_tensor_tensor(rowa, p6[0:1, :], 1.0 / D, rowa, ALU.mult, ALU.subtract))
        b.step(v, lambda e: e.tensor_single_scalar(rowa, rowa, 1e-5, ALU.add))
        b.step(v, lambda e: e.reciprocal(rowb, rowa))
        b.step(s, lambda e: e.activation(row_rs, rowb, AF.Sqrt))
        b.step(t, lambda e: e.matmul(p0[:, :], ones1f[:, :], row_mu, start=True, stop=True))
        b.step(t, lambda e: e.matmul(p1[:, :], ones1f[:, :], row_rs, start=True, stop=True))
        for c in range(NCH):
            b.step(v, lambda e, c=c: e.tensor_sub(st_[:, c * TT:(c + 1) * TT], st_[:, c * TT:(c + 1) * TT], p0[:, :]))
            b.step(v, lambda e, c=c: e.tensor_mul(st_[:, c * TT:(c + 1) * TT], st_[:, c * TT:(c + 1) * TT], p1[:, :]))
        b.wait(sy)
        b.bump(sy.dma_start(out=xo.rearrange("(c p) t -> p c t", p=128)[:, :, t0:t0 + TT], in_=st_[:]), 16)
        if dbg and tile == 0:
            for nm, tile_ap in [("dlt", dlt[:]), ("xm", xm[:]), ("rbf", r_bf[:]), ("kbf", k_bf[:]),
                                ("wp", wp[:]), ("cumw", cumw[:]), ("vtm", v_tm[:]), ("gtm", g_tm[:]),
                                ("obuf", o_buf[:]), ("Ss", S_s[:]), ("sdiag", sdiag[:]),
                                ("qmid", qmid[:]), ("kmid", kmid[:]), ("qst", qst[:]), ("kendf", kendf[:]),
                                ("bnd", bnd[:]), ("dend8", dend8[:])]:
                dt_ = tile_ap.dtype
                od = nc.dram_tensor("dbg_" + nm, list(tile_ap.shape), dt_, kind="ExternalOutput").ap()
                b.wait(sy)
                b.bump(sy.dma_start(out=od[:, :], in_=tile_ap), 16)
            break

    b.wait(sy)
    if own:
        stack.__exit__(None, None, None)
    return b


def l1attn_inputs(inp, core):
    wx2 = inp["L1_attn_W_x2"].reshape(D, 5, 32)
    wx2r = np.ascontiguousarray(wx2.transpose(2, 1, 0).reshape(32, 5 * D))
    wx2r = np.ascontiguousarray(np.transpose(wx2, (2, 1, 0))).reshape(32, 5 * D)
    xb = np.zeros((128, 40), np.float32)
    for n in range(5):
        xb[:, n * 8:(n + 1) * 8] = inp["L1_attn_x_bias"][n].reshape(8, 128).T
    gnwb = np.concatenate([inp["L1_attn_gn_weight"], inp["L1_attn_gn_bias"]])[None, :]
    return {
        "l1a_wx1t": np.ascontiguousarray(inp["L1_attn_W_x1"].T),
        "l1a_wx2r": wx2r.astype(np.float32),
        "l1a_xbias": xb,
        "l1a_wrt": np.ascontiguousarray(inp["L1_attn_W_r"].T),
        "l1a_wkt": np.ascontiguousarray(inp["L1_attn_W_k"].T),
        "l1a_wvt": np.ascontiguousarray(inp["L1_attn_W_v"].T),
        "l1a_wgt": np.ascontiguousarray(inp["L1_attn_W_g"].T),
        "l1a_awt": np.ascontiguousarray(inp["L1_attn_A_w"].T),
        "l1a_bwt": np.ascontiguousarray(inp["L1_attn_B_w"].T),
        "l1a_bwb": np.ascontiguousarray(inp["L1_attn_b_w"].reshape(4, 128).T),
        "l1a_bonus": np.ascontiguousarray(inp["L1_attn_bonus"].T),
        "l1a_gnwb": np.ascontiguousarray(gnwb),
        "l1a_wot": np.ascontiguousarray(inp["L1_attn_W_o"].T),
        "l1a_mux": np.ascontiguousarray(inp["L1_attn_mu_x"].reshape(8, 128).T),
    }


CAP = 2926
SLOT_TILES = 6
ESLOTS = SLOT_TILES * 512
TRASH = 7 * ESLOTS


def _ln_rows_emit(b, t, v, s, ones_bf, ones1f, psum_s, psum_q, pb1, pb2, sbf, sq, row_mu, row_rs, rowa, rowb):
    b.wait(t)
    for c in range(NCH):
        last = t.matmul(psum_s, ones_bf[:, :], sbf[:, c * TT:(c + 1) * TT],
                        start=(c == 0), stop=(c == NCH - 1))
    b.bump(last)
    b.wait(t)
    for c in range(NCH):
        last = t.matmul(psum_q, ones_bf[:, :], sq[:, c * TT:(c + 1) * TT],
                        start=(c == 0), stop=(c == NCH - 1))
    b.bump(last)
    b.step(v, lambda e: e.tensor_single_scalar(row_mu, psum_s, 1.0 / D, ALU.mult))
    b.step(s, lambda e: e.activation(rowa, row_mu, AF.Square))
    b.step(v, lambda e: e.scalar_tensor_tensor(rowa, psum_q, 1.0 / D, rowa, ALU.mult, ALU.subtract))
    b.step(v, lambda e: e.tensor_single_scalar(rowa, rowa, 1e-5, ALU.add))
    b.step(v, lambda e: e.reciprocal(rowb, rowa))
    b.step(s, lambda e: e.activation(row_rs, rowb, AF.Sqrt))
    b.step(t, lambda e: e.matmul(pb1, ones1f[:, :], row_mu, start=True, stop=True))
    b.step(t, lambda e: e.matmul(pb2, ones1f[:, :], row_rs, start=True, stop=True))


def build_l0route(nc, io_in="x1_fm", stack=None, b=None, dram=None):
    own = stack is None
    if own:
        from contextlib import ExitStack
        stack = ExitStack()
        stack.__enter__()
    if dram is None:
        dram = {}

    def din(name, shape, dtype=F32):
        if name not in dram:
            dram[name] = nc.dram_tensor(name, shape, dtype, kind="ExternalInput").ap()
        return dram[name]

    x = din(io_in, [D, TOK])
    lng = din("l0f_lng", [128, 16])
    wr = din("l0f_wr", [D, 7])
    hout = nc.dram_tensor("h_fm", [D, TOK], BF16, kind="ExternalOutput").ap()
    lout = nc.dram_tensor("logits_tm", [TOK, 7], F32, kind="ExternalOutput").ap()

    if b is None:
        b = B(nc, stack)
    sy, g, v, s, t = nc.sync, nc.gpsimd, nc.vector, nc.scalar, nc.tensor

    xt = b.sbuf("r_xt", [128, NCH * TT], F32)
    hf = b.sbuf("r_hf", [128, NCH * TT], F32)
    hb = b.sbuf("r_hb", [128, NCH * TT], BF16)
    sq = b.sbuf("r_sq", [128, NCH * TT], BF16)
    sbf = b.sbuf("r_sbf", [128, NCH * TT], BF16)
    lng_s = b.sbuf("r_lng", [128, 16], F32)
    wr_s = b.sbuf("r_wrs", [128, NCH * 7], F32)
    lrow = b.sbuf("r_lrow", [128, 7], F32)
    ones_bf = b.sbuf("r_onesbf", [128, 1], BF16)
    ones1f = b.sbuf("r_ones1f", [1, 128], F32)
    rowscr = b.sbuf("r_rowscr", [1, 4 * TT], F32)
    rowa, rowb = rowscr[0:1, 0:TT], rowscr[0:1, TT:2 * TT]
    row_mu, row_rs = rowscr[0:1, 2 * TT:3 * TT], rowscr[0:1, 3 * TT:4 * TT]

    p0 = b.psum("r_p0", [128, TT])
    p1 = b.psum("r_p1", [128, TT])
    p2 = b.psum("r_p2", [128, TT])
    p3 = b.psum("r_p3", [128, TT])

    b.step(v, lambda e: e.memset(ones_bf[:], 1.0))
    b.step(v, lambda e: e.memset(ones1f[:], 1.0))
    b.wait(sy); b.bump(sy.dma_start(out=lng_s[:], in_=lng[:, :]), 16)
    b.wait(sy); b.bump(sy.dma_start(out=wr_s[:].rearrange("p (c j) -> p c j", c=NCH),
                                    in_=wr.rearrange("(c p) j -> p c j", p=128)), 16)

    for tile in range(NTILES):
        t0 = tile * TT
        b.wait(sy)
        b.bump(sy.dma_start(out=xt[:].rearrange("p (c t) -> p c t", c=NCH),
                            in_=x.rearrange("(c p) t -> p c t", p=128)[:, :, t0:t0 + TT]), 16)
        b.step(v, lambda e: e.tensor_copy(sbf[:], xt[:]))
        b.step(s, lambda e: e.activation(sq[:], xt[:], AF.Square))
        _ln_rows_emit(b, t, v, s, ones_bf, ones1f, p0[0:1, :], p1[0:1, :], p2[:, :], p3[:, :],
                      sbf, sq, row_mu, row_rs, rowa, rowb)
        for c in range(NCH):
            b.step(v, lambda e, c=c: e.tensor_sub(hf[:, c * TT:(c + 1) * TT], xt[:, c * TT:(c + 1) * TT], p2[:, :]))
            b.step(v, lambda e, c=c: e.tensor_mul(hf[:, c * TT:(c + 1) * TT], hf[:, c * TT:(c + 1) * TT], p3[:, :]))
            b.step(v, lambda e, c=c: e.tensor_scalar(hf[:, c * TT:(c + 1) * TT], hf[:, c * TT:(c + 1) * TT],
                                                     lng_s[:, c:c + 1], lng_s[:, 8 + c:8 + c + 1], ALU.mult, ALU.add))
        b.step(v, lambda e: e.tensor_copy(hb[:], hf[:]))
        b.wait(sy)
        b.bump(sy.dma_start(out=hout.rearrange("(c p) t -> p c t", p=128)[:, :, t0:t0 + TT], in_=hb[:]), 16)
        for sub in range(4):
            so = sub * 128
            b.wait(t)
            for dc in range(NCH):
                last = t.matmul(p0[:, 0:7], hf[:, dc * TT + so: dc * TT + so + 128],
                                wr_s[:, dc * 7:(dc + 1) * 7], start=(dc == 0), stop=(dc == NCH - 1))
            b.bump(last)
            b.step(v, lambda e: e.tensor_copy(lrow[:], p0[:, 0:7]))
            b.wait(sy)
            b.bump(sy.dma_start(out=lout[t0 + so:t0 + so + 128, :], in_=lrow[:]), 16)

    b.wait(sy)
    if own:
        stack.__exit__(None, None, None)
    return b


def build_l0rank(nc, stack=None, b=None, dram=None):
    own = stack is None
    if own:
        from contextlib import ExitStack
        stack = ExitStack()
        stack.__enter__()
    if dram is None:
        dram = {}
    NT = 256
    NE = 7
    W = NT * NE
    if "logits_all" in dram:
        logits = dram["logits_all"]
    else:
        logits = nc.dram_tensor("logits_all", [32768, 7], F32, kind="ExternalInput").ap()
    if "route_info" in dram:
        info = dram["route_info"]
    else:
        info = nc.dram_tensor("route_info", [128, 4 * NT], F32, kind="ExternalOutput").ap()
    if b is None:
        b = B(nc, stack)
    sy, g, v, s, t = nc.sync, nc.gpsimd, nc.vector, nc.scalar, nc.tensor

    lg = b.sbuf("k_lg", [128, W], F32)
    v1 = b.sbuf("k_v1", [128, NT], F32)
    v2 = b.sbuf("k_v2", [128, NT], F32)
    oh1 = b.sbuf("k_oh1", [128, W], BF16)
    oh2 = b.sbuf("k_oh2", [128, W], BF16)
    msk = b.sbuf("k_msk", [128, W], F32)
    wt1 = b.sbuf("k_wt1", [128, W], F32)
    wt2 = b.sbuf("k_wt2", [128, W], F32)
    cnt = b.sbuf("k_cnt", [1, 2 * W], F32)
    base = b.sbuf("k_base", [1, 2 * W], F32)
    eoff = b.sbuf("k_eoff", [128, NE], F32)
    ls = b.sbuf("k_ls", [128, 128], BF16)
    ones_bf = b.sbuf("k_onesbf", [128, 1], BF16)
    ones1f = b.sbuf("k_ones1f", [1, 128], F32)
    g1 = b.sbuf("k_g1", [128, NT], F32)
    g2 = b.sbuf("k_g2", [128, NT], F32)
    scr = b.sbuf("k_scr", [128, W], F32)
    out_s = b.sbuf("k_out", [128, 4 * NT], F32)

    p0 = b.psum("k_p0", [128, TT])
    p1 = b.psum("k_p1", [128, TT])
    p2 = b.psum("k_p2", [128, TT])
    p3 = b.psum("k_p3", [128, TT])
    p4 = b.psum("k_p4", [128, TT])
    p5 = b.psum("k_p5", [128, TT])
    p6 = b.psum("k_p6", [128, TT])
    p7 = b.psum("k_p7", [128, TT])

    b.step(v, lambda e: e.memset(ones_bf[:], 1.0))
    b.step(v, lambda e: e.memset(ones1f[:], 1.0))
    b.step(g, lambda e: e.iota(scr[:, 0:128], pattern=[[1, 128]], base=0, channel_multiplier=-1,
                               allow_small_or_imprecise_dtypes=True))
    b.step(v, lambda e: e.tensor_single_scalar(ls[:], scr[:, 0:128], 0.0, ALU.is_gt))
    b.step(g, lambda e: e.iota(eoff[:], pattern=[[ESLOTS, NE]], base=0, channel_multiplier=0,
                               allow_small_or_imprecise_dtypes=True))
    b.wait(sy)
    b.bump(sy.dma_start(out=lg[:].rearrange("p (T e) -> p T e", T=NT),
                        in_=logits.rearrange("(T p) e -> p T e", p=128)), 16)
    lgv = lg[:].rearrange("p (T e) -> p T e", T=NT)
    b.step(v, lambda e: e.tensor_reduce(v1[:].rearrange("p (T o) -> p T o", o=1), lgv, mybir.AxisListType.X, ALU.max))
    b.step(v, lambda e: e.tensor_tensor(oh1[:].rearrange("p (T e) -> p T e", T=NT), lgv,
                                        v1[:].rearrange("p (T o) -> p T o", o=1).to_broadcast([128, NT, NE]), ALU.is_equal))
    b.step(v, lambda e: e.scalar_tensor_tensor(msk[:], oh1[:], -1e30, lg[:], ALU.mult, ALU.add))
    b.step(v, lambda e: e.tensor_reduce(v2[:].rearrange("p (T o) -> p T o", o=1),
                                        msk[:].rearrange("p (T e) -> p T e", T=NT), mybir.AxisListType.X, ALU.max))
    b.step(v, lambda e: e.tensor_tensor(oh2[:].rearrange("p (T e) -> p T e", T=NT),
                                        msk[:].rearrange("p (T e) -> p T e", T=NT),
                                        v2[:].rearrange("p (T o) -> p T o", o=1).to_broadcast([128, NT, NE]), ALU.is_equal))
    b.step(v, lambda e: e.tensor_sub(g1[:], v1[:], v2[:]))
    b.step(s, lambda e: e.activation(g1[:], g1[:], AF.Sigmoid))
    b.step(v, lambda e: e.tensor_scalar(g2[:], g1[:], -1.0, 1.0, ALU.mult, ALU.add))
    for k_, oh in ((0, oh1), (1, oh2)):
        for q in range(4):
            b.step(t, lambda e, k_=k_, oh=oh, q=q: e.matmul(
                [p0, p1, p2, p3][q][0:1, 0:448], ones_bf[:, :], oh[:, q * 448:(q + 1) * 448], start=True, stop=True))
        for q in range(4):
            b.step(v, lambda e, k_=k_, q=q: e.tensor_copy(cnt[0:1, k_ * W + q * 448:k_ * W + (q + 1) * 448],
                                                          [p0, p1, p2, p3][q][0:1, 0:448]))
    zr = scr[0:1, 0:NT]
    b.step(v, lambda e: e.memset(zr, 0.0))
    for e_i in range(NE):
        b.step(v, lambda e, e_i=e_i: e.tensor_tensor_scan(
            base[0:1, e_i:W:NE], cnt[0:1, e_i:W:NE], zr, 0.0, ALU.add, ALU.add))
        b.step(v, lambda e, e_i=e_i: e.tensor_tensor_scan(
            base[0:1, W + e_i:2 * W:NE], cnt[0:1, W + e_i:2 * W:NE], zr,
            base[0:1, (NT - 1) * NE + e_i:(NT - 1) * NE + e_i + 1], ALU.add, ALU.add))
    b.step(v, lambda e: e.tensor_sub(base[:], base[:], cnt[:]))
    for k_, oh, wt in ((0, oh1, wt1), (1, oh2, wt2)):
        for q in range(4):
            b.step(t, lambda e, oh=oh, q=q: e.matmul(
                [p0, p1, p2, p3][q][:, 0:448], ls[:, :], oh[:, q * 448:(q + 1) * 448], start=True, stop=True))
        for q in range(4):
            b.step(t, lambda e, k_=k_, q=q: e.matmul(
                [p4, p5, p6, p7][q][:, 0:448], ones1f[:, :], base[0:1, k_ * W + q * 448:k_ * W + (q + 1) * 448],
                start=True, stop=True))
        for q in range(4):
            b.step(v, lambda e, wt=wt, q=q: e.tensor_copy(
                wt[:, q * 448:(q + 1) * 448], [p0, p1, p2, p3][q][:, 0:448]))
            b.step(v, lambda e, wt=wt, q=q: e.tensor_add(
                wt[:, q * 448:(q + 1) * 448], wt[:, q * 448:(q + 1) * 448], [p4, p5, p6, p7][q][:, 0:448]))
    for k_, oh, wt, gk, col in ((0, oh1, wt1, g1, 0), (1, oh2, wt2, g2, 1)):
        b.step(v, lambda e, wt=wt: e.tensor_tensor(
            scr[:].rearrange("p (T e) -> p T e", T=NT), wt[:].rearrange("p (T e) -> p T e", T=NT),
            eoff[:].rearrange("p e -> p 1 e" if False else "p (o e) -> p o e", o=1).to_broadcast([128, NT, NE]), ALU.add))
        b.step(v, lambda e, oh=oh: e.tensor_mul(scr[:], scr[:], oh[:]))
        b.step(v, lambda e, col=col: e.tensor_reduce(
            out_s[:, col * NT:(col + 1) * NT].rearrange("p (T o) -> p T o", o=1),
            scr[:].rearrange("p (T e) -> p T e", T=NT), mybir.AxisListType.X, ALU.add))
        b.step(v, lambda e, oh=oh, wt=wt: e.tensor_mul(scr[:], wt[:], oh[:]))
        b.step(v, lambda e: e.tensor_reduce(
            v1[:].rearrange("p (T o) -> p T o", o=1),
            scr[:].rearrange("p (T e) -> p T e", T=NT), mybir.AxisListType.X, ALU.add))
        b.step(v, lambda e: e.tensor_single_scalar(v1[:], v1[:], float(CAP) - 0.5, ALU.is_lt))
        b.step(v, lambda e, gk=gk, col=col: e.tensor_mul(out_s[:, (2 + col) * NT:(3 + col) * NT], gk[:], v1[:]))
        b.step(v, lambda e, col=col: e.tensor_single_scalar(
            out_s[:, col * NT:(col + 1) * NT], out_s[:, col * NT:(col + 1) * NT], -float(TRASH), ALU.add))
        b.step(v, lambda e, col=col: e.tensor_mul(
            out_s[:, col * NT:(col + 1) * NT], out_s[:, col * NT:(col + 1) * NT], v1[:]))
        b.step(v, lambda e, col=col: e.tensor_single_scalar(
            out_s[:, col * NT:(col + 1) * NT], out_s[:, col * NT:(col + 1) * NT], float(TRASH), ALU.add))
    b.wait(sy)
    b.bump(sy.dma_start(out=info[:, :], in_=out_s[:]), 16)
    b.wait(sy)
    if own:
        stack.__exit__(None, None, None)
    return b


def build_l0slots(nc, stack=None, b=None, dram=None):
    own = stack is None
    if own:
        from contextlib import ExitStack
        stack = ExitStack()
        stack.__enter__()
    if dram is None:
        dram = {}
    hs = nc.dram_tensor("hs_fm", [D, ESLOTS], BF16, kind="ExternalInput").ap()
    gwa = nc.dram_tensor("l0f_gw", [D, 4096], F32, kind="ExternalInput").ap()
    uwa = nc.dram_tensor("l0f_uw", [D, 4096], F32, kind="ExternalInput").ap()
    dwa = nc.dram_tensor("l0f_dw", [4096, D], F32, kind="ExternalInput").ap()
    ye = nc.dram_tensor("ye_fm", [D, ESLOTS], F32, kind="ExternalOutput").ap()
    if b is None:
        b = B(nc, stack)
    sy, g, v, s, t = nc.sync, nc.gpsimd, nc.vector, nc.scalar, nc.tensor

    hb = b.sbuf("d_hb", [128, NCH * TT], BF16)
    wg = b.sbuf("d_wg", [128, NCH * 1024], BF16)
    wu = b.sbuf("d_wu", [128, NCH * 1024], BF16)
    mp = b.sbuf("d_mp", [128, 32 * TT], BF16)
    yt = b.sbuf("d_yt", [128, NCH * TT], F32)
    sil = b.sbuf("d_sil", [128, TT], F32)

    pa = b.psum("d_pa", [128, TT])
    pu = b.psum("d_pu", [128, TT])
    pdn = [b.psum(f"d_pd{i}", [128, TT]) for i in range(6)]

    for tile in range(SLOT_TILES):
        t0 = tile * TT
        b.wait(g)
        b.bump(g.dma_start(out=hb[:].rearrange("p (c t) -> p c t", c=NCH),
                           in_=hs.rearrange("(c p) t -> p c t", p=128)[:, :, t0:t0 + TT]), 16)
        for fg in range(4):
            b.wait(g)
            b.bump(g.dma_start(out=wg[:].rearrange("p (c f) -> p c f", c=NCH),
                               in_=gwa.rearrange("(c p) f -> p c f", p=128)[:, :, fg * 1024:(fg + 1) * 1024]), 16)
            b.wait(g)
            b.bump(g.dma_start(out=wu[:].rearrange("p (c f) -> p c f", c=NCH),
                               in_=uwa.rearrange("(c p) f -> p c f", p=128)[:, :, fg * 1024:(fg + 1) * 1024]), 16)
            for fc in range(NCH):
                b.wait(t)
                for dc in range(NCH):
                    last = t.matmul(pa[:, :], wg[:, dc * 1024 + fc * 128: dc * 1024 + (fc + 1) * 128],
                                    hb[:, dc * TT:(dc + 1) * TT], start=(dc == 0), stop=(dc == NCH - 1))
                b.bump(last)
                b.wait(t)
                for dc in range(NCH):
                    last = t.matmul(pu[:, :], wu[:, dc * 1024 + fc * 128: dc * 1024 + (fc + 1) * 128],
                                    hb[:, dc * TT:(dc + 1) * TT], start=(dc == 0), stop=(dc == NCH - 1))
                b.bump(last)
                b.step(s, lambda e: e.activation(sil[:, :], pa[:, :], AF.Silu))
                b.step(v, lambda e, fg=fg, fc=fc: e.tensor_mul(
                    mp[:, (fg * NCH + fc) * TT: (fg * NCH + fc + 1) * TT], sil[:, :], pu[:, :]))
        for half, dcs in ((0, range(0, 6)), (1, range(6, 8))):
            for fg in range(4):
                b.wait(g)
                b.bump(g.dma_start(out=wg[:].rearrange("p (c f) -> p c f", c=NCH),
                                   in_=dwa.rearrange("(c p) f -> p c f", p=128)[:, fg * NCH:(fg + 1) * NCH, :]), 16)
                b.wait(t)
                for dc in dcs:
                    for fc in range(NCH):
                        last = t.matmul(pdn[dc % 6][:, :], wg[:, fc * 1024 + dc * 128: fc * 1024 + (dc + 1) * 128],
                                        mp[:, (fg * NCH + fc) * TT: (fg * NCH + fc + 1) * TT],
                                        start=(fg == 0 and fc == 0), stop=(fg == 3 and fc == NCH - 1))
                b.bump(last)
            for dc in dcs:
                b.step(v, lambda e, dc=dc: e.tensor_copy(yt[:, dc * TT:(dc + 1) * TT], pdn[dc % 6][:, :]))
        b.wait(sy)
        b.bump(sy.dma_start(out=ye.rearrange("(c p) t -> p c t", p=128)[:, :, t0:t0 + TT], in_=yt[:]), 16)

    b.wait(sy)
    if own:
        stack.__exit__(None, None, None)
    return b


def build_l0combine(nc, io_in="x1_fm", io_out="x2_fm", stack=None, b=None, dram=None):
    own = stack is None
    if own:
        from contextlib import ExitStack
        stack = ExitStack()
        stack.__enter__()
    if dram is None:
        dram = {}

    def din(name, shape, dtype=F32):
        if name not in dram:
            dram[name] = nc.dram_tensor(name, shape, dtype, kind="ExternalInput").ap()
        return dram[name]

    x = din(io_in, [D, TOK])
    ye0 = din("ye0_fm", [D, TOK])
    ye1 = din("ye1_fm", [D, TOK])
    kg = din("kg_rows", [1, 2 * TOK])
    lng = din("l0f_lng", [128, 16])
    w1s_d = din("l0f_w1s", [D, 4096])
    w3s_d = din("l0f_w3s", [D, 4096])
    w2s_d = din("l0f_w2s", [4096, D])
    if io_out not in dram:
        dram[io_out] = nc.dram_tensor(io_out, [D, TOK], F32, kind="ExternalOutput").ap()
    xo = dram[io_out]
    if b is None:
        b = B(nc, stack)
    sy, g, v, s, t = nc.sync, nc.gpsimd, nc.vector, nc.scalar, nc.tensor

    xt = b.sbuf("e_xt", [128, NCH * TT], F32)
    yt0 = b.sbuf("e_yt0", [128, NCH * TT], F32)
    yt1 = b.sbuf("e_yt1", [128, NCH * TT], F32)
    acc = b.sbuf("e_acc", [128, NCH * TT], F32)
    hb = b.sbuf("e_hb", [128, NCH * TT], BF16)
    wg = b.sbuf("e_wg", [128, NCH * 1024], BF16)
    wu = b.sbuf("e_wu", [128, NCH * 1024], BF16)
    mp = b.sbuf("e_mp", [128, 32 * TT], BF16)
    sil = b.sbuf("e_sil", [128, TT], F32)
    kgr = b.sbuf("e_kgr", [1, 2 * TOK], F32)
    lng_s = b.sbuf("e_lng", [128, 16], F32)
    sq = b.sbuf("e_sq", [128, NCH * TT], BF16)
    sbf = b.sbuf("e_sbf", [128, NCH * TT], BF16)
    ones_bf = b.sbuf("e_onesbf", [128, 1], BF16)
    ones1f = b.sbuf("e_ones1f", [1, 128], F32)
    rowscr = b.sbuf("e_rowscr", [1, 4 * TT], F32)
    rowa, rowb = rowscr[0:1, 0:TT], rowscr[0:1, TT:2 * TT]
    row_mu, row_rs = rowscr[0:1, 2 * TT:3 * TT], rowscr[0:1, 3 * TT:4 * TT]

    pa = b.psum("e_pa", [128, TT])
    pu = b.psum("e_pu", [128, TT])
    pb1 = b.psum("e_pb1", [128, TT])
    pb2 = b.psum("e_pb2", [128, TT])
    pdn0 = b.psum("e_pdn0", [128, TT])
    pdn1 = b.psum("e_pdn1", [128, TT])
    pr1 = b.psum("e_pr1", [128, TT])
    pr2 = b.psum("e_pr2", [128, TT])

    b.step(v, lambda e: e.memset(ones_bf[:], 1.0))
    b.step(v, lambda e: e.memset(ones1f[:], 1.0))
    b.wait(sy); b.bump(sy.dma_start(out=kgr[:], in_=kg[:, :]), 16)
    b.wait(sy); b.bump(sy.dma_start(out=lng_s[:], in_=lng[:, :]), 16)

    for tile in range(NTILES):
        t0 = tile * TT
        for dst, srcd in ((xt, x), (yt0, ye0), (yt1, ye1)):
            b.wait(sy)
            b.bump(sy.dma_start(out=dst[:].rearrange("p (c t) -> p c t", c=NCH),
                                in_=srcd.rearrange("(c p) t -> p c t", p=128)[:, :, t0:t0 + TT]), 16)
        b.step(t, lambda e, t0=t0: e.matmul(pr1[:, :], ones1f[:, :], kgr[0:1, t0:t0 + TT], start=True, stop=True))
        b.step(t, lambda e, t0=t0: e.matmul(pr2[:, :], ones1f[:, :], kgr[0:1, TOK + t0:TOK + t0 + TT], start=True, stop=True))
        for c in range(NCH):
            b.step(v, lambda e, c=c: e.tensor_mul(acc[:, c * TT:(c + 1) * TT], yt0[:, c * TT:(c + 1) * TT], pr1[:, :]))
            b.step(v, lambda e, c=c: e.tensor_mul(yt1[:, c * TT:(c + 1) * TT], yt1[:, c * TT:(c + 1) * TT], pr2[:, :]))
            b.step(v, lambda e, c=c: e.tensor_add(acc[:, c * TT:(c + 1) * TT], acc[:, c * TT:(c + 1) * TT],
                                                  yt1[:, c * TT:(c + 1) * TT]))
            b.step(v, lambda e, c=c: e.tensor_add(acc[:, c * TT:(c + 1) * TT], acc[:, c * TT:(c + 1) * TT],
                                                  xt[:, c * TT:(c + 1) * TT]))
        b.step(v, lambda e: e.tensor_copy(sbf[:], xt[:]))
        b.step(s, lambda e: e.activation(sq[:], xt[:], AF.Square))
        _ln_rows_emit(b, t, v, s, ones_bf, ones1f, pa[0:1, :], pu[0:1, :], pb1[:, :], pb2[:, :],
                      sbf, sq, row_mu, row_rs, rowa, rowb)
        for c in range(NCH):
            b.step(v, lambda e, c=c: e.tensor_sub(sil[:, :], xt[:, c * TT:(c + 1) * TT], pb1[:, :]))
            b.step(v, lambda e, c=c: e.tensor_mul(sil[:, :], sil[:, :], pb2[:, :]))
            b.step(v, lambda e, c=c: e.tensor_scalar(hb[:, c * TT:(c + 1) * TT], sil[:, :],
                                                     lng_s[:, c:c + 1], lng_s[:, 8 + c:8 + c + 1], ALU.mult, ALU.add))
        for fg in range(4):
            b.wait(g)
            b.bump(g.dma_start(out=wg[:].rearrange("p (c f) -> p c f", c=NCH),
                               in_=w1s_d.rearrange("(c p) f -> p c f", p=128)[:, :, fg * 1024:(fg + 1) * 1024]), 16)
            b.wait(g)
            b.bump(g.dma_start(out=wu[:].rearrange("p (c f) -> p c f", c=NCH),
                               in_=w3s_d.rearrange("(c p) f -> p c f", p=128)[:, :, fg * 1024:(fg + 1) * 1024]), 16)
            for fc in range(NCH):
                b.wait(t)
                for dc in range(NCH):
                    last = t.matmul(pa[:, :], wg[:, dc * 1024 + fc * 128: dc * 1024 + (fc + 1) * 128],
                                    hb[:, dc * TT:(dc + 1) * TT], start=(dc == 0), stop=(dc == NCH - 1))
                b.bump(last)
                b.wait(t)
                for dc in range(NCH):
                    last = t.matmul(pu[:, :], wu[:, dc * 1024 + fc * 128: dc * 1024 + (fc + 1) * 128],
                                    hb[:, dc * TT:(dc + 1) * TT], start=(dc == 0), stop=(dc == NCH - 1))
                b.bump(last)
                b.step(s, lambda e: e.activation(sil[:, :], pa[:, :], AF.Silu))
                b.step(v, lambda e, fg=fg, fc=fc: e.tensor_mul(
                    mp[:, (fg * NCH + fc) * TT: (fg * NCH + fc + 1) * TT], sil[:, :], pu[:, :]))
        for dc in range(NCH):
            pd_ = pdn0 if dc % 2 == 0 else pdn1
            for fg in range(4):
                b.wait(g)
                if dc == 0 or True:
                    b.bump(g.dma_start(out=wg[:, 0:NCH * 128].rearrange("p (c f) -> p c f", c=NCH),
                                       in_=w2s_d.rearrange("(c p) f -> p c f", p=128)[:, fg * NCH:(fg + 1) * NCH,
                                                                                      dc * 128:(dc + 1) * 128]), 16)
                b.wait(t)
                for fc in range(NCH):
                    last = t.matmul(pd_[:, :], wg[:, fc * 128:(fc + 1) * 128],
                                    mp[:, (fg * NCH + fc) * TT: (fg * NCH + fc + 1) * TT],
                                    start=(fg == 0 and fc == 0), stop=(fg == 3 and fc == NCH - 1))
                b.bump(last)
            b.step(v, lambda e, dc=dc: e.tensor_add(acc[:, dc * TT:(dc + 1) * TT], acc[:, dc * TT:(dc + 1) * TT],
                                                    pd_[:, :]))
        b.step(v, lambda e: e.tensor_copy(sbf[:], acc[:]))
        b.step(s, lambda e: e.activation(sq[:], acc[:], AF.Square))
        _ln_rows_emit(b, t, v, s, pa[0:1, :], pu[0:1, :], pb1[:, :], pb2[:, :], sbf, sq,
                      row_mu, row_rs, rowa, rowb) if False else \
            _ln_rows_emit(b, t, v, s, ones_bf, ones1f, pa[0:1, :], pu[0:1, :], pb1[:, :], pb2[:, :],
                          sbf, sq, row_mu, row_rs, rowa, rowb)
        for c in range(NCH):
            b.step(v, lambda e, c=c: e.tensor_sub(acc[:, c * TT:(c + 1) * TT], acc[:, c * TT:(c + 1) * TT], pb1[:, :]))
            b.step(v, lambda e, c=c: e.tensor_mul(acc[:, c * TT:(c + 1) * TT], acc[:, c * TT:(c + 1) * TT], pb2[:, :]))
        b.wait(sy)
        b.bump(sy.dma_start(out=xo.rearrange("(c p) t -> p c t", p=128)[:, :, t0:t0 + TT], in_=acc[:]), 16)

    b.wait(sy)
    if own:
        stack.__exit__(None, None, None)
    return b
